# version 03
# baseline (speedup 1.0000x reference)
import numpy as np
import ml_dtypes
from contextlib import ExitStack

import concourse.bass as bass
import concourse.mybir as mybir
from concourse.bass_utils import run_bass_kernel_spmd

F32 = mybir.dt.float32
BF16 = mybir.dt.bfloat16
AF = mybir.ActivationFunctionType
ALU = mybir.AluOpType
AX = mybir.AxisListType
NPBF = ml_dtypes.bfloat16

NCORES = 8
B_PER = 2
S = 2048
D = 1024
DC = D // 128
NT = S // 512
LN_EPS = 1e-5
BIG = 1e30


class Ctr:
    def __init__(self):
        self.v = 0
    def inc(self, n=1):
        self.v += n
        return self.v


class Ctx:
    ENGS = ["sync", "tensor", "vector", "scalar", "gpsimd"]

    def __init__(self, nc, es):
        self.nc = nc
        self.prog = {e: [] for e in self.ENGS}
        self.sem = {}
        self.ctr = {}
        for name in ["ld", "ld0", "ld1", "ldg", "t", "v", "s", "g", "wr"]:
            self.sem[name] = es.enter_context(nc.semaphore("sem_" + name))
            self.ctr[name] = Ctr()
        self.chain = {}
        for e in ["tensor", "vector", "scalar", "gpsimd"]:
            self.chain[e] = [es.enter_context(nc.semaphore("chain_" + e)), 0]

    def new_dma_sems(self, es, suffix):
        for name in ["ld", "ld0", "ld1", "ldg", "wr"]:
            self.sem[name] = es.enter_context(self.nc.semaphore(f"sem_{name}_{suffix}"))
            self.ctr[name] = Ctr()

    def em(self, eng, fn):
        self.prog[eng].append(fn)

    def dma(self, q, out, in_, sem="ld"):
        try:
            nm = out.tensor.name
            if nm.endswith("stg0") or nm.endswith("stg1"):
                sem = "ld" + nm[-1]
        except Exception:
            pass
        s = self.sem[sem]
        def f(eng):
            eng.dma_start(out=out, in_=in_).then_inc(s, 16)
        self.em(q, f)
        return (sem, self.ctr[sem].inc(16))

    def wait(self, eng, sem, val):
        s = self.sem[sem]
        self.em(eng, lambda e: e.wait_ge(s, val))

    def finalize(self, block):
        for eng in self.ENGS:
            fns = self.prog[eng]
            dec = getattr(block, eng)
            def make(fns):
                def body(engine):
                    for f in fns:
                        f(engine)
                return body
            dec(make(fns))


class _Rec:
    def __init__(self, eng, chain):
        object.__setattr__(self, "_e", eng)
        object.__setattr__(self, "last", None)
        object.__setattr__(self, "chain", chain)

    def __getattr__(self, n):
        a = getattr(self._e, n)
        if callable(a):
            def w(*args, **kw):
                if self.last is not None and self.chain is not None:
                    csem, _ = self.chain
                    self.last.then_inc(csem, 1)
                    self.chain[1] += 1
                    self._e.wait_ge(csem, self.chain[1])
                    object.__setattr__(self, "last", None)
                r = a(*args, **kw)
                if hasattr(r, "then_inc"):
                    object.__setattr__(self, "last", r)
                return r
            return w
        return a


def _mk(ctx, eng, waits, fn, inc_sem=None):
    sems = ctx.sem
    chain = ctx.chain.get(eng)
    waits = [(sn, ctx.ctr["ld"].v) if sn == "ld" else (sn, v) for (sn, v) in waits]
    waits = [(ctx.sem[sn], v) for (sn, v) in waits]
    inc_h = ctx.sem[inc_sem] if inc_sem is not None else None
    def f(engine):
        for (sh, val) in waits:
            engine.wait_ge(sh, val)
        rec = _Rec(engine, chain)
        fn(rec)
        if inc_h is not None:
            if rec.last is not None:
                rec.last.then_inc(inc_h, 1)
            else:
                engine.sem_inc(inc_h, 1)
    ctx.em(eng, f)
    if inc_sem is not None:
        return ctx.ctr[inc_sem].inc()
    return None



def emit_ln(ctx, h_tiles, out_tiles, ns, ready, ps_a, ps_b, ps_c,
            hsqA, hsqB, mean_row, var_row, tmp_row, eps_col,
            mean_bc, rstd_bc, onesb, ones1f, col0=0):
    nslab = ns // 512
    out_marks = []
    for n in range(nslab):
        c0 = col0 + n * 512
        sl = slice(c0, c0 + 512)
        def fsum(t, sl=sl):
            for K in range(DC):
                t.matmul(ps_a[0:1, :], onesb[:], h_tiles[K][:, sl],
                         start=(K == 0), stop=(K == DC - 1))
        m_sum = _mk(ctx, "tensor", [ready], fsum, "t")
        sq_marks = []
        for K in range(DC):
            buf = hsqA if K % 2 == 0 else hsqB
            w = [ready]
            if K >= 2:
                w.append(("t", sq_marks[K - 2]))
            m_sq = _mk(ctx, "scalar", w,
                       lambda s, K=K, buf=buf, sl=sl: s.square(out=buf[:], in_=h_tiles[K][:, sl]),
                       "s")
            def fmm(t, K=K, buf=buf):
                t.matmul(ps_b[0:1, :], onesb[:], buf[:],
                         start=(K == 0), stop=(K == DC - 1))
            m_mm = _mk(ctx, "tensor", [("s", m_sq)], fmm, "t")
            sq_marks.append(m_mm)
        m_stats = _mk(ctx, "vector", [("t", sq_marks[-1]), ("t", m_sum)],
                      lambda v, sl=sl: (
                          v.tensor_single_scalar(out=mean_row[:], in_=ps_a[0:1, :], scalar=1.0 / D, op=ALU.mult),
                          v.tensor_single_scalar(out=var_row[:], in_=ps_b[0:1, :], scalar=1.0 / D, op=ALU.mult),
                          v.tensor_tensor(out=tmp_row[:], in0=mean_row[:], in1=mean_row[:], op=ALU.mult),
                          v.tensor_tensor(out=var_row[:], in0=var_row[:], in1=tmp_row[:], op=ALU.subtract),
                      ), "v")
        m_rstd = _mk(ctx, "scalar", [("v", m_stats)],
                     lambda s: (
                         s.activation(out=var_row[:], in_=var_row[:], func=AF.Ln, bias=eps_col[:], scale=1.0),
                         s.activation(out=var_row[:], in_=var_row[:], func=AF.Exp, scale=-0.5),
                     ), "s")
        m_bc1 = _mk(ctx, "tensor", [("s", m_rstd)],
                    lambda t: t.matmul(ps_c[:, :], ones1f[:], mean_row[:], start=True, stop=True),
                    "t")
        m_bc1e = _mk(ctx, "vector", [("t", m_bc1)],
                     lambda v, n=n: v.tensor_copy(mean_bc[n][:], ps_c[:, :]), "v")
        m_bc2 = _mk(ctx, "tensor", [("v", m_bc1e)],
                    lambda t: t.matmul(ps_c[:, :], ones1f[:], var_row[:], start=True, stop=True),
                    "t")
        m_bc2e = _mk(ctx, "vector", [("t", m_bc2)],
                     lambda v, n=n: v.tensor_copy(rstd_bc[n][:], ps_c[:, :]), "v")
        for K in range(DC):
            m_ap = _mk(ctx, "vector", [("v", m_bc2e)],
                       lambda v, K=K, n=n, sl=sl: (
                           v.tensor_tensor(out=out_tiles[K][:, sl], in0=h_tiles[K][:, sl],
                                           in1=mean_bc[n][:], op=ALU.subtract),
                           v.tensor_tensor(out=out_tiles[K][:, sl], in0=out_tiles[K][:, sl],
                                           in1=rstd_bc[n][:], op=ALU.mult),
                       ), "v")
            out_marks.append(m_ap)
    return out_marks[-1]



def emit_stage_a(nc, ctx, DD):
    xf = DD["xf"]
    xt = DD["xt"]
    wqT = DD["wqT"]
    wkT = DD["wkT"]
    bqv = DD["bqv"]
    bkv = DD["bkv"]
    identbf = DD["identbf"]
    x1 = DD["x1"]
    SCALE = 1.0 / np.sqrt(D)

    with ExitStack() as es:
        E = es.enter_context
        pool = [E(nc.sbuf_tensor(f"pool{i}", [128, S], BF16)) for i in range(8)]
        wq_bf = [pool[i][:, 0:1024] for i in range(DC)]
        wk_bf = [pool[i][:, 1024:2048] for i in range(DC)]
        stg = [E(nc.sbuf_tensor(f"stg{i}", [128, S], F32)) for i in range(2)]
        bq_sb = E(nc.sbuf_tensor("bq_sb", [128, DC], F32))
        bk_sb = E(nc.sbuf_tensor("bk_sb", [128, DC], F32))
        ident = E(nc.sbuf_tensor("ident", [128, 128], BF16))
        onesb = E(nc.sbuf_tensor("onesb", [128, 1], BF16))
        ones1f = E(nc.sbuf_tensor("ones1f", [1, 128], F32))
        two_col = E(nc.sbuf_tensor("two_col", [128, 1], F32))
        eps_col = E(nc.sbuf_tensor("eps_col", [1, 1], F32))
        xf_bf = [E(nc.sbuf_tensor(f"xfb{i}", [128, S], BF16)) for i in range(DC)]
        xt_bf = [E(nc.sbuf_tensor(f"xtb{i}", [128, D], BF16)) for i in range(16)]
        q_bf = [E(nc.sbuf_tensor(f"qb{i}", [128, S], BF16)) for i in range(DC)]
        k_bf = [E(nc.sbuf_tensor(f"kb{i}", [128, S], BF16)) for i in range(DC)]
        p_grp = [pool[g][:, :] for g in range(4)]
        pt_grp = [pool[4 + tc // 4][:, (tc % 4) * 512:(tc % 4 + 1) * 512] for tc in range(16)]
        mrow = E(nc.sbuf_tensor("mrow", [128, 4], F32))
        mcol = E(nc.sbuf_tensor("mcol", [128, 1], F32))
        mbias = E(nc.sbuf_tensor("mbias", [128, 1], F32))
        zrow = E(nc.sbuf_tensor("zrow", [128, 4], F32))
        zcol = E(nc.sbuf_tensor("zcol", [128, 1], F32))
        corr = E(nc.sbuf_tensor("corr", [128, DC], F32))
        h_bf = xf_bf
        hsqA = E(nc.sbuf_tensor("hsqA", [128, 512], BF16))
        hsqB = E(nc.sbuf_tensor("hsqB", [128, 512], BF16))
        xf32 = E(nc.sbuf_tensor("xf32", [128, 512], F32))
        attn_sb = E(nc.sbuf_tensor("attn_sb", [128, 512], F32))
        mean_row = E(nc.sbuf_tensor("mean_row", [1, 512], F32))
        var_row = E(nc.sbuf_tensor("var_row", [1, 512], F32))
        tmp_row = E(nc.sbuf_tensor("tmp_row", [1, 512], F32))
        mean_bc = [E(nc.sbuf_tensor(f"meanbc{i}", [128, 512], BF16)) for i in range(NT)]
        rstd_bc = [E(nc.sbuf_tensor(f"rstdbc{i}", [128, 512], BF16)) for i in range(NT)]
        o_bf = h_bf

        ps_sc = [E(nc.psum_tensor(f"pssc{i}", [128, 512], F32)) for i in range(4)]
        ps_tr = E(nc.psum_tensor("pstr", [128, 128], BF16))
        ps_pv = E(nc.psum_tensor("pspv", [128, 512], F32))
        ps_ln = E(nc.psum_tensor("psln", [128, 512], F32))


        m_c1 = ctx.dma("sync", ident[:], identbf)
        m_c2 = ctx.dma("sync", bq_sb[:], bqv)
        m_c3 = ctx.dma("sync", bk_sb[:], bkv)
        m_const = _mk(ctx, "gpsimd", [],
                      lambda g: (g.memset(onesb[:], 1.0), g.memset(ones1f[:], 1.0),
                                 g.memset(two_col[:], 2.0), g.memset(eps_col[:], LN_EPS)),
                      "g")
        for _e in ("tensor", "vector", "scalar"):
            ctx.wait(_e, "ld", ctx.ctr["ld"].v)
            ctx.wait(_e, "g", ctx.ctr["g"].v)

        last_cast = {0: 0, 1: 0}
        wm = 0
        for wi, (wT, wdst) in enumerate([(wqT, wq_bf), (wkT, wk_bf)]):
            for i in range(DC):
                bi = (wi * DC + i) % 2
                ctx.wait("sync", "v", last_cast[bi])
                m_ld = ctx.dma("sync", stg[bi][:, 0:D], wT[i * 128:(i + 1) * 128, :])
                wm = _mk(ctx, "vector", [m_ld],
                         lambda v, bi=bi, dst=wdst[i]: v.tensor_copy(dst[:, :], stg[bi][:, 0:D]), "v")
                last_cast[bi] = wm
        w_done = wm

        for b in range(B_PER):
            if b > 0:
                ctx.wait("vector", "wr", ctx.ctr["wr"].v)
            xm = 0
            for i in range(DC):
                bi = i % 2
                ctx.wait("sync", "v", last_cast[bi])
                m_ld = ctx.dma("sync", stg[bi][:], xf[b, i * 128:(i + 1) * 128, :])
                xm = _mk(ctx, "vector", [m_ld],
                         lambda v, bi=bi, dst=xf_bf[i]: v.tensor_copy(dst[:], stg[bi][:]), "v")
                last_cast[bi] = xm
            for i in range(16):
                bi = i % 2
                ctx.wait("sync", "v", last_cast[bi])
                m_ld = ctx.dma("sync", stg[bi][:, 0:D], xt[b, i * 128:(i + 1) * 128, :])
                xm = _mk(ctx, "vector", [m_ld],
                         lambda v, bi=bi, dst=xt_bf[i]: v.tensor_copy(dst[:], stg[bi][:, 0:D]), "v")
                last_cast[bi] = xm
            x_done = xm

            evac_marks = []
            for wtiles, btile, dst in [(wq_bf, bq_sb, q_bf), (wk_bf, bk_sb, k_bf)]:
                for M in range(DC):
                    for n in range(NT):
                        bank = ps_sc[n % 4]
                        waits = [("v", max(w_done, x_done))]
                        if evac_marks:
                            idx = len(evac_marks) - 4
                            if idx >= 0:
                                waits.append(("s", evac_marks[idx]))
                        def fmm(t, wtiles=wtiles, M=M, n=n, bank=bank):
                            for K in range(DC):
                                t.matmul(bank[:], wtiles[K][:, M * 128:(M + 1) * 128],
                                         xf_bf[K][:, n * 512:(n + 1) * 512],
                                         start=(K == 0), stop=(K == DC - 1))
                        m_mm = _mk(ctx, "tensor", waits, fmm, "t")
                        m_ev = _mk(ctx, "scalar", [("t", m_mm)],
                                   lambda s, M=M, n=n, dst=dst, btile=btile, bank=bank:
                                   s.activation(out=dst[M][:, n * 512:(n + 1) * 512],
                                                in_=bank[:], func=AF.Identity,
                                                bias=btile[:, M:M + 1], scale=1.0),
                                   "s")
                        evac_marks.append(m_ev)
            qk_done_s = evac_marks[-1]

            corr_mark = 0
            for M in range(DC):
                m_cs = _mk(ctx, "tensor",
                           ([("v", x_done)] if M == 0 else []) +
                           ([("v", corr_mark)] if M > 0 else []),
                           lambda t, M=M: [t.matmul(ps_pv[:, 0:1],
                                                    xt_bf[K][:, M * 128:(M + 1) * 128], onesb[:],
                                                    start=(K == 0), stop=(K == 15)) for K in range(16)],
                           "t")
                corr_mark = _mk(ctx, "vector", [("t", m_cs)],
                                lambda v, M=M: v.tensor_single_scalar(
                                    out=corr[:, M:M + 1], in_=ps_pv[:, 0:1],
                                    scalar=-2.0 / S, op=ALU.mult), "v")

            h_mark = 0
            for grp in range(4):
                exp_marks = []
                norm_marks = []
                for sti in range(4):
                    st = grp * 4 + sti
                    waits = [("s", qk_done_s)]
                    if exp_marks:
                        waits.append(("s", exp_marks[-1]))
                    def fsc(t, st=st):
                        for K in range(DC):
                            for tc in range(4):
                                t.matmul(ps_sc[tc][:],
                                         q_bf[K][:, st * 128:(st + 1) * 128],
                                         k_bf[K][:, tc * 512:(tc + 1) * 512],
                                         start=(K == 0), stop=(K == DC - 1))
                    m_sc = _mk(ctx, "tensor", waits, fsc, "t")
                    m_max = _mk(ctx, "vector", [("t", m_sc)],
                                lambda v: (
                                    [v.tensor_reduce(out=mrow[:, tc:tc + 1], in_=ps_sc[tc][:],
                                                     axis=AX.X, op=ALU.max) for tc in range(4)],
                                    v.tensor_reduce(out=mcol[:], in_=mrow[:], axis=AX.X, op=ALU.max),
                                    v.tensor_single_scalar(out=mbias[:], in_=mcol[:],
                                                           scalar=-SCALE, op=ALU.mult),
                                ), "v")
                    m_exp = _mk(ctx, "scalar", [("v", m_max)],
                                lambda s, sti=sti: [s.activation(
                                    out=p_grp[sti][:, tc * 512:(tc + 1) * 512],
                                    in_=ps_sc[tc][:], func=AF.Exp,
                                    bias=mbias[:], scale=SCALE) for tc in range(4)],
                                "s")
                    exp_marks.append(m_exp)
                    m_norm = _mk(ctx, "vector", [("s", m_exp)],
                                 lambda v, sti=sti: (
                                     [v.tensor_reduce(out=zrow[:, tc:tc + 1],
                                                      in_=p_grp[sti][:, tc * 512:(tc + 1) * 512],
                                                      axis=AX.X, op=ALU.add) for tc in range(4)],
                                     v.tensor_reduce(out=zcol[:], in_=zrow[:], axis=AX.X, op=ALU.add),
                                     v.reciprocal(out=zcol[:], in_=zcol[:]),
                                     v.tensor_scalar_mul(p_grp[sti][:], p_grp[sti][:], zcol[:]),
                                 ), "v")
                    norm_marks.append(m_norm)
                norm_done = norm_marks[-1]

                tr_ev = 0
                for tc in range(16):
                    for sti in range(4):
                        waits = [("v", norm_done)]
                        if tr_ev:
                            waits.append(("v", tr_ev))
                        m_tr = _mk(ctx, "tensor", waits,
                                   lambda t, tc=tc, sti=sti: t.transpose(
                                       ps_tr[:], p_grp[sti][:, tc * 128:(tc + 1) * 128], ident[:]),
                                   "t")
                        tr_ev = _mk(ctx, "vector", [("t", m_tr)],
                                    lambda v, tc=tc, sti=sti: v.tensor_copy(
                                        pt_grp[tc][:, sti * 128:(sti + 1) * 128], ps_tr[:]),
                                    "v")
                trans_done = tr_ev

                for M in range(DC):
                    def fpv(t, M=M):
                        for K in range(16):
                            t.matmul(ps_pv[:], xt_bf[K][:, M * 128:(M + 1) * 128],
                                     pt_grp[K][:], start=(K == 0), stop=(K == 15))
                    waits = [("v", trans_done), ("v", corr_mark)]
                    if h_mark:
                        waits.append(("v", h_mark))
                    m_pv = _mk(ctx, "tensor", waits, fpv, "t")
                    if h_mark:
                        ctx.wait("gpsimd", "v", h_mark)
                    m_x32 = ctx.dma("gpsimd", xf32[:],
                                    xf[b, M * 128:(M + 1) * 128, grp * 512:(grp + 1) * 512],
                                    sem="ldg")
                    m_ep1 = _mk(ctx, "scalar", [("t", m_pv)],
                                lambda s, M=M: s.activation(out=attn_sb[:], in_=ps_pv[:],
                                                            func=AF.Identity, bias=corr[:, M:M + 1],
                                                            scale=1.0), "s")
                    h_mark = _mk(ctx, "vector", [("s", m_ep1), m_x32],
                                 lambda v, M=M, grp=grp: v.scalar_tensor_tensor(
                                     out=h_bf[M][:, grp * 512:(grp + 1) * 512],
                                     in0=xf32[:], scalar=two_col[:], in1=attn_sb[:],
                                     op0=ALU.mult, op1=ALU.add), "v")
            h_done = h_mark

            m_ln = emit_ln(ctx, h_bf, o_bf, S, ("v", h_done), ps_sc[0], ps_sc[1], ps_ln,
                           hsqA, hsqB, mean_row, var_row, tmp_row, eps_col,
                           mean_bc, rstd_bc, onesb, ones1f)
            for M in range(DC):
                ctx.wait("gpsimd", "v", m_ln)
                ctx.dma("gpsimd", x1[b, M * 128:(M + 1) * 128, :], o_bf[M][:], sem="wr")




def emit_stage_b(nc, ctx, DD):
    x1 = DD["x1"]
    wg = DD["wg"]
    w1eT = DD["w1eT"]
    w3eT = DD["w3eT"]
    w2eT = DD["w2eT"]
    w1sT = DD["w1sT"]
    w3sT = DD["w3sT"]
    w2sT = DD["w2sT"]
    identf = DD["identf"]
    x2 = DD["x2"]

    NE = 16

    with ExitStack() as es:
        E = es.enter_context
        x1_bf = [E(nc.sbuf_tensor(f"x1b{i}", [128, S], BF16)) for i in range(DC)]
        wg_bf = [E(nc.sbuf_tensor(f"wgb{i}", [128, 16], BF16)) for i in range(DC)]
        identfs = E(nc.sbuf_tensor("identfs", [128, 128], F32))
        stg = [E(nc.sbuf_tensor(f"stg{i}", [128, 1024], F32)) for i in range(2)]
        w1_sb = [[E(nc.sbuf_tensor(f"w1_{pp}_{i}", [128, 512], BF16)) for i in range(DC)] for pp in range(2)]
        w3_sb = [[E(nc.sbuf_tensor(f"w3_{pp}_{i}", [128, 512], BF16)) for i in range(DC)] for pp in range(2)]
        w2_sb = [[E(nc.sbuf_tensor(f"w2_{pp}_{i}", [128, 1024], BF16)) for i in range(4)] for pp in range(2)]
        gsc_fm = E(nc.sbuf_tensor("gsc_fm", [16, S], F32))
        gtm = E(nc.sbuf_tensor("gtm", [128, 16], F32))
        gates_fm = E(nc.sbuf_tensor("gates_fm", [16, S], F32))
        grow = E(nc.sbuf_tensor("grow", [1, S], F32))
        gate_bc = E(nc.sbuf_tensor("gate_bc", [128, S], BF16))
        t4a = E(nc.sbuf_tensor("t4a", [128, 4], F32))
        t4b = E(nc.sbuf_tensor("t4b", [128, 4], F32))
        t16a = E(nc.sbuf_tensor("t16a", [128, 16], F32))
        t16b = E(nc.sbuf_tensor("t16b", [128, 16], F32))
        c1 = E(nc.sbuf_tensor("c1", [128, 1], F32))
        c2 = E(nc.sbuf_tensor("c2", [128, 1], F32))
        nbig = E(nc.sbuf_tensor("nbig", [128, 1], F32))
        pbig = E(nc.sbuf_tensor("pbig", [128, 1], F32))
        g_sb = [E(nc.sbuf_tensor(f"g_sb{i}", [128, 512], BF16)) for i in range(4)]
        u_sb = [E(nc.sbuf_tensor(f"u_sb{i}", [128, 512], BF16)) for i in range(4)]
        racc = [E(nc.sbuf_tensor(f"racc{i}", [128, S], F32)) for i in range(DC)]
        h_bf = x1_bf
        hsqA = E(nc.sbuf_tensor("hsqA", [128, 512], BF16))
        hsqB = E(nc.sbuf_tensor("hsqB", [128, 512], BF16))
        mean_row = E(nc.sbuf_tensor("mean_row", [1, 512], F32))
        var_row = E(nc.sbuf_tensor("var_row", [1, 512], F32))
        tmp_row = E(nc.sbuf_tensor("tmp_row", [1, 512], F32))
        eps_col = E(nc.sbuf_tensor("eps_col", [1, 1], F32))
        mean_bc = [E(nc.sbuf_tensor(f"meanbc{i}", [128, 512], BF16)) for i in range(NT)]
        rstd_bc = [E(nc.sbuf_tensor(f"rstdbc{i}", [128, 512], BF16)) for i in range(NT)]
        onesb = E(nc.sbuf_tensor("onesb", [128, 1], BF16))
        ones1f = E(nc.sbuf_tensor("ones1f", [1, 128], F32))
        o_bf = h_bf

        ps_p = [E(nc.psum_tensor(f"psp{i}", [128, 512], F32)) for i in range(4)]
        ps_w2 = E(nc.psum_tensor("psw2", [128, 512], F32))
        ps_g = E(nc.psum_tensor("psg", [128, 512], F32))
        ps_ln = E(nc.psum_tensor("psln", [128, 512], F32))


        m_id = ctx.dma("sync", identfs[:], identf)
        _mk(ctx, "gpsimd", [],
            lambda g: (g.memset(onesb[:], 1.0), g.memset(ones1f[:], 1.0),
                       g.memset(eps_col[:], LN_EPS), g.memset(nbig[:], -BIG),
                       g.memset(pbig[:], BIG)), "g")
        m_gconst = ctx.ctr["g"].v
        for _e in ("tensor", "vector", "scalar"):
            ctx.wait(_e, "ld", ctx.ctr["ld"].v)
            ctx.wait(_e, "g", ctx.ctr["g"].v)

        last_cast = {0: 0, 1: 0}
        wgm = 0
        for i in range(DC):
            bi = i % 2
            ctx.wait("sync", "v", last_cast[bi])
            m_ld = ctx.dma("sync", stg[bi][:, 0:16], wg[i * 128:(i + 1) * 128, :])
            wgm = _mk(ctx, "vector", [m_ld],
                      lambda v, bi=bi, i=i: v.tensor_copy(wg_bf[i][:], stg[bi][:, 0:16]), "v")
            last_cast[bi] = wgm

        def load_expert(e, pp):
            nonlocal last_cast
            mm = 0
            srcs = []
            if e < 16:
                for i in range(DC):
                    srcs.append((w1eT[e, i * 128:(i + 1) * 128, :], w1_sb[pp][i], 512))
                for i in range(DC):
                    srcs.append((w3eT[e, i * 128:(i + 1) * 128, :], w3_sb[pp][i], 512))
                for i in range(4):
                    srcs.append((w2eT[e, i * 128:(i + 1) * 128, :], w2_sb[pp][i], 1024))
            else:
                half = e - 16
                c0 = half * 512
                for i in range(DC):
                    srcs.append((w1sT[i * 128:(i + 1) * 128, c0:c0 + 512], w1_sb[pp][i], 512))
                for i in range(DC):
                    srcs.append((w3sT[i * 128:(i + 1) * 128, c0:c0 + 512], w3_sb[pp][i], 512))
                for i in range(4):
                    srcs.append((w2sT[c0 + i * 128:c0 + (i + 1) * 128, :], w2_sb[pp][i], 1024))
            for j, (src, dst, w) in enumerate(srcs):
                bi = j % 2
                ctx.wait("sync", "v", last_cast[bi])
                m_ld = ctx.dma("sync", stg[bi][:, 0:w], src)
                mm = _mk(ctx, "vector", [m_ld],
                         lambda v, bi=bi, dst=dst, w=w: v.tensor_copy(dst[:], stg[bi][:, 0:w]), "v")
                last_cast[bi] = mm
            return mm

        for b in range(B_PER):
            if b > 0:
                ctx.wait("sync", "wr", ctx.ctr["wr"].v)
            x_marks = []
            for i in range(DC):
                x_marks.append(ctx.dma("sync", x1_bf[i][:], x1[b, i * 128:(i + 1) * 128, :]))
            x_done_ld = x_marks[-1]

            glast = 0
            for n in range(NT):
                def fg(t, n=n):
                    for K in range(DC):
                        t.matmul(ps_g[0:16, :], wg_bf[K][:], x1_bf[K][:, n * 512:(n + 1) * 512],
                                 start=(K == 0), stop=(K == DC - 1))
                waits = [x_done_ld, ("v", wgm)]
                if glast:
                    waits.append(("s", glast))
                m_g = _mk(ctx, "tensor", waits, fg, "t")
                glast = _mk(ctx, "scalar", [("t", m_g)],
                            lambda s, n=n: s.activation(out=gsc_fm[:, n * 512:(n + 1) * 512],
                                                        in_=ps_g[0:16, :], func=AF.Sigmoid),
                            "s")
            gdone = 0
            for stt in range(16):
                waits = [("s", glast), m_id, ("g", m_gconst)]
                if gdone:
                    waits.append(("v", gdone))
                m_tr = _mk(ctx, "tensor", waits,
                           lambda t, stt=stt: t.transpose(
                               ps_g[:, 0:16], gsc_fm[:, stt * 128:(stt + 1) * 128],
                               identfs[0:16, 0:16]),
                           "t")
                m_tm = _mk(ctx, "vector", [("t", m_tr)],
                           lambda v: v.tensor_copy(gtm[:], ps_g[:, 0:16]), "v")

                def ftopk(v):
                    sv = gtm[:, :].rearrange("p (g e) -> p g e", g=4)
                    v.tensor_reduce(out=t4a[:], in_=sv, axis=AX.X, op=ALU.max)
                    m1bc = t4a[:, :].unsqueeze(2).broadcast_to([128, 4, 4])
                    v.tensor_tensor(out=t16a[:, :].rearrange("p (g e) -> p g e", g=4),
                                    in0=sv, in1=m1bc, op=ALU.is_ge)
                    v.scalar_tensor_tensor(out=t16b[:], in0=t16a[:], scalar=nbig[:],
                                           in1=gtm[:], op0=ALU.mult, op1=ALU.add)
                    v.tensor_reduce(out=t4b[:], in_=t16b[:, :].rearrange("p (g e) -> p g e", g=4),
                                    axis=AX.X, op=ALU.max)
                    v.tensor_tensor(out=t4a[:], in0=t4a[:], in1=t4b[:], op=ALU.add)
                    v.tensor_reduce(out=c1[:], in_=t4a[:], axis=AX.X, op=ALU.max)
                    v.tensor_scalar(out=t4b[:], in0=t4a[:], scalar1=c1[:], scalar2=None,
                                    op0=ALU.is_ge)
                    v.scalar_tensor_tensor(out=t4b[:], in0=t4b[:], scalar=nbig[:],
                                           in1=t4a[:], op0=ALU.mult, op1=ALU.add)
                    v.tensor_reduce(out=c2[:], in_=t4b[:], axis=AX.X, op=ALU.max)
                    v.tensor_scalar(out=t4a[:], in0=t4a[:], scalar1=c2[:], scalar2=None,
                                    op0=ALU.is_ge)
                    v.tensor_single_scalar(out=t4a[:], in_=t4a[:], scalar=-1.0, op=ALU.add)
                    a_bc = t4a[:, :].unsqueeze(2).broadcast_to([128, 4, 4])
                    v.scalar_tensor_tensor(out=t16a[:, :].rearrange("p (g e) -> p g e", g=4),
                                           in0=a_bc, scalar=pbig[:],
                                           in1=sv, op0=ALU.mult, op1=ALU.add)
                    for it in range(3):
                        v.tensor_reduce(out=c1[:], in_=t16a[:], axis=AX.X, op=ALU.max)
                        v.tensor_scalar(out=t16b[:], in0=t16a[:], scalar1=c1[:], scalar2=None,
                                        op0=ALU.is_ge)
                        v.scalar_tensor_tensor(out=t16a[:], in0=t16b[:], scalar=nbig[:],
                                               in1=t16a[:], op0=ALU.mult, op1=ALU.add)
                    v.tensor_reduce(out=c1[:], in_=t16a[:], axis=AX.X, op=ALU.max)
                    a_bc2 = t4a[:, :].unsqueeze(2).broadcast_to([128, 4, 4])
                    v.scalar_tensor_tensor(out=t16b[:, :].rearrange("p (g e) -> p g e", g=4),
                                           in0=a_bc2, scalar=pbig[:],
                                           in1=sv, op0=ALU.mult, op1=ALU.add)
                    v.tensor_scalar(out=t16b[:], in0=t16b[:], scalar1=c1[:], scalar2=None,
                                    op0=ALU.is_ge)
                    v.tensor_tensor(out=t16b[:], in0=gtm[:], in1=t16b[:], op=ALU.mult)
                    v.tensor_reduce(out=c2[:], in_=t16b[:], axis=AX.X, op=ALU.add)
                    v.tensor_single_scalar(out=c2[:], in_=c2[:], scalar=1e-6, op=ALU.add)
                    v.reciprocal(out=c2[:], in_=c2[:])
                    v.tensor_scalar_mul(t16b[:], t16b[:], c2[:])
                m_tk = _mk(ctx, "vector", [("v", m_tm)], ftopk, "v")
                m_tb = _mk(ctx, "tensor", [("v", m_tk)],
                           lambda t: t.transpose(ps_g[0:16, 0:128], t16b[:], identfs[:]), "t")
                gdone = _mk(ctx, "vector", [("t", m_tb)],
                            lambda v, stt=stt: v.tensor_copy(
                                gates_fm[:, stt * 128:(stt + 1) * 128], ps_g[0:16, 0:128]), "v")

            racc_mark = 0
            for M in range(DC):
                racc_mark = _mk(ctx, "vector", [x_done_ld],
                                lambda v, M=M: v.tensor_copy(racc[M][:], x1_bf[M][:]), "v")

            m_wload = load_expert(0, 0)
            last_gb_t = 0
            evac_w2_mark = racc_mark
            p_evac = {0: 0, 1: 0, 2: 0, 3: 0}
            for e in range(18):
                pp = e % 2
                w_ready = m_wload
                if e < 17:
                    m_wload = load_expert(e + 1, (e + 1) % 2)
                if e < 16:
                    ctx.wait("gpsimd", "v", gdone)
                    if last_gb_t:
                        ctx.wait("gpsimd", "t", last_gb_t)
                    m_gr = ctx.dma("gpsimd", grow[:], gates_fm[e:e + 1, :], sem="ldg")
                    gb_marks = []
                    for n in range(NT):
                        m_gb = _mk(ctx, "tensor", [m_gr, ("v", gdone)] +
                                   ([("v", gb_marks[-1])] if gb_marks else []),
                                   lambda t, n=n: t.matmul(ps_g[:], ones1f[:],
                                                           grow[:, n * 512:(n + 1) * 512],
                                                           start=True, stop=True), "t")
                        last_gb_t = m_gb
                        gb_marks.append(_mk(ctx, "vector", [("t", m_gb)],
                                            lambda v, n=n: v.tensor_copy(
                                                gate_bc[:, n * 512:(n + 1) * 512], ps_g[:]), "v"))
                    gate_ready = gb_marks[-1]
                else:
                    gate_ready = 0

                for n in range(NT):
                    sl = slice(n * 512, (n + 1) * 512)
                    ev1 = []
                    for Mf in range(4):
                        waits = [("v", w_ready)]
                        if p_evac[Mf]:
                            waits.append(("s", p_evac[Mf]))
                        def f1(t, Mf=Mf, pp=pp, sl=sl):
                            for K in range(DC):
                                t.matmul(ps_p[Mf][:], w1_sb[pp][K][:, Mf * 128:(Mf + 1) * 128],
                                         x1_bf[K][:, sl], start=(K == 0), stop=(K == DC - 1))
                        m1 = _mk(ctx, "tensor", waits, f1, "t")
                        ev1.append(_mk(ctx, "scalar", [("t", m1)],
                                       lambda s, Mf=Mf: s.activation(out=g_sb[Mf][:], in_=ps_p[Mf][:],
                                                                     func=AF.Silu), "s"))
                    ev3 = []
                    for Mf in range(4):
                        def f3(t, Mf=Mf, pp=pp, sl=sl):
                            for K in range(DC):
                                t.matmul(ps_p[Mf][:], w3_sb[pp][K][:, Mf * 128:(Mf + 1) * 128],
                                         x1_bf[K][:, sl], start=(K == 0), stop=(K == DC - 1))
                        m3 = _mk(ctx, "tensor", [("s", ev1[Mf])], f3, "t")
                        ev3.append(_mk(ctx, "scalar", [("t", m3)],
                                       lambda s, Mf=Mf: s.activation(out=u_sb[Mf][:], in_=ps_p[Mf][:],
                                                                     func=AF.Copy), "s"))
                        p_evac[Mf] = ev3[-1]
                    gu_marks = []
                    for Mf in range(4):
                        if e < 16:
                            def fgu(v, Mf=Mf, sl=sl):
                                v.tensor_tensor(out=g_sb[Mf][:], in0=g_sb[Mf][:], in1=u_sb[Mf][:],
                                                op=ALU.mult)
                                v.tensor_tensor(out=g_sb[Mf][:], in0=g_sb[Mf][:],
                                                in1=gate_bc[:, sl], op=ALU.mult)
                            gu_marks.append(_mk(ctx, "vector",
                                                [("s", ev3[Mf]), ("v", gate_ready)], fgu, "v"))
                        else:
                            gu_marks.append(_mk(ctx, "vector", [("s", ev3[Mf])],
                                                lambda v, Mf=Mf: v.tensor_tensor(
                                                    out=g_sb[Mf][:], in0=g_sb[Mf][:],
                                                    in1=u_sb[Mf][:], op=ALU.mult), "v"))
                    gu_done = gu_marks[-1]
                    for Md in range(DC):
                        def fw2(t, Md=Md, pp=pp):
                            for Kf in range(4):
                                t.matmul(ps_w2[:], w2_sb[pp][Kf][:, Md * 128:(Md + 1) * 128],
                                         g_sb[Kf][:], start=(Kf == 0), stop=(Kf == 3))
                        waits = [("v", gu_done), ("v", evac_w2_mark)]
                        mw2 = _mk(ctx, "tensor", waits, fw2, "t")
                        evac_w2_mark = _mk(ctx, "vector", [("t", mw2)],
                                           lambda v, Md=Md, sl=sl: v.tensor_tensor(
                                               out=racc[Md][:, sl], in0=racc[Md][:, sl],
                                               in1=ps_w2[:], op=ALU.add), "v")
            ffn_done = evac_w2_mark

            hc = 0
            for M in range(DC):
                hc = _mk(ctx, "vector", [("v", ffn_done)],
                         lambda v, M=M: v.tensor_copy(h_bf[M][:], racc[M][:]), "v")
            m_ln = emit_ln(ctx, h_bf, o_bf, S, ("v", hc), ps_p[0], ps_p[1], ps_ln,
                           hsqA, hsqB, mean_row, var_row, tmp_row, eps_col,
                           mean_bc, rstd_bc, onesb, ones1f)
            for M in range(DC):
                ctx.wait("gpsimd", "v", m_ln)
                ctx.dma("gpsimd", x2[b, M * 128:(M + 1) * 128, :], o_bf[M][:], sem="wr")




def emit_stage_c(nc, ctx, DD):
    x2 = DD["x2"]
    wqkvT = DD["wqkvT"]
    woT = DD["woT"]
    wpl = DD["wpl"]
    pVgs = DD["pVgs"]
    pVT0 = DD["pVT0"]
    pBtau = DD["pBtau"]
    pBgm = DD["pBgm"]
    pCch = DD["pCch"]
    pGam = DD["pGam"]
    pAlp = DD["pAlp"]
    pIC = DD["pIC"]
    repsel = DD["repsel"]
    deltam = DD["deltam"]
    mklnT = DD["mklnT"]
    hselT = DD["hselT"]
    x3 = DD["x3"]

    CT = 512
    NCH = S // CT

    with ExitStack() as es:
        E = es.enter_context
        stg = [E(nc.sbuf_tensor(f"stg{i}", [128, 1024], F32)) for i in range(2)]
        wqkv_bf = [E(nc.sbuf_tensor(f"wqkv{i}", [128, 3 * D], BF16)) for i in range(DC)]
        wo_bf = [E(nc.sbuf_tensor(f"wo{i}", [128, D], BF16)) for i in range(DC)]
        wlt_bf = E(nc.sbuf_tensor("wlt_bf", [128, 1024], BF16))
        r_cols = E(nc.sbuf_tensor("r_cols", [128, 512], F32))
        c_cols = E(nc.sbuf_tensor("c_cols", [128, 512], F32))
        repsel_sb = E(nc.sbuf_tensor("repsel_sb", [128, 32 * 128], BF16))
        deltam_sb = E(nc.sbuf_tensor("deltam_sb", [128, 32 * 64], BF16))
        mkln_sb = E(nc.sbuf_tensor("mkln_sb", [128, 8 * 16], BF16))
        hsel_sb = E(nc.sbuf_tensor("hsel_sb", [16, 8 * 128], BF16))
        states = E(nc.sbuf_tensor("states", [128, 1024], F32))
        onescol = E(nc.sbuf_tensor("onescol", [64, 1], F32))
        ksq = E(nc.sbuf_tensor("ksq", [128, CT], BF16))
        kmu = E(nc.sbuf_tensor("kmu", [16, CT], F32))
        kvar = E(nc.sbuf_tensor("kvar", [16, CT], F32))
        ktmp = E(nc.sbuf_tensor("ktmp", [16, CT], F32))
        kmu_bf = E(nc.sbuf_tensor("kmu_bf", [16, CT], BF16))
        krs_bf = E(nc.sbuf_tensor("krs_bf", [16, CT], BF16))
        mu_bc = E(nc.sbuf_tensor("mu_bc", [128, CT], BF16))
        rs_bc = E(nc.sbuf_tensor("rs_bc", [128, CT], BF16))
        eps16 = E(nc.sbuf_tensor("eps16", [16, 1], F32))
        krep = E(nc.sbuf_tensor("krep", [128, CT], F32))
        qrep = E(nc.sbuf_tensor("qrep", [128, CT], F32))
        u_t = [E(nc.sbuf_tensor(f"u_t{i}", [128, CT], F32)) for i in range(2)]
        ftr = E(nc.sbuf_tensor("ftr", [128, CT], F32))
        p2 = [E(nc.sbuf_tensor(f"p2_{i}", [128, CT], BF16)) for i in range(2)]
        o_sb = E(nc.sbuf_tensor("o_sb", [128, 512], BF16))
        h_bf = [E(nc.sbuf_tensor(f"hb{i}", [128, 512], BF16)) for i in range(DC)]
        x3_bf = [E(nc.sbuf_tensor(f"x3b{i}", [128, 512], BF16)) for i in range(DC)]
        hsqA = E(nc.sbuf_tensor("hsqA", [128, 512], BF16))
        hsqB = E(nc.sbuf_tensor("hsqB", [128, 512], BF16))
        mean_row = E(nc.sbuf_tensor("mean_row", [1, 512], F32))
        var_row = E(nc.sbuf_tensor("var_row", [1, 512], F32))
        tmp_row = E(nc.sbuf_tensor("tmp_row", [1, 512], F32))
        eps1 = E(nc.sbuf_tensor("eps1", [1, 1], F32))
        mean_bc = [E(nc.sbuf_tensor("meanbcC", [128, 512], BF16))]
        rstd_bc = [E(nc.sbuf_tensor("rstdbcC", [128, 512], BF16))]
        onesb = E(nc.sbuf_tensor("onesb", [128, 1], BF16))
        ones1f = E(nc.sbuf_tensor("ones1f", [1, 128], F32))

        ps_a = [E(nc.psum_tensor(f"psa{i}", [128, 512], F32)) for i in range(4)]
        ps_vr = [E(nc.psum_tensor(f"psvr{i}", [128, 512], F32)) for i in range(2)]
        ps_y = E(nc.psum_tensor("psy", [64, 512], F32))
        ps_ln = E(nc.psum_tensor("pslnC", [128, 512], F32))


        cm = []
        cm.append(ctx.dma("sync", repsel_sb[:], repsel))
        cm.append(ctx.dma("sync", deltam_sb[:], deltam))
        cm.append(ctx.dma("sync", mkln_sb[:], mklnT))
        cm.append(ctx.dma("sync", hsel_sb[:], hselT))
        es2 = ExitStack()
        E2 = es2.enter_context
        pl = {}
        for nm in ["w", "vgs", "vt0", "btau", "bgm", "cch", "gam", "alp", "ic"]:
            pl[nm] = E2(nc.sbuf_tensor("pl_" + nm, [64, 1024], F32))
        sA = E2(nc.sbuf_tensor("sA", [64, 1024], F32))
        sB = E2(nc.sbuf_tensor("sB", [64, 1024], F32))
        sC = E2(nc.sbuf_tensor("sC", [64, 1024], F32))
        r_pl = E2(nc.sbuf_tensor("r_pl", [64, 1024], F32))
        c_pl = E2(nc.sbuf_tensor("c_pl", [64, 1024], F32))
        for nm, src in [("w", wpl), ("vgs", pVgs), ("vt0", pVT0), ("btau", pBtau),
                        ("bgm", pBgm), ("cch", pCch), ("gam", pGam), ("alp", pAlp), ("ic", pIC)]:
            cm.append(ctx.dma("sync", pl[nm][:], src))
        pl_ld = cm[-1]
        m_gc = _mk(ctx, "gpsimd", [],
                   lambda g: (g.memset(onesb[:], 1.0), g.memset(ones1f[:], 1.0),
                              g.memset(eps16[:], LN_EPS), g.memset(eps1[:], LN_EPS),
                              g.memset(onescol[:], 1.0)), "g")
        for _e in ("tensor", "vector", "scalar"):
            ctx.wait(_e, "ld", ctx.ctr["ld"].v)
            ctx.wait(_e, "g", ctx.ctr["g"].v)

        def f1(v):
            v.tensor_tensor(out=sA[:], in0=pl["vgs"][:], in1=pl["vt0"][:], op=ALU.subtract)
            v.tensor_tensor(out=sA[:], in0=sA[:], in1=pl["w"][:], op=ALU.add)
        m1 = _mk(ctx, "vector", [pl_ld], f1, "v")
        def f2(s):
            s.activation(out=sB[:], in_=sA[:], func=AF.Abs)
            s.activation(out=sB[:], in_=sB[:], func=AF.Exp, scale=-1.0)
        m2 = _mk(ctx, "scalar", [("v", m1), ("g", m_gc)], f2, "s")
        def f3(s):
            s.activation(out=sB[:], in_=sB[:], func=AF.Ln, bias=onescol[:], scale=1.0)
        m3 = _mk(ctx, "scalar", [("s", m2)], f3, "s")
        def f4(v):
            v.tensor_relu(sC[:], sA[:])
            v.tensor_tensor(out=sB[:], in0=sB[:], in1=sC[:], op=ALU.add)
            v.tensor_tensor(out=sC[:], in0=pl["btau"][:], in1=sB[:], op=ALU.mult)
            v.reciprocal(out=r_pl[:], in_=pl["cch"][:])
            v.tensor_tensor(out=r_pl[:], in0=sC[:], in1=r_pl[:], op=ALU.mult)
        m4 = _mk(ctx, "vector", [("s", m3)], f4, "v")
        def f5(s):
            s.activation(out=r_pl[:], in_=r_pl[:], func=AF.Exp, scale=-1.0)
            s.activation(out=c_pl[:], in_=sA[:], func=AF.Sigmoid)
        m5 = _mk(ctx, "scalar", [("v", m4)], f5, "s")
        def f6(v):
            v.tensor_tensor(out=c_pl[:], in0=c_pl[:], in1=sB[:], op=ALU.mult)
            v.tensor_tensor(out=c_pl[:], in0=c_pl[:], in1=pl["bgm"][:], op=ALU.mult)
            v.tensor_tensor(out=sB[:], in0=sC[:], in1=pl["ic"][:], op=ALU.subtract)
            v.tensor_tensor(out=sB[:], in0=sB[:], in1=pl["alp"][:], op=ALU.mult)
        m6 = _mk(ctx, "vector", [("s", m5)], f6, "v")
        def f7(s):
            s.activation(out=sB[:], in_=sB[:], func=AF.Tanh)
        m7 = _mk(ctx, "scalar", [("v", m6)], f7, "s")
        def f8(v):
            v.tensor_tensor(out=c_pl[:], in0=c_pl[:], in1=sB[:], op=ALU.mult)
            v.tensor_tensor(out=c_pl[:], in0=c_pl[:], in1=pl["gam"][:], op=ALU.mult)
            for dpar in range(2):
                for (colsrc, coldst) in [(r_pl, r_cols), (c_pl, c_cols)]:
                    src = colsrc[:, :].rearrange("e (h d) -> e h d", h=16)[:, :, dpar::2]
                    dst = coldst[dpar * 64:(dpar + 1) * 64, :].rearrange("e (h j) -> e h j", h=16)
                    v.tensor_copy(dst, src)
            v.tensor_copy(wlt_bf[0:64, :], pl["w"][:])
            v.tensor_copy(wlt_bf[64:128, :], pl["w"][:])
            v.memset(states[:], 0.0)
        m_planes = _mk(ctx, "vector", [("s", m7)], f8, "v")
        es2.close()
        x2c = [E(nc.sbuf_tensor(f"x2c_{i}", [128, CT], BF16)) for i in range(DC)]
        qc = [E(nc.sbuf_tensor(f"qc_{i}", [128, CT], BF16)) for i in range(DC)]
        kc = [E(nc.sbuf_tensor(f"kc_{i}", [128, CT], BF16)) for i in range(DC)]
        vc = [E(nc.sbuf_tensor(f"vc_{i}", [128, CT], BF16)) for i in range(DC)]
        yc = [E(nc.sbuf_tensor(f"yc_{i}", [128, CT], BF16)) for i in range(DC)]

        last_cast = {0: 0, 1: 0}
        wm = 0
        for i in range(DC):
            for part in range(3):
                bi = (i * 3 + part) % 2
                ctx.wait("sync", "v", last_cast[bi])
                m_ld = ctx.dma("sync", stg[bi][:], wqkvT[i * 128:(i + 1) * 128,
                                                        part * 1024:(part + 1) * 1024])
                def fc(v, bi=bi, i=i, part=part):
                    v.tensor_copy(wqkv_bf[i][:, part * 1024:(part + 1) * 1024], stg[bi][:])
                wm = _mk(ctx, "vector", [m_ld], fc, "v")
                last_cast[bi] = wm
        for i in range(DC):
            bi = i % 2
            ctx.wait("sync", "v", last_cast[bi])
            m_ld = ctx.dma("sync", stg[bi][:], woT[i * 128:(i + 1) * 128, :])
            def fc2(v, bi=bi, i=i):
                v.tensor_copy(wo_bf[i][:], stg[bi][:])
            wm = _mk(ctx, "vector", [m_ld], fc2, "v")
            last_cast[bi] = wm
        w_done = wm

        chunk_consumed = m_planes
        u_free = [0, 0]
        p2_free = [0, 0]
        vr_free = [0, 0]
        y_wr = 0

        for ch in range(NCH):
            t0 = ch * CT
            for b in range(B_PER):
                ctx.wait("sync", "v", chunk_consumed)
                xm = []
                for i in range(DC):
                    xm.append(ctx.dma("sync", x2c[i][:], x2[b, i * 128:(i + 1) * 128, t0:t0 + CT]))
                x_ld = xm[-1]
                evq = None
                evq_marks = []
                for part, dstt in [(0, qc), (1, kc), (2, vc)]:
                    for M in range(DC):
                        bank = ps_a[(part * DC + M) % 4]
                        waits = [x_ld, ("v", max(w_done, m_planes))]
                        if len(evq_marks) >= 4:
                            waits.append(("s", evq_marks[-4]))
                        def fqkv(t, part=part, M=M, bank=bank, b=b):
                            for K in range(DC):
                                t.matmul(bank[:], wqkv_bf[K][:, part * 1024 + M * 128:
                                                             part * 1024 + (M + 1) * 128],
                                         x2c[K][:], start=(K == 0), stop=(K == DC - 1))
                        m_mm = _mk(ctx, "tensor", waits, fqkv, "t")
                        def fev(s, dstt=dstt, M=M, bank=bank):
                            s.activation(out=dstt[M][:], in_=bank[:], func=AF.Copy)
                        evq = _mk(ctx, "scalar", [("t", m_mm)], fev, "s")
                        evq_marks.append(evq)
                qkv_done = evq

                def fmu(t, b=b):
                    for j in range(DC):
                        t.matmul(ps_a[0][0:16, :], mkln_sb[:, j * 16:(j + 1) * 16], kc[j][:],
                                 start=(j == 0), stop=(j == DC - 1))
                m_mu = _mk(ctx, "tensor", [("s", qkv_done)], fmu, "t")
                sq_prev = None
                for j in range(DC):
                    w = [("s", qkv_done)] if j == 0 else [("t", sq_prev)]
                    m_s = _mk(ctx, "scalar", w,
                              lambda s, j=j, b=b: s.square(out=ksq[:], in_=kc[j][:]), "s")
                    def fsq(t, j=j):
                        t.matmul(ps_a[1][0:16, :], mkln_sb[:, j * 16:(j + 1) * 16], ksq[:],
                                 start=(j == 0), stop=(j == DC - 1))
                    sq_prev = _mk(ctx, "tensor", [("s", m_s)], fsq, "t")
                def fkstat(v):
                    v.tensor_single_scalar(out=kmu[:], in_=ps_a[0][0:16, :], scalar=1.0 / 64, op=ALU.mult)
                    v.tensor_single_scalar(out=kvar[:], in_=ps_a[1][0:16, :], scalar=1.0 / 64, op=ALU.mult)
                    v.tensor_tensor(out=ktmp[:], in0=kmu[:], in1=kmu[:], op=ALU.mult)
                    v.tensor_tensor(out=kvar[:], in0=kvar[:], in1=ktmp[:], op=ALU.subtract)
                    v.tensor_copy(kmu_bf[:], kmu[:])
                m_kst = _mk(ctx, "vector", [("t", m_mu), ("t", sq_prev)], fkstat, "v")
                def fkrs(s):
                    s.activation(out=kvar[:], in_=kvar[:], func=AF.Ln, bias=eps16[:], scale=1.0)
                    s.activation(out=kvar[:], in_=kvar[:], func=AF.Exp, scale=-0.5)
                m_krs = _mk(ctx, "scalar", [("v", m_kst)], fkrs, "s")
                m_krsc = _mk(ctx, "vector", [("s", m_krs)],
                             lambda v: v.tensor_copy(krs_bf[:], kvar[:]), "v")
                kn_done = None
                for j in range(DC):
                    def fbc1(t, j=j):
                        t.matmul(ps_a[2][:, :], hsel_sb[:, j * 128:(j + 1) * 128], kmu_bf[:],
                                 start=True, stop=True)
                        t.matmul(ps_a[3][:, :], hsel_sb[:, j * 128:(j + 1) * 128], krs_bf[:],
                                 start=True, stop=True)
                    m_bc = _mk(ctx, "tensor", [("v", m_krsc)], fbc1, "t")
                    def fbce(v, j=j, b=b):
                        v.tensor_copy(mu_bc[:], ps_a[2][:, :])
                        v.tensor_copy(rs_bc[:], ps_a[3][:, :])
                        v.tensor_tensor(out=kc[j][:], in0=kc[j][:], in1=mu_bc[:], op=ALU.subtract)
                        v.tensor_tensor(out=kc[j][:], in0=kc[j][:], in1=rs_bc[:], op=ALU.mult)
                    kn_done = _mk(ctx, "vector", [("t", m_bc)], fbce, "v")
                    if j < DC - 1:
                        ctx.wait("tensor", "v", kn_done)

                for h in range(16):
                    tile = h // 2
                    r0 = (h % 2) * 64
                    kn_slice = kc[tile][r0:r0 + 64, :]
                    q_slice = qc[tile][r0:r0 + 64, :]
                    v_slice = vc[tile][r0:r0 + 64, :]
                    def frep(v, kn_slice=kn_slice, q_slice=q_slice):
                        v.tensor_copy(krep[0:64, :], kn_slice)
                        v.tensor_copy(krep[64:128, :], kn_slice)
                        v.tensor_copy(qrep[0:64, :], q_slice)
                        v.tensor_copy(qrep[64:128, :], q_slice)
                    waits = [("v", kn_done)]
                    m_rep = _mk(ctx, "vector", waits, frep, "v")
                    def fw(t, h=h, q_slice=q_slice, r0=r0):
                        t.matmul(ps_y[:, :], wlt_bf[r0:r0 + 64, h * 64:(h + 1) * 64], q_slice,
                                 start=True, stop=False, skip_group_check=True)
                    w_y = [("s", qkv_done), ("v", m_planes)]
                    if y_wr:
                        w_y.append(("s", y_wr))
                    m_w = _mk(ctx, "tensor", w_y, fw, "t")
                    last_scan = None
                    for j in range(32):
                        pp = j % 2
                        gidx = b * 512 + h * 32 + j
                        cidx = h * 32 + j
                        def fvr(t, j=j, pp=pp, v_slice=v_slice, r0=r0):
                            t.matmul(ps_vr[pp][:, :], repsel_sb[r0:r0 + 64, j * 128:(j + 1) * 128],
                                     v_slice, start=True, stop=True, skip_group_check=True)
                        w_vr = [("s", qkv_done)]
                        if vr_free[pp]:
                            w_vr.append(("v", vr_free[pp]))
                        m_vr = _mk(ctx, "tensor", w_vr, fvr, "t")
                        def fu(v, pp=pp, cidx=cidx):
                            v.scalar_tensor_tensor(out=u_t[pp][:], in0=ps_vr[pp][:, :],
                                                   scalar=c_cols[:, cidx:cidx + 1], in1=krep[:],
                                                   op0=ALU.mult, op1=ALU.mult)
                        w_u = [("t", m_vr), ("v", m_rep)]
                        if u_free[pp]:
                            w_u.append(("v", u_free[pp]))
                        m_u = _mk(ctx, "vector", w_u, fu, "v")
                        vr_free[pp] = m_u
                        def fscan(v, pp=pp, gidx=gidx, cidx=cidx):
                            v.tensor_tensor_scan(out=ftr[:], data0=r_cols[:, cidx:cidx + 1].broadcast_to([128, CT]),
                                                 data1=u_t[pp][:], initial=states[:, gidx:gidx + 1],
                                                 op0=ALU.mult, op1=ALU.add)
                            v.tensor_copy(states[:, gidx:gidx + 1], ftr[:, CT - 1:CT])
                            v.tensor_tensor(out=p2[pp][:], in0=ftr[:], in1=qrep[:], op=ALU.mult)
                        w_s = [("v", m_u)]
                        if p2_free[pp]:
                            w_s.append(("t", p2_free[pp]))
                        m_scan = _mk(ctx, "vector", w_s, fscan, "v")
                        u_free[pp] = m_scan
                        def fro(t, j=j, pp=pp):
                            t.matmul(ps_y[:, :], deltam_sb[:, j * 64:(j + 1) * 64], p2[pp][:],
                                     start=False, stop=(j == 31), skip_group_check=True)
                        m_ro = _mk(ctx, "tensor", [("v", m_scan), ("t", m_w)], fro, "t")
                        p2_free[pp] = m_ro
                        last_scan = m_ro
                    def fye(s, b=b, tile=tile, r0=r0):
                        s.activation(out=yc[tile][r0:r0 + 64, :], in_=ps_y[:, :], func=AF.Copy)
                    y_wr = _mk(ctx, "scalar", [("t", last_scan)], fye, "s")
                y_done = y_wr

                evo = None
                for M in range(DC):
                    def fwo(t, M=M, b=b):
                        for K in range(DC):
                            t.matmul(ps_ln[:, :], wo_bf[K][:, M * 128:(M + 1) * 128], yc[K][:],
                                     start=(K == 0), stop=(K == DC - 1))
                    w_wo = [("s", y_done)]
                    if evo:
                        w_wo.append(("v", evo))
                    m_wo = _mk(ctx, "tensor", w_wo, fwo, "t")
                    def fre(v, M=M, b=b):
                        v.tensor_copy(o_sb[:], ps_ln[:, :])
                        v.tensor_tensor(out=h_bf[M][:], in0=o_sb[:], in1=x2c[M][:], op=ALU.add)
                    evo = _mk(ctx, "vector", [("t", m_wo)], fre, "v")
                h_done = evo
                ctx.wait("vector", "wr", ctx.ctr["wr"].v)
                m_ln = emit_ln(ctx, h_bf, x3_bf, 512, ("v", h_done), ps_a[0], ps_a[1], ps_ln,
                               hsqA, hsqB, mean_row, var_row, tmp_row, eps1,
                               mean_bc, rstd_bc, onesb, ones1f)
                for M in range(DC):
                    ctx.wait("gpsimd", "v", m_ln)
                    ctx.dma("gpsimd", x3[b, M * 128:(M + 1) * 128, t0:t0 + CT], x3_bf[M][:], sem="wr")
                chunk_consumed = m_ln




def emit_stage_d(nc, ctx, DD):
    x3 = DD["x3"]
    rw = DD["rw"]
    rb = DD["rb"]
    sw1 = DD["sw1"]
    sw2 = DD["sw2"]
    sw3 = DD["sw3"]
    ew1 = DD["ew1"]
    ew2 = DD["ew2"]
    ew3 = DD["ew3"]
    identf = DD["identf"]
    outp = DD["outp"]

    with ExitStack() as es:
        E = es.enter_context
        x3_bf = [E(nc.sbuf_tensor(f"x3b{i}", [128, S], BF16)) for i in range(DC)]
        racc = [E(nc.sbuf_tensor(f"racc{i}", [128, S], F32)) for i in range(DC)]
        stg = [E(nc.sbuf_tensor(f"stg{i}", [128, 1024], F32)) for i in range(2)]
        w1d = [E(nc.sbuf_tensor(f"w1d{i}", [128, 1024], BF16)) for i in range(DC)]
        w2d = [E(nc.sbuf_tensor(f"w2d{i}", [128, 1024], BF16)) for i in range(DC)]
        w3d = [E(nc.sbuf_tensor(f"w3d{i}", [128, 1024], BF16)) for i in range(DC)]
        rw_bf = [E(nc.sbuf_tensor(f"rwb{i}", [128, 8], BF16)) for i in range(DC)]
        rb_sb = E(nc.sbuf_tensor("rb_sb", [8, 1], F32))
        identfs = E(nc.sbuf_tensor("identfs", [128, 128], F32))
        rout_fm = E(nc.sbuf_tensor("rout_fm", [8, S], F32))
        gates_fm = E(nc.sbuf_tensor("gates_fm", [8, S], F32))
        grow = E(nc.sbuf_tensor("grow", [1, S], F32))
        gate_bc = E(nc.sbuf_tensor("gate_bc", [128, S], BF16))
        gtm = E(nc.sbuf_tensor("gtm", [128, 8], F32))
        t8a = E(nc.sbuf_tensor("t8a", [128, 8], F32))
        t8b = E(nc.sbuf_tensor("t8b", [128, 8], F32))
        c1 = E(nc.sbuf_tensor("c1", [128, 1], F32))
        c2 = E(nc.sbuf_tensor("c2", [128, 1], F32))
        nbig = E(nc.sbuf_tensor("nbig", [128, 1], F32))
        g_sb = [E(nc.sbuf_tensor(f"g_sb{i}", [128, 512], BF16)) for i in range(DC)]
        u_sb = [E(nc.sbuf_tensor(f"u_sb{i}", [128, 512], BF16)) for i in range(DC)]
        h_bf = g_sb
        hsqA = E(nc.sbuf_tensor("hsqA", [128, 512], BF16))
        hsqB = E(nc.sbuf_tensor("hsqB", [128, 512], BF16))
        mean_row = E(nc.sbuf_tensor("mean_row", [1, 512], F32))
        var_row = E(nc.sbuf_tensor("var_row", [1, 512], F32))
        tmp_row = E(nc.sbuf_tensor("tmp_row", [1, 512], F32))
        eps1 = E(nc.sbuf_tensor("eps1", [1, 1], F32))
        mean_bc = [E(nc.sbuf_tensor("meanbcD", [128, 512], BF16))]
        rstd_bc = [E(nc.sbuf_tensor("rstdbcD", [128, 512], BF16))]
        onesb = E(nc.sbuf_tensor("onesb", [128, 1], BF16))
        ones1f = E(nc.sbuf_tensor("ones1f", [1, 128], F32))

        ps_p = [E(nc.psum_tensor(f"psp{i}", [128, 512], F32)) for i in range(4)]
        ps_w2 = E(nc.psum_tensor("psw2", [128, 512], F32))
        ps_g = E(nc.psum_tensor("psg", [128, 512], F32))
        ps_ln = E(nc.psum_tensor("pslnD", [128, 512], F32))


        m_id = ctx.dma("sync", identfs[:], identf)
        m_rb = ctx.dma("sync", rb_sb[:], rb)
        m_gc = _mk(ctx, "gpsimd", [],
                   lambda g: (g.memset(onesb[:], 1.0), g.memset(ones1f[:], 1.0),
                              g.memset(eps1[:], LN_EPS), g.memset(nbig[:], -BIG)), "g")
        for _e in ("tensor", "vector", "scalar"):
            ctx.wait(_e, "ld", ctx.ctr["ld"].v)
            ctx.wait(_e, "g", ctx.ctr["g"].v)

        last_cast = {0: 0, 1: 0}
        wm = 0
        for i in range(DC):
            bi = i % 2
            ctx.wait("sync", "v", last_cast[bi])
            m_ld = ctx.dma("sync", stg[bi][:, 0:8], rw[i * 128:(i + 1) * 128, :])
            def fc(v, bi=bi, i=i):
                v.tensor_copy(rw_bf[i][:], stg[bi][:, 0:8])
            wm = _mk(ctx, "vector", [m_ld], fc, "v")
            last_cast[bi] = wm
        rw_done = wm

        def load_half(e, half):
            nonlocal wm
            h0 = half * 1024
            srcs = []
            if e < 8:
                for i in range(DC):
                    srcs.append((ew1[e, i * 128:(i + 1) * 128, h0:h0 + 1024], w1d[i]))
                for i in range(DC):
                    srcs.append((ew2[e, i * 128:(i + 1) * 128, h0:h0 + 1024], w2d[i]))
                for i in range(DC):
                    srcs.append((ew3[e, h0 + i * 128:h0 + (i + 1) * 128, :], w3d[i]))
            else:
                for i in range(DC):
                    srcs.append((sw1[i * 128:(i + 1) * 128, h0:h0 + 1024], w1d[i]))
                for i in range(DC):
                    srcs.append((sw2[i * 128:(i + 1) * 128, h0:h0 + 1024], w2d[i]))
                for i in range(DC):
                    srcs.append((sw3[h0 + i * 128:h0 + (i + 1) * 128, :], w3d[i]))
            marks = []
            for j, (src, dst) in enumerate(srcs):
                bi = j % 2
                ctx.wait("sync", "v", last_cast[bi])
                m_ld = ctx.dma("sync", stg[bi][:], src)
                def fc(v, bi=bi, dst=dst):
                    v.tensor_copy(dst[:], stg[bi][:])
                wm = _mk(ctx, "vector", [m_ld], fc, "v")
                last_cast[bi] = wm
                marks.append(wm)
            return marks[-1]

        for b in range(B_PER):
            if b > 0:
                ctx.wait("sync", "wr", ctx.ctr["wr"].v)
            xm = []
            for i in range(DC):
                xm.append(ctx.dma("sync", x3_bf[i][:], x3[b, i * 128:(i + 1) * 128, :]))
            x_ld = xm[-1]

            rlast = 0
            for n in range(NT):
                def fr(t, n=n):
                    for K in range(DC):
                        t.matmul(ps_g[0:8, :], rw_bf[K][:], x3_bf[K][:, n * 512:(n + 1) * 512],
                                 start=(K == 0), stop=(K == DC - 1))
                waits = [x_ld, m_rb, ("v", rw_done)]
                if rlast:
                    waits.append(("s", rlast))
                m_r = _mk(ctx, "tensor", waits, fr, "t")
                def fre(s, n=n):
                    s.activation(out=rout_fm[:, n * 512:(n + 1) * 512], in_=ps_g[0:8, :],
                                 func=AF.Identity, bias=rb_sb[:], scale=1.0)
                rlast = _mk(ctx, "scalar", [("t", m_r)], fre, "s")
            gdone = 0
            for stt in range(16):
                waits = [("s", rlast), m_id, ("g", m_gc)]
                if gdone:
                    waits.append(("v", gdone))
                m_tr = _mk(ctx, "tensor", waits,
                           lambda t, stt=stt: t.transpose(
                               ps_g[:, 0:8], rout_fm[:, stt * 128:(stt + 1) * 128],
                               identfs[0:8, 0:8]),
                           "t")
                def ftop(v):
                    v.tensor_copy(gtm[:], ps_g[:, 0:8])
                    v.tensor_reduce(out=c1[:], in_=gtm[:], axis=AX.X, op=ALU.max)
                    v.tensor_single_scalar(out=c1[:], in_=c1[:], scalar=-1.0, op=ALU.mult)
                m_t1 = _mk(ctx, "vector", [("t", m_tr)], ftop, "v")
                def fexp(s):
                    s.activation(out=gtm[:], in_=gtm[:], func=AF.Exp, bias=c1[:], scale=1.0)
                m_t2 = _mk(ctx, "scalar", [("v", m_t1)], fexp, "s")
                def ftop2(v, stt=stt):
                    v.tensor_reduce(out=c2[:], in_=gtm[:], axis=AX.X, op=ALU.add)
                    v.reciprocal(out=c2[:], in_=c2[:])
                    v.tensor_scalar_mul(gtm[:], gtm[:], c2[:])
                    v.tensor_reduce(out=c1[:], in_=gtm[:], axis=AX.X, op=ALU.max)
                    v.tensor_scalar(out=t8a[:], in0=gtm[:], scalar1=c1[:], scalar2=None,
                                    op0=ALU.is_ge)
                    v.scalar_tensor_tensor(out=t8b[:], in0=t8a[:], scalar=nbig[:],
                                           in1=gtm[:], op0=ALU.mult, op1=ALU.add)
                    v.tensor_reduce(out=c2[:], in_=t8b[:], axis=AX.X, op=ALU.max)
                    v.tensor_scalar(out=t8a[:], in0=gtm[:], scalar1=c2[:], scalar2=None,
                                    op0=ALU.is_ge)
                    v.tensor_tensor(out=t8a[:], in0=gtm[:], in1=t8a[:], op=ALU.mult)
                    v.tensor_reduce(out=c2[:], in_=t8a[:], axis=AX.X, op=ALU.add)
                    v.reciprocal(out=c2[:], in_=c2[:])
                    v.tensor_scalar_mul(t8a[:], t8a[:], c2[:])
                m_t3 = _mk(ctx, "vector", [("s", m_t2)], ftop2, "v")
                m_tb = _mk(ctx, "tensor", [("v", m_t3)],
                           lambda t: t.transpose(ps_g[0:8, 0:128], t8a[:], identfs[:]), "t")
                gdone = _mk(ctx, "vector", [("t", m_tb)],
                            lambda v, stt=stt: v.tensor_copy(
                                gates_fm[:, stt * 128:(stt + 1) * 128], ps_g[0:8, 0:128]), "v")

            racc_mark = 0
            for M in range(DC):
                racc_mark = _mk(ctx, "vector", [x_ld],
                                lambda v, M=M: v.tensor_copy(racc[M][:], x3_bf[M][:]), "v")

            m_wload = load_half(0, 0)
            last_gb_t = 0
            evac_mark = racc_mark
            p_evac = {0: 0, 1: 0, 2: 0, 3: 0}
            for ee in range(18):
                e, half = ee // 2, ee % 2
                pp = 0
                w_ready = m_wload
                if e < 8 and half == 0:
                    ctx.wait("gpsimd", "v", gdone)
                    if last_gb_t:
                        ctx.wait("gpsimd", "t", last_gb_t)
                    m_gr = ctx.dma("gpsimd", grow[:], gates_fm[e:e + 1, :], sem="ldg")
                    gb = 0
                    for n in range(NT):
                        m_gb = _mk(ctx, "tensor", [m_gr, ("v", gdone)] +
                                   ([("v", gb)] if gb else []),
                                   lambda t, n=n: t.matmul(ps_g[:], ones1f[:],
                                                           grow[:, n * 512:(n + 1) * 512],
                                                           start=True, stop=True), "t")
                        last_gb_t = m_gb
                        gb = _mk(ctx, "vector", [("t", m_gb)],
                                 lambda v, n=n: v.tensor_copy(
                                     gate_bc[:, n * 512:(n + 1) * 512], ps_g[:]), "v")
                    gate_ready = gb
                elif e >= 8:
                    gate_ready = 0
                for n in range(NT):
                    sl = slice(n * 512, (n + 1) * 512)
                    ev1 = [0] * DC
                    for Mf in range(DC):
                        bank = ps_p[Mf % 4]
                        waits = [("v", w_ready)]
                        if p_evac[Mf % 4]:
                            waits.append(("s", p_evac[Mf % 4]))
                        def f1(t, Mf=Mf, bank=bank, sl=sl):
                            for K in range(DC):
                                t.matmul(bank[:], w1d[K][:, Mf * 128:(Mf + 1) * 128],
                                         x3_bf[K][:, sl], start=(K == 0), stop=(K == DC - 1))
                        mm1 = _mk(ctx, "tensor", waits, f1, "t")
                        ev1[Mf] = _mk(ctx, "scalar", [("t", mm1)],
                                      lambda s, Mf=Mf, bank=bank: s.activation(
                                          out=g_sb[Mf][:], in_=bank[:], func=AF.Silu), "s")
                        p_evac[Mf % 4] = ev1[Mf]
                    ev3 = [0] * DC
                    for Mf in range(DC):
                        bank = ps_p[Mf % 4]
                        def f3(t, Mf=Mf, bank=bank, sl=sl):
                            for K in range(DC):
                                t.matmul(bank[:], w2d[K][:, Mf * 128:(Mf + 1) * 128],
                                         x3_bf[K][:, sl], start=(K == 0), stop=(K == DC - 1))
                        mm3 = _mk(ctx, "tensor", [("s", p_evac[Mf % 4])], f3, "t")
                        ev3[Mf] = _mk(ctx, "scalar", [("t", mm3)],
                                      lambda s, Mf=Mf, bank=bank: s.activation(
                                          out=u_sb[Mf][:], in_=bank[:], func=AF.Copy), "s")
                        p_evac[Mf % 4] = ev3[Mf]
                    gu_done = 0
                    for Mf in range(DC):
                        if e < 8:
                            def fgu(v, Mf=Mf, sl=sl):
                                v.tensor_tensor(out=g_sb[Mf][:], in0=g_sb[Mf][:], in1=u_sb[Mf][:],
                                                op=ALU.mult)
                                v.tensor_tensor(out=g_sb[Mf][:], in0=g_sb[Mf][:],
                                                in1=gate_bc[:, sl], op=ALU.mult)
                            gu_done = _mk(ctx, "vector", [("s", ev3[Mf]), ("v", gate_ready)], fgu, "v")
                        else:
                            gu_done = _mk(ctx, "vector", [("s", ev3[Mf])],
                                          lambda v, Mf=Mf: v.tensor_tensor(
                                              out=g_sb[Mf][:], in0=g_sb[Mf][:], in1=u_sb[Mf][:],
                                              op=ALU.mult), "v")
                    for Md in range(DC):
                        def fw2(t, Md=Md):
                            for Kf in range(DC):
                                t.matmul(ps_w2[:], w3d[Kf][:, Md * 128:(Md + 1) * 128],
                                         g_sb[Kf][:], start=(Kf == 0), stop=(Kf == DC - 1))
                        mw2 = _mk(ctx, "tensor", [("v", gu_done), ("v", evac_mark)], fw2, "t")
                        evac_mark = _mk(ctx, "vector", [("t", mw2)],
                                        lambda v, Md=Md, sl=sl: v.tensor_tensor(
                                            out=racc[Md][:, sl], in0=racc[Md][:, sl],
                                            in1=ps_w2[:], op=ALU.add), "v")
                if ee < 17:
                    e2, h2 = (ee + 1) // 2, (ee + 1) % 2
                    m_wload = load_half(e2, h2)
            ffn_done = evac_mark

            for n in range(NT):
                sl = slice(n * 512, (n + 1) * 512)
                hc = 0
                for M in range(DC):
                    hc = _mk(ctx, "vector", [("v", ffn_done)],
                             lambda v, M=M, sl=sl: v.tensor_copy(h_bf[M][:], racc[M][:, sl]), "v")
                o_view = [racc[K][:, sl] for K in range(DC)]
                m_ln = emit_ln(ctx, h_bf, o_view, 512, ("v", hc), ps_p[0], ps_p[1], ps_ln,
                               hsqA, hsqB, mean_row, var_row, tmp_row, eps1,
                               mean_bc, rstd_bc, onesb, ones1f)
                for M in range(DC):
                    ctx.wait("gpsimd", "v", m_ln)
                    ctx.dma("gpsimd", outp[b, M * 128:(M + 1) * 128, sl], racc[M][:, sl], sem="wr")




def _constants():
    identbf = np.eye(128, dtype=NPBF)
    identf = np.eye(128, dtype=np.float32)
    repsel = np.zeros((64, 32, 128), dtype=NPBF)
    for j in range(32):
        for p in range(128):
            repsel[2 * j + p // 64, j, p] = 1
    deltam = np.zeros((128, 32, 64), dtype=NPBF)
    for p in range(128):
        for j in range(32):
            deltam[p, j, 2 * j + p // 64] = 1
    mkln = np.zeros((128, 8, 16), dtype=NPBF)
    for jj in range(8):
        for p in range(128):
            mkln[p, jj, 2 * jj + p // 64] = 1
    hsel = np.zeros((16, 8, 128), dtype=NPBF)
    for jj in range(8):
        for p in range(128):
            hsel[2 * jj + p // 64, jj, p] = 1
    repsel2 = np.concatenate([repsel.reshape(64, 32 * 128)] * 2, axis=0)
    return (identbf, identf,
            np.ascontiguousarray(repsel2),
            np.ascontiguousarray(deltam.reshape(128, 32 * 64)),
            np.ascontiguousarray(mkln.reshape(128, 8 * 16)),
            np.ascontiguousarray(hsel.reshape(16, 8 * 128)))


_NC_CACHE = {}


def _get_nc(name, builder):
    if name not in _NC_CACHE:
        _NC_CACHE[name] = builder()
    return _NC_CACHE[name]



def build_full():
    nc = bass.Bass("TRN2", target_bir_lowering=False)
    D_ = {}
    def inp(name, shape, dt=F32):
        D_[name] = nc.dram_tensor(name, shape, dt, kind="ExternalInput").ap()
    def internal(name, shape, dt):
        D_[name] = nc.dram_tensor(name, shape, dt, kind="Internal").ap()
    inp("xf", [B_PER, D, S]); inp("xt", [B_PER, S, D])
    inp("wqT", [D, D]); inp("wkT", [D, D])
    inp("bqv", [128, DC]); inp("bkv", [128, DC])
    inp("identbf", [128, 128], BF16)
    internal("x1", [B_PER, D, S], BF16)
    inp("wg", [D, 16])
    inp("w1eT", [16, D, 512]); inp("w3eT", [16, D, 512]); inp("w2eT", [16, 512, D])
    inp("w1sT", [D, D]); inp("w3sT", [D, D]); inp("w2sT", [D, D])
    inp("identf", [128, 128])
    internal("x2", [B_PER, D, S], BF16)
    inp("wqkvT", [D, 3 * D]); inp("woT", [D, D]); inp("wpl", [64, 1024])
    for nm in ["pVgs", "pVT0", "pBtau", "pBgm", "pCch", "pGam", "pAlp", "pIC"]:
        inp(nm, [64, 1024])
    inp("repsel", [128, 32 * 128], BF16); inp("deltam", [128, 32 * 64], BF16)
    inp("mklnT", [128, 8 * 16], BF16); inp("hselT", [16, 8 * 128], BF16)
    internal("x3", [B_PER, D, S], BF16)
    inp("rw", [D, 8]); inp("rb", [8, 1])
    inp("sw1", [D, 2048]); inp("sw2", [D, 2048]); inp("sw3", [2048, D])
    inp("ew1", [8, D, 2048]); inp("ew2", [8, D, 2048]); inp("ew3", [8, 2048, D])
    D_["outp"] = nc.dram_tensor("outp", [B_PER, D, S], F32, kind="ExternalOutput").ap()

    class _NCProxy:
        def __init__(self, base, px):
            self._b = base
            self._px = px
        def sbuf_tensor(self, name, *a, **k):
            return self._b.sbuf_tensor(self._px + name, *a, **k)
        def psum_tensor(self, name, *a, **k):
            return self._b.psum_tensor(self._px + name, *a, **k)
        def __getattr__(self, n):
            return getattr(self._b, n)

    with ExitStack() as es0:
        ctx = Ctx(nc, es0)
        block = es0.enter_context(nc.Block())

        def barrier():
            marks = [(s, ctx.ctr[s].v) for s in
                     ("t", "v", "s", "g", "wr", "ld", "ld0", "ld1", "ldg")]
            for e in Ctx.ENGS:
                for (sn, v) in marks:
                    if v:
                        ctx.wait(e, sn, v)

        emit_stage_a(_NCProxy(nc, 'a_'), ctx, D_)
        barrier()
        ctx.new_dma_sems(es0, "b")
        emit_stage_b(_NCProxy(nc, 'b_'), ctx, D_)
        barrier()
        ctx.new_dma_sems(es0, "c")
        emit_stage_c(_NCProxy(nc, 'c_'), ctx, D_)
        barrier()
        ctx.new_dma_sems(es0, "d")
        emit_stage_d(_NCProxy(nc, 'd_'), ctx, D_)
        ctx.wait("gpsimd", "wr", ctx.ctr["wr"].v)
        ctx.finalize(block)
    return nc


def kernel(**inp):
    identbf, identf, repsel, deltam, mkln, hsel = _constants()
    x = np.asarray(inp["L0_attn_x"], dtype=np.float32)
    wqT = np.ascontiguousarray(np.asarray(inp["L0_attn_Wq"]).T)
    wkT = np.ascontiguousarray(np.asarray(inp["L0_attn_Wk"]).T)
    bq = np.ascontiguousarray(np.asarray(inp["L0_attn_bq"]).reshape(DC, 128).T)
    bk = np.ascontiguousarray(np.asarray(inp["L0_attn_bk"]).reshape(DC, 128).T)
    w1eT = np.ascontiguousarray(np.transpose(np.asarray(inp["L0_ffn_w1_e"]), (0, 2, 1)))
    w3eT = np.ascontiguousarray(np.transpose(np.asarray(inp["L0_ffn_w3_e"]), (0, 2, 1)))
    w2eT = np.ascontiguousarray(np.transpose(np.asarray(inp["L0_ffn_w2_e"]), (0, 2, 1)))
    w1sT = np.ascontiguousarray(np.asarray(inp["L0_ffn_w1_s"]).T)
    w3sT = np.ascontiguousarray(np.asarray(inp["L0_ffn_w3_s"]).T)
    w2sT = np.ascontiguousarray(np.asarray(inp["L0_ffn_w2_s"]).T)
    wg = np.ascontiguousarray(np.asarray(inp["L0_ffn_w_gate"]).T)
    wqkvT = np.ascontiguousarray(np.asarray(inp["L1_attn_W_qkv"]).T)
    woT = np.ascontiguousarray(np.asarray(inp["L1_attn_W_o"]).T)
    wltm = np.asarray(inp["L1_attn_W_LTM"])
    wpl = np.ascontiguousarray(np.transpose(wltm, (2, 0, 1)).reshape(64, 1024))
    def plane16(v):
        return np.ascontiguousarray(
            np.broadcast_to(np.asarray(v, dtype=np.float32)[None, :, None],
                            (64, 16, 64)).reshape(64, 1024))
    def plane1(v):
        return np.full((64, 1024), np.float32(np.asarray(v).reshape(-1)[0]), dtype=np.float32)
    rb = np.ascontiguousarray(np.asarray(inp["L1_ffn_rb"]).reshape(8, 1))

    nc = _get_nc("full", build_full)
    in_maps = []
    for c in range(NCORES):
        xs = x[2 * c:2 * c + 2]
        in_maps.append(dict(
            xf=np.ascontiguousarray(np.transpose(xs, (0, 2, 1))),
            xt=np.ascontiguousarray(xs),
            wqT=wqT, wkT=wkT, bqv=bq, bkv=bk, identbf=identbf,
            wg=wg, w1eT=w1eT, w3eT=w3eT, w2eT=w2eT,
            w1sT=w1sT, w3sT=w3sT, w2sT=w2sT, identf=identf,
            wqkvT=wqkvT, woT=woT, wpl=wpl,
            pVgs=plane16(inp["L1_attn_V_gs"]), pVT0=plane16(inp["L1_attn_V_T0"]),
            pBtau=plane16(inp["L1_attn_beta_tau"]), pBgm=plane16(inp["L1_attn_beta_gm"]),
            pCch=plane16(inp["L1_attn_C_ch"]), pGam=plane16(inp["L1_attn_gamma"]),
            pAlp=plane1(inp["L1_attn_alpha_ppd"]), pIC=plane1(inp["L1_attn_IC_threshold"]),
            repsel=repsel, deltam=deltam, mklnT=mkln, hselT=hsel,
            rw=np.asarray(inp["L1_ffn_rw"]), rb=rb,
            sw1=np.asarray(inp["L1_ffn_sw1"]), sw2=np.asarray(inp["L1_ffn_sw2"]),
            sw3=np.asarray(inp["L1_ffn_sw3"]), ew1=np.asarray(inp["L1_ffn_ew1"]),
            ew2=np.asarray(inp["L1_ffn_ew2"]), ew3=np.asarray(inp["L1_ffn_ew3"])))
    res = run_bass_kernel_spmd(nc, in_maps, core_ids=list(range(NCORES)))
    outs = [np.transpose(np.asarray(r["outp"], dtype=np.float32), (0, 2, 1))
            for r in res.results]
    return np.ascontiguousarray(np.concatenate(outs, axis=0))


# version 04
# speedup vs baseline: 1.0039x; 1.0039x over previous
import numpy as np
import ml_dtypes
from contextlib import ExitStack

import concourse.bass as bass
import concourse.mybir as mybir
from concourse.bass_utils import run_bass_kernel_spmd

F32 = mybir.dt.float32
BF16 = mybir.dt.bfloat16
AF = mybir.ActivationFunctionType
ALU = mybir.AluOpType
AX = mybir.AxisListType
NPBF = ml_dtypes.bfloat16

NCORES = 8
B_PER = 2
S = 2048
D = 1024
DC = D // 128
NT = S // 512
LN_EPS = 1e-5
BIG = 1e30


class Ctr:
    def __init__(self):
        self.v = 0
    def inc(self, n=1):
        self.v += n
        return self.v


class Ctx:
    ENGS = ["sync", "tensor", "vector", "scalar", "gpsimd"]

    def __init__(self, nc, es):
        self.nc = nc
        self.prog = {e: [] for e in self.ENGS}
        self.sem = {}
        self.ctr = {}
        for name in ["ld", "ld0", "ld1", "ldg", "t", "v", "s", "g", "wr"]:
            self.sem[name] = es.enter_context(nc.semaphore("sem_" + name))
            self.ctr[name] = Ctr()
        self.chain = {}
        for e in ["tensor", "vector", "scalar", "gpsimd"]:
            self.chain[e] = [es.enter_context(nc.semaphore("chain_" + e)), 0]

    def new_dma_sems(self, es, suffix):
        for name in ["ld", "ld0", "ld1", "ldg", "wr"]:
            self.sem[name] = es.enter_context(self.nc.semaphore(f"sem_{name}_{suffix}"))
            self.ctr[name] = Ctr()

    def em(self, eng, fn):
        self.prog[eng].append(fn)

    def dma(self, q, out, in_, sem="ld"):
        try:
            nm = out.tensor.name
            if nm.endswith("stg0") or nm.endswith("stg1"):
                sem = "ld" + nm[-1]
        except Exception:
            pass
        s = self.sem[sem]
        def f(eng):
            eng.dma_start(out=out, in_=in_).then_inc(s, 16)
        self.em(q, f)
        return (sem, self.ctr[sem].inc(16))

    def wait(self, eng, sem, val):
        s = self.sem[sem]
        self.em(eng, lambda e: e.wait_ge(s, val))

    def finalize(self, block):
        for eng in self.ENGS:
            fns = self.prog[eng]
            dec = getattr(block, eng)
            def make(fns):
                def body(engine):
                    for f in fns:
                        f(engine)
                return body
            dec(make(fns))


class _Rec:
    def __init__(self, eng, chain):
        object.__setattr__(self, "_e", eng)
        object.__setattr__(self, "last", None)
        object.__setattr__(self, "chain", chain)

    def __getattr__(self, n):
        a = getattr(self._e, n)
        if callable(a):
            def w(*args, **kw):
                if self.last is not None and self.chain is not None:
                    csem, _ = self.chain
                    self.last.then_inc(csem, 1)
                    self.chain[1] += 1
                    self._e.wait_ge(csem, self.chain[1])
                    object.__setattr__(self, "last", None)
                r = a(*args, **kw)
                if hasattr(r, "then_inc"):
                    object.__setattr__(self, "last", r)
                return r
            return w
        return a


def _mk(ctx, eng, waits, fn, inc_sem=None):
    sems = ctx.sem
    chain = ctx.chain.get(eng)
    waits = [(sn, ctx.ctr["ld"].v) if sn == "ld" else (sn, v) for (sn, v) in waits]
    waits = [(ctx.sem[sn], v) for (sn, v) in waits]
    inc_h = ctx.sem[inc_sem] if inc_sem is not None else None
    def f(engine):
        for (sh, val) in waits:
            engine.wait_ge(sh, val)
        rec = _Rec(engine, chain)
        fn(rec)
        if inc_h is not None:
            if rec.last is not None:
                rec.last.then_inc(inc_h, 1)
            else:
                engine.sem_inc(inc_h, 1)
    ctx.em(eng, f)
    if inc_sem is not None:
        return ctx.ctr[inc_sem].inc()
    return None



def emit_ln(ctx, h_tiles, out_tiles, ns, ready, ps_a, ps_b, ps_c,
            hsqA, hsqB, mean_row, var_row, tmp_row, eps_col,
            mean_bc, rstd_bc, onesb, ones1f, col0=0):
    nslab = ns // 512
    out_marks = []
    for n in range(nslab):
        c0 = col0 + n * 512
        sl = slice(c0, c0 + 512)
        def fsum(t, sl=sl):
            for K in range(DC):
                t.matmul(ps_a[0:1, :], onesb[:], h_tiles[K][:, sl],
                         start=(K == 0), stop=(K == DC - 1))
        m_sum = _mk(ctx, "tensor", [ready], fsum, "t")
        sq_marks = []
        for K in range(DC):
            buf = hsqA if K % 2 == 0 else hsqB
            w = [ready]
            if K >= 2:
                w.append(("t", sq_marks[K - 2]))
            m_sq = _mk(ctx, "scalar", w,
                       lambda s, K=K, buf=buf, sl=sl: s.square(out=buf[:], in_=h_tiles[K][:, sl]),
                       "s")
            def fmm(t, K=K, buf=buf):
                t.matmul(ps_b[0:1, :], onesb[:], buf[:],
                         start=(K == 0), stop=(K == DC - 1))
            m_mm = _mk(ctx, "tensor", [("s", m_sq)], fmm, "t")
            sq_marks.append(m_mm)
        m_stats = _mk(ctx, "vector", [("t", sq_marks[-1]), ("t", m_sum)],
                      lambda v, sl=sl: (
                          v.tensor_single_scalar(out=mean_row[:], in_=ps_a[0:1, :], scalar=1.0 / D, op=ALU.mult),
                          v.tensor_single_scalar(out=var_row[:], in_=ps_b[0:1, :], scalar=1.0 / D, op=ALU.mult),
                          v.tensor_tensor(out=tmp_row[:], in0=mean_row[:], in1=mean_row[:], op=ALU.mult),
                          v.tensor_tensor(out=var_row[:], in0=var_row[:], in1=tmp_row[:], op=ALU.subtract),
                      ), "v")
        m_rstd = _mk(ctx, "scalar", [("v", m_stats)],
                     lambda s: (
                         s.activation(out=var_row[:], in_=var_row[:], func=AF.Ln, bias=eps_col[:], scale=1.0),
                         s.activation(out=var_row[:], in_=var_row[:], func=AF.Exp, scale=-0.5),
                     ), "s")
        m_bc1 = _mk(ctx, "tensor", [("s", m_rstd)],
                    lambda t: t.matmul(ps_c[:, :], ones1f[:], mean_row[:], start=True, stop=True),
                    "t")
        m_bc1e = _mk(ctx, "vector", [("t", m_bc1)],
                     lambda v, n=n: v.tensor_copy(mean_bc[n][:], ps_c[:, :]), "v")
        m_bc2 = _mk(ctx, "tensor", [("v", m_bc1e)],
                    lambda t: t.matmul(ps_c[:, :], ones1f[:], var_row[:], start=True, stop=True),
                    "t")
        m_bc2e = _mk(ctx, "vector", [("t", m_bc2)],
                     lambda v, n=n: v.tensor_copy(rstd_bc[n][:], ps_c[:, :]), "v")
        for K in range(DC):
            m_ap = _mk(ctx, "vector", [("v", m_bc2e)],
                       lambda v, K=K, n=n, sl=sl: (
                           v.tensor_tensor(out=out_tiles[K][:, sl], in0=h_tiles[K][:, sl],
                                           in1=mean_bc[n][:], op=ALU.subtract),
                           v.tensor_tensor(out=out_tiles[K][:, sl], in0=out_tiles[K][:, sl],
                                           in1=rstd_bc[n][:], op=ALU.mult),
                       ), "v")
            out_marks.append(m_ap)
    return out_marks[-1]



def emit_stage_a(nc, ctx, DD):
    xf = DD["xf"]
    xt = DD["xt"]
    wqT = DD["wqT"]
    wkT = DD["wkT"]
    bqv = DD["bqv"]
    bkv = DD["bkv"]
    identbf = DD["identbf"]
    x1 = DD["x1"]
    SCALE = 1.0 / np.sqrt(D)

    with ExitStack() as es:
        E = es.enter_context
        pool = [E(nc.sbuf_tensor(f"pool{i}", [128, S], BF16)) for i in range(8)]
        wq_bf = [pool[i][:, 0:1024] for i in range(DC)]
        wk_bf = [pool[i][:, 1024:2048] for i in range(DC)]
        stg = [E(nc.sbuf_tensor(f"stg{i}", [128, S], F32)) for i in range(2)]
        bq_sb = E(nc.sbuf_tensor("bq_sb", [128, DC], F32))
        bk_sb = E(nc.sbuf_tensor("bk_sb", [128, DC], F32))
        ident = E(nc.sbuf_tensor("ident", [128, 128], BF16))
        onesb = E(nc.sbuf_tensor("onesb", [128, 1], BF16))
        ones1f = E(nc.sbuf_tensor("ones1f", [1, 128], F32))
        two_col = E(nc.sbuf_tensor("two_col", [128, 1], F32))
        eps_col = E(nc.sbuf_tensor("eps_col", [1, 1], F32))
        xf_bf = [E(nc.sbuf_tensor(f"xfb{i}", [128, S], BF16)) for i in range(DC)]
        xt_bf = [E(nc.sbuf_tensor(f"xtb{i}", [128, D], BF16)) for i in range(16)]
        q_bf = [E(nc.sbuf_tensor(f"qb{i}", [128, S], BF16)) for i in range(DC)]
        k_bf = [E(nc.sbuf_tensor(f"kb{i}", [128, S], BF16)) for i in range(DC)]
        p_grp = [pool[g][:, :] for g in range(4)]
        pt_grp = [pool[4 + tc // 4][:, (tc % 4) * 512:(tc % 4 + 1) * 512] for tc in range(16)]
        mrow = E(nc.sbuf_tensor("mrow", [128, 4], F32))
        mcol = E(nc.sbuf_tensor("mcol", [128, 1], F32))
        mbias = E(nc.sbuf_tensor("mbias", [128, 1], F32))
        zrow = E(nc.sbuf_tensor("zrow", [128, 4], F32))
        zcol = E(nc.sbuf_tensor("zcol", [128, 1], F32))
        corr = E(nc.sbuf_tensor("corr", [128, DC], F32))
        h_bf = xf_bf
        hsqA = E(nc.sbuf_tensor("hsqA", [128, 512], BF16))
        hsqB = E(nc.sbuf_tensor("hsqB", [128, 512], BF16))
        xf32 = E(nc.sbuf_tensor("xf32", [128, 512], F32))
        attn_sb = E(nc.sbuf_tensor("attn_sb", [128, 512], F32))
        mean_row = E(nc.sbuf_tensor("mean_row", [1, 512], F32))
        var_row = E(nc.sbuf_tensor("var_row", [1, 512], F32))
        tmp_row = E(nc.sbuf_tensor("tmp_row", [1, 512], F32))
        mean_bc = [E(nc.sbuf_tensor(f"meanbc{i}", [128, 512], BF16)) for i in range(NT)]
        rstd_bc = [E(nc.sbuf_tensor(f"rstdbc{i}", [128, 512], BF16)) for i in range(NT)]
        o_bf = h_bf

        ps_sc = [E(nc.psum_tensor(f"pssc{i}", [128, 512], F32)) for i in range(4)]
        ps_tr = E(nc.psum_tensor("pstr", [128, 128], BF16))
        ps_pv = E(nc.psum_tensor("pspv", [128, 512], F32))
        ps_ln = E(nc.psum_tensor("psln", [128, 512], F32))


        m_c1 = ctx.dma("sync", ident[:], identbf)
        m_c2 = ctx.dma("sync", bq_sb[:], bqv)
        m_c3 = ctx.dma("sync", bk_sb[:], bkv)
        m_const = _mk(ctx, "gpsimd", [],
                      lambda g: (g.memset(onesb[:], 1.0), g.memset(ones1f[:], 1.0),
                                 g.memset(two_col[:], 2.0), g.memset(eps_col[:], LN_EPS)),
                      "g")
        for _e in ("tensor", "vector", "scalar"):
            ctx.wait(_e, "ld", ctx.ctr["ld"].v)
            ctx.wait(_e, "g", ctx.ctr["g"].v)

        last_cast = {0: 0, 1: 0}
        wm = 0
        for wi, (wT, wdst) in enumerate([(wqT, wq_bf), (wkT, wk_bf)]):
            for i in range(DC):
                bi = (wi * DC + i) % 2
                ctx.wait("sync", "v", last_cast[bi])
                m_ld = ctx.dma("sync", stg[bi][:, 0:D], wT[i * 128:(i + 1) * 128, :])
                wm = _mk(ctx, "vector", [m_ld],
                         lambda v, bi=bi, dst=wdst[i]: v.tensor_copy(dst[:, :], stg[bi][:, 0:D]), "v")
                last_cast[bi] = wm
        w_done = wm

        for b in range(B_PER):
            if b > 0:
                ctx.wait("vector", "wr", ctx.ctr["wr"].v)
            xm = 0
            for i in range(DC):
                bi = i % 2
                ctx.wait("sync", "v", last_cast[bi])
                m_ld = ctx.dma("sync", stg[bi][:], xf[b, i * 128:(i + 1) * 128, :])
                xm = _mk(ctx, "vector", [m_ld],
                         lambda v, bi=bi, dst=xf_bf[i]: v.tensor_copy(dst[:], stg[bi][:]), "v")
                last_cast[bi] = xm
            for i in range(16):
                bi = i % 2
                ctx.wait("sync", "v", last_cast[bi])
                m_ld = ctx.dma("sync", stg[bi][:, 0:D], xt[b, i * 128:(i + 1) * 128, :])
                xm = _mk(ctx, "vector", [m_ld],
                         lambda v, bi=bi, dst=xt_bf[i]: v.tensor_copy(dst[:], stg[bi][:, 0:D]), "v")
                last_cast[bi] = xm
            x_done = xm

            evac_marks = []
            for wtiles, btile, dst in [(wq_bf, bq_sb, q_bf), (wk_bf, bk_sb, k_bf)]:
                for M in range(DC):
                    for n in range(NT):
                        bank = ps_sc[n % 4]
                        waits = [("v", max(w_done, x_done))]
                        if evac_marks:
                            idx = len(evac_marks) - 4
                            if idx >= 0:
                                waits.append(("s", evac_marks[idx]))
                        def fmm(t, wtiles=wtiles, M=M, n=n, bank=bank):
                            for K in range(DC):
                                t.matmul(bank[:], wtiles[K][:, M * 128:(M + 1) * 128],
                                         xf_bf[K][:, n * 512:(n + 1) * 512],
                                         start=(K == 0), stop=(K == DC - 1))
                        m_mm = _mk(ctx, "tensor", waits, fmm, "t")
                        m_ev = _mk(ctx, "scalar", [("t", m_mm)],
                                   lambda s, M=M, n=n, dst=dst, btile=btile, bank=bank:
                                   s.activation(out=dst[M][:, n * 512:(n + 1) * 512],
                                                in_=bank[:], func=AF.Identity,
                                                bias=btile[:, M:M + 1], scale=1.0),
                                   "s")
                        evac_marks.append(m_ev)
            qk_done_s = evac_marks[-1]

            corr_mark = 0
            for M in range(DC):
                m_cs = _mk(ctx, "tensor",
                           ([("v", x_done)] if M == 0 else []) +
                           ([("v", corr_mark)] if M > 0 else []),
                           lambda t, M=M: [t.matmul(ps_pv[:, 0:1],
                                                    xt_bf[K][:, M * 128:(M + 1) * 128], onesb[:],
                                                    start=(K == 0), stop=(K == 15)) for K in range(16)],
                           "t")
                corr_mark = _mk(ctx, "vector", [("t", m_cs)],
                                lambda v, M=M: v.tensor_single_scalar(
                                    out=corr[:, M:M + 1], in_=ps_pv[:, 0:1],
                                    scalar=-2.0 / S, op=ALU.mult), "v")

            h_mark = 0
            for grp in range(4):
                exp_marks = []
                norm_marks = []
                for sti in range(4):
                    st = grp * 4 + sti
                    waits = [("s", qk_done_s)]
                    if exp_marks:
                        waits.append(("s", exp_marks[-1]))
                    def fsc(t, st=st):
                        for K in range(DC):
                            for tc in range(4):
                                t.matmul(ps_sc[tc][:],
                                         q_bf[K][:, st * 128:(st + 1) * 128],
                                         k_bf[K][:, tc * 512:(tc + 1) * 512],
                                         start=(K == 0), stop=(K == DC - 1))
                    m_sc = _mk(ctx, "tensor", waits, fsc, "t")
                    m_max = _mk(ctx, "vector", [("t", m_sc)],
                                lambda v: (
                                    [v.tensor_reduce(out=mrow[:, tc:tc + 1], in_=ps_sc[tc][:],
                                                     axis=AX.X, op=ALU.max) for tc in range(4)],
                                    v.tensor_reduce(out=mcol[:], in_=mrow[:], axis=AX.X, op=ALU.max),
                                    v.tensor_single_scalar(out=mbias[:], in_=mcol[:],
                                                           scalar=-SCALE, op=ALU.mult),
                                ), "v")
                    m_exp = _mk(ctx, "scalar", [("v", m_max)],
                                lambda s, sti=sti: [s.activation(
                                    out=p_grp[sti][:, tc * 512:(tc + 1) * 512],
                                    in_=ps_sc[tc][:], func=AF.Exp,
                                    bias=mbias[:], scale=SCALE) for tc in range(4)],
                                "s")
                    exp_marks.append(m_exp)
                    m_norm = _mk(ctx, "vector", [("s", m_exp)],
                                 lambda v, sti=sti: (
                                     [v.tensor_reduce(out=zrow[:, tc:tc + 1],
                                                      in_=p_grp[sti][:, tc * 512:(tc + 1) * 512],
                                                      axis=AX.X, op=ALU.add) for tc in range(4)],
                                     v.tensor_reduce(out=zcol[:], in_=zrow[:], axis=AX.X, op=ALU.add),
                                     v.reciprocal(out=zcol[:], in_=zcol[:]),
                                     v.tensor_scalar_mul(p_grp[sti][:], p_grp[sti][:], zcol[:]),
                                 ), "v")
                    norm_marks.append(m_norm)
                norm_done = norm_marks[-1]

                tr_ev = 0
                for tc in range(16):
                    for sti in range(4):
                        waits = [("v", norm_done)]
                        if tr_ev:
                            waits.append(("v", tr_ev))
                        m_tr = _mk(ctx, "tensor", waits,
                                   lambda t, tc=tc, sti=sti: t.transpose(
                                       ps_tr[:], p_grp[sti][:, tc * 128:(tc + 1) * 128], ident[:]),
                                   "t")
                        tr_ev = _mk(ctx, "vector", [("t", m_tr)],
                                    lambda v, tc=tc, sti=sti: v.tensor_copy(
                                        pt_grp[tc][:, sti * 128:(sti + 1) * 128], ps_tr[:]),
                                    "v")
                trans_done = tr_ev

                for M in range(DC):
                    def fpv(t, M=M):
                        for K in range(16):
                            t.matmul(ps_pv[:], xt_bf[K][:, M * 128:(M + 1) * 128],
                                     pt_grp[K][:], start=(K == 0), stop=(K == 15))
                    waits = [("v", trans_done), ("v", corr_mark)]
                    if h_mark:
                        waits.append(("v", h_mark))
                    m_pv = _mk(ctx, "tensor", waits, fpv, "t")
                    if h_mark:
                        ctx.wait("gpsimd", "v", h_mark)
                    m_x32 = ctx.dma("gpsimd", xf32[:],
                                    xf[b, M * 128:(M + 1) * 128, grp * 512:(grp + 1) * 512],
                                    sem="ldg")
                    m_ep1 = _mk(ctx, "scalar", [("t", m_pv)],
                                lambda s, M=M: s.activation(out=attn_sb[:], in_=ps_pv[:],
                                                            func=AF.Identity, bias=corr[:, M:M + 1],
                                                            scale=1.0), "s")
                    h_mark = _mk(ctx, "vector", [("s", m_ep1), m_x32],
                                 lambda v, M=M, grp=grp: v.scalar_tensor_tensor(
                                     out=h_bf[M][:, grp * 512:(grp + 1) * 512],
                                     in0=xf32[:], scalar=two_col[:], in1=attn_sb[:],
                                     op0=ALU.mult, op1=ALU.add), "v")
            h_done = h_mark

            m_ln = emit_ln(ctx, h_bf, o_bf, S, ("v", h_done), ps_sc[0], ps_sc[1], ps_ln,
                           hsqA, hsqB, mean_row, var_row, tmp_row, eps_col,
                           mean_bc, rstd_bc, onesb, ones1f)
            for M in range(DC):
                ctx.wait("gpsimd", "v", m_ln)
                ctx.dma("gpsimd", x1[b, M * 128:(M + 1) * 128, :], o_bf[M][:], sem="wr")




def emit_stage_b(nc, ctx, DD):
    x1 = DD["x1"]
    wg = DD["wg"]
    w1eT = DD["w1eT"]
    w3eT = DD["w3eT"]
    w2eT = DD["w2eT"]
    w1sT = DD["w1sT"]
    w3sT = DD["w3sT"]
    w2sT = DD["w2sT"]
    identf = DD["identf"]
    x2 = DD["x2"]

    NE = 16

    with ExitStack() as es:
        E = es.enter_context
        x1_bf = [E(nc.sbuf_tensor(f"x1b{i}", [128, S], BF16)) for i in range(DC)]
        wg_bf = [E(nc.sbuf_tensor(f"wgb{i}", [128, 16], BF16)) for i in range(DC)]
        identfs = E(nc.sbuf_tensor("identfs", [128, 128], F32))
        stg = [E(nc.sbuf_tensor(f"stg{i}", [128, 1024], F32)) for i in range(2)]
        w1_sb = [[E(nc.sbuf_tensor(f"w1_{pp}_{i}", [128, 512], BF16)) for i in range(DC)] for pp in range(2)]
        w3_sb = [[E(nc.sbuf_tensor(f"w3_{pp}_{i}", [128, 512], BF16)) for i in range(DC)] for pp in range(2)]
        w2_sb = [[E(nc.sbuf_tensor(f"w2_{pp}_{i}", [128, 1024], BF16)) for i in range(4)] for pp in range(2)]
        gsc_fm = E(nc.sbuf_tensor("gsc_fm", [16, S], F32))
        gtm = E(nc.sbuf_tensor("gtm", [128, 16], F32))
        gates_fm = E(nc.sbuf_tensor("gates_fm", [16, S], BF16))
        grow = E(nc.sbuf_tensor("grow", [1, S], BF16))
        gate_bc = E(nc.sbuf_tensor("gate_bc", [128, S], BF16))
        t4a = E(nc.sbuf_tensor("t4a", [128, 4], F32))
        t4b = E(nc.sbuf_tensor("t4b", [128, 4], F32))
        t16a = E(nc.sbuf_tensor("t16a", [128, 16], F32))
        t16b = E(nc.sbuf_tensor("t16b", [128, 16], F32))
        c1 = E(nc.sbuf_tensor("c1", [128, 1], F32))
        c2 = E(nc.sbuf_tensor("c2", [128, 1], F32))
        nbig = E(nc.sbuf_tensor("nbig", [128, 1], F32))
        pbig = E(nc.sbuf_tensor("pbig", [128, 1], F32))
        g_sb = [E(nc.sbuf_tensor(f"g_sb{i}", [128, 512], BF16)) for i in range(4)]
        g_sb2 = [E(nc.sbuf_tensor(f"g_sb2{i}", [128, 512], BF16)) for i in range(4)]
        u_sb = [E(nc.sbuf_tensor(f"u_sb{i}", [128, 512], BF16)) for i in range(4)]
        racc = [E(nc.sbuf_tensor(f"racc{i}", [128, S], F32)) for i in range(DC)]
        h_bf = x1_bf
        hsqA = E(nc.sbuf_tensor("hsqA", [128, 512], BF16))
        hsqB = E(nc.sbuf_tensor("hsqB", [128, 512], BF16))
        mean_row = E(nc.sbuf_tensor("mean_row", [1, 512], F32))
        var_row = E(nc.sbuf_tensor("var_row", [1, 512], F32))
        tmp_row = E(nc.sbuf_tensor("tmp_row", [1, 512], F32))
        eps_col = E(nc.sbuf_tensor("eps_col", [1, 1], F32))
        mean_bc = [E(nc.sbuf_tensor("meanbcB", [128, 512], BF16))] * NT
        rstd_bc = [E(nc.sbuf_tensor("rstdbcB", [128, 512], BF16))] * NT
        onesb = E(nc.sbuf_tensor("onesb", [128, 1], BF16))
        ones1f = E(nc.sbuf_tensor("ones1f", [1, 128], F32))
        ones1b = E(nc.sbuf_tensor("ones1b", [1, 128], BF16))
        o_bf = h_bf

        ps_p = [E(nc.psum_tensor(f"psp{i}", [128, 512], F32)) for i in range(4)]
        ps_w2 = E(nc.psum_tensor("psw2", [128, 512], F32))
        ps_g = E(nc.psum_tensor("psg", [128, 512], F32))
        ps_ln = E(nc.psum_tensor("psln", [128, 512], F32))


        m_id = ctx.dma("sync", identfs[:], identf)
        _mk(ctx, "gpsimd", [],
            lambda g: (g.memset(onesb[:], 1.0), g.memset(ones1f[:], 1.0),
                       g.memset(ones1b[:], 1.0),
                       g.memset(eps_col[:], LN_EPS), g.memset(nbig[:], -BIG),
                       g.memset(pbig[:], BIG)), "g")
        m_gconst = ctx.ctr["g"].v
        for _e in ("tensor", "vector", "scalar"):
            ctx.wait(_e, "ld", ctx.ctr["ld"].v)
            ctx.wait(_e, "g", ctx.ctr["g"].v)

        last_cast = {0: 0, 1: 0}
        wgm = 0
        for i in range(DC):
            bi = i % 2
            ctx.wait("sync", "v", last_cast[bi])
            m_ld = ctx.dma("sync", stg[bi][:, 0:16], wg[i * 128:(i + 1) * 128, :])
            wgm = _mk(ctx, "vector", [m_ld],
                      lambda v, bi=bi, i=i: v.tensor_copy(wg_bf[i][:], stg[bi][:, 0:16]), "v")
            last_cast[bi] = wgm

        def load_expert(e, pp):
            nonlocal last_cast
            mm = 0
            srcs = []
            if e < 16:
                for i in range(DC):
                    srcs.append((w1eT[e, i * 128:(i + 1) * 128, :], w1_sb[pp][i], 512))
                for i in range(DC):
                    srcs.append((w3eT[e, i * 128:(i + 1) * 128, :], w3_sb[pp][i], 512))
                for i in range(4):
                    srcs.append((w2eT[e, i * 128:(i + 1) * 128, :], w2_sb[pp][i], 1024))
            else:
                half = e - 16
                c0 = half * 512
                for i in range(DC):
                    srcs.append((w1sT[i * 128:(i + 1) * 128, c0:c0 + 512], w1_sb[pp][i], 512))
                for i in range(DC):
                    srcs.append((w3sT[i * 128:(i + 1) * 128, c0:c0 + 512], w3_sb[pp][i], 512))
                for i in range(4):
                    srcs.append((w2sT[c0 + i * 128:c0 + (i + 1) * 128, :], w2_sb[pp][i], 1024))
            for j, (src, dst, w) in enumerate(srcs):
                bi = j % 2
                ctx.wait("sync", "v", last_cast[bi])
                m_ld = ctx.dma("sync", stg[bi][:, 0:w], src)
                mm = _mk(ctx, "vector", [m_ld],
                         lambda v, bi=bi, dst=dst, w=w: v.tensor_copy(dst[:], stg[bi][:, 0:w]), "v")
                last_cast[bi] = mm
            return mm

        for b in range(B_PER):
            if b > 0:
                ctx.wait("sync", "wr", ctx.ctr["wr"].v)
            x_marks = []
            for i in range(DC):
                x_marks.append(ctx.dma("sync", x1_bf[i][:], x1[b, i * 128:(i + 1) * 128, :]))
            x_done_ld = x_marks[-1]

            glast = 0
            for n in range(NT):
                def fg(t, n=n):
                    for K in range(DC):
                        t.matmul(ps_g[0:16, :], wg_bf[K][:], x1_bf[K][:, n * 512:(n + 1) * 512],
                                 start=(K == 0), stop=(K == DC - 1))
                waits = [x_done_ld, ("v", wgm)]
                if glast:
                    waits.append(("s", glast))
                m_g = _mk(ctx, "tensor", waits, fg, "t")
                glast = _mk(ctx, "scalar", [("t", m_g)],
                            lambda s, n=n: s.activation(out=gsc_fm[:, n * 512:(n + 1) * 512],
                                                        in_=ps_g[0:16, :], func=AF.Sigmoid),
                            "s")
            gdone = 0
            for stt in range(16):
                waits = [("s", glast), m_id, ("g", m_gconst)]
                if gdone:
                    waits.append(("v", gdone))
                m_tr = _mk(ctx, "tensor", waits,
                           lambda t, stt=stt: t.transpose(
                               ps_g[:, 0:16], gsc_fm[:, stt * 128:(stt + 1) * 128],
                               identfs[0:16, 0:16]),
                           "t")
                m_tm = _mk(ctx, "vector", [("t", m_tr)],
                           lambda v: v.tensor_copy(gtm[:], ps_g[:, 0:16]), "v")

                def ftopk(v):
                    sv = gtm[:, :].rearrange("p (g e) -> p g e", g=4)
                    v.tensor_reduce(out=t4a[:], in_=sv, axis=AX.X, op=ALU.max)
                    m1bc = t4a[:, :].unsqueeze(2).broadcast_to([128, 4, 4])
                    v.tensor_tensor(out=t16a[:, :].rearrange("p (g e) -> p g e", g=4),
                                    in0=sv, in1=m1bc, op=ALU.is_ge)
                    v.scalar_tensor_tensor(out=t16b[:], in0=t16a[:], scalar=nbig[:],
                                           in1=gtm[:], op0=ALU.mult, op1=ALU.add)
                    v.tensor_reduce(out=t4b[:], in_=t16b[:, :].rearrange("p (g e) -> p g e", g=4),
                                    axis=AX.X, op=ALU.max)
                    v.tensor_tensor(out=t4a[:], in0=t4a[:], in1=t4b[:], op=ALU.add)
                    v.tensor_reduce(out=c1[:], in_=t4a[:], axis=AX.X, op=ALU.max)
                    v.tensor_scalar(out=t4b[:], in0=t4a[:], scalar1=c1[:], scalar2=None,
                                    op0=ALU.is_ge)
                    v.scalar_tensor_tensor(out=t4b[:], in0=t4b[:], scalar=nbig[:],
                                           in1=t4a[:], op0=ALU.mult, op1=ALU.add)
                    v.tensor_reduce(out=c2[:], in_=t4b[:], axis=AX.X, op=ALU.max)
                    v.tensor_scalar(out=t4a[:], in0=t4a[:], scalar1=c2[:], scalar2=None,
                                    op0=ALU.is_ge)
                    v.tensor_single_scalar(out=t4a[:], in_=t4a[:], scalar=-1.0, op=ALU.add)
                    a_bc = t4a[:, :].unsqueeze(2).broadcast_to([128, 4, 4])
                    v.scalar_tensor_tensor(out=t16a[:, :].rearrange("p (g e) -> p g e", g=4),
                                           in0=a_bc, scalar=pbig[:],
                                           in1=sv, op0=ALU.mult, op1=ALU.add)
                    for it in range(3):
                        v.tensor_reduce(out=c1[:], in_=t16a[:], axis=AX.X, op=ALU.max)
                        v.tensor_scalar(out=t16b[:], in0=t16a[:], scalar1=c1[:], scalar2=None,
                                        op0=ALU.is_ge)
                        v.scalar_tensor_tensor(out=t16a[:], in0=t16b[:], scalar=nbig[:],
                                               in1=t16a[:], op0=ALU.mult, op1=ALU.add)
                    v.tensor_reduce(out=c1[:], in_=t16a[:], axis=AX.X, op=ALU.max)
                    a_bc2 = t4a[:, :].unsqueeze(2).broadcast_to([128, 4, 4])
                    v.scalar_tensor_tensor(out=t16b[:, :].rearrange("p (g e) -> p g e", g=4),
                                           in0=a_bc2, scalar=pbig[:],
                                           in1=sv, op0=ALU.mult, op1=ALU.add)
                    v.tensor_scalar(out=t16b[:], in0=t16b[:], scalar1=c1[:], scalar2=None,
                                    op0=ALU.is_ge)
                    v.tensor_tensor(out=t16b[:], in0=gtm[:], in1=t16b[:], op=ALU.mult)
                    v.tensor_reduce(out=c2[:], in_=t16b[:], axis=AX.X, op=ALU.add)
                    v.tensor_single_scalar(out=c2[:], in_=c2[:], scalar=1e-6, op=ALU.add)
                    v.reciprocal(out=c2[:], in_=c2[:])
                    v.tensor_scalar_mul(t16b[:], t16b[:], c2[:])
                m_tk = _mk(ctx, "vector", [("v", m_tm)], ftopk, "v")
                m_tb = _mk(ctx, "tensor", [("v", m_tk)],
                           lambda t: t.transpose(ps_g[0:16, 0:128], t16b[:], identfs[:]), "t")
                gdone = _mk(ctx, "vector", [("t", m_tb)],
                            lambda v, stt=stt: v.tensor_copy(
                                gates_fm[:, stt * 128:(stt + 1) * 128], ps_g[0:16, 0:128]), "v")

            racc_mark = 0
            for M in range(DC):
                racc_mark = _mk(ctx, "vector", [x_done_ld],
                                lambda v, M=M: v.tensor_copy(racc[M][:], x1_bf[M][:]), "v")

            m_wload = load_expert(0, 0)
            last_gb_t = 0
            evac_w2_mark = racc_mark
            p_evac = {0: 0, 1: 0, 2: 0, 3: 0}
            pendb = None
            prev_gub = 0
            prev_w2b_for = {0: 0, 1: 0}
            for e in range(18):
                pp = e % 2
                w_ready = m_wload
                if e < 17:
                    m_wload = load_expert(e + 1, (e + 1) % 2)
                if e < 16:
                    ctx.wait("gpsimd", "v", gdone)
                    if last_gb_t:
                        ctx.wait("gpsimd", "t", last_gb_t)
                    m_gr = ctx.dma("gpsimd", grow[:], gates_fm[e:e + 1, :], sem="ldg")
                    gb_marks = []
                    for n in range(NT):
                        m_gb = _mk(ctx, "tensor", [m_gr, ("v", gdone)] +
                                   ([("v", gb_marks[-1])] if gb_marks else []),
                                   lambda t, n=n: t.matmul(ps_g[:], ones1b[:],
                                                           grow[:, n * 512:(n + 1) * 512],
                                                           start=True, stop=True), "t")
                        last_gb_t = m_gb
                        gb_marks.append(_mk(ctx, "vector", [("t", m_gb)],
                                            lambda v, n=n: v.tensor_copy(
                                                gate_bc[:, n * 512:(n + 1) * 512], ps_g[:]), "v"))
                    gate_ready = gb_marks[-1]
                else:
                    gate_ready = 0

                def emit_w2b(pendb):
                    nonlocal evac_w2_mark
                    (pn, p_gu, gset, wref) = pendb
                    psl = slice(pn * 512, (pn + 1) * 512)
                    for Md in range(DC):
                        def fw2(t, Md=Md, gset=gset, wref=wref):
                            for Kf in range(4):
                                t.matmul(ps_w2[:], wref[Kf][:, Md * 128:(Md + 1) * 128],
                                         gset[Kf][:], start=(Kf == 0), stop=(Kf == 3))
                        mw2 = _mk(ctx, "tensor", [("v", p_gu), ("v", evac_w2_mark)], fw2, "t")
                        evac_w2_mark = _mk(ctx, "vector", [("t", mw2)],
                                           lambda v, Md=Md, psl=psl: v.tensor_tensor(
                                               out=racc[Md][:, psl], in0=racc[Md][:, psl],
                                               in1=ps_w2[:], op=ALU.add), "v")

                for n in range(NT):
                    sl = slice(n * 512, (n + 1) * 512)
                    gset = g_sb if n % 2 == 0 else g_sb2
                    ev1 = []
                    for Mf in range(4):
                        waits = [("v", w_ready)]
                        if p_evac[Mf]:
                            waits.append(("s", p_evac[Mf]))
                        if prev_w2b_for.get(n % 2):
                            waits.append(("v", prev_w2b_for[n % 2]))
                        def f1(t, Mf=Mf, pp=pp, sl=sl):
                            for K in range(DC):
                                t.matmul(ps_p[Mf][:], w1_sb[pp][K][:, Mf * 128:(Mf + 1) * 128],
                                         x1_bf[K][:, sl], start=(K == 0), stop=(K == DC - 1))
                        m1 = _mk(ctx, "tensor", waits, f1, "t")
                        ev1.append(_mk(ctx, "scalar", [("t", m1)],
                                       lambda s, Mf=Mf, gset=gset: s.activation(
                                           out=gset[Mf][:], in_=ps_p[Mf][:], func=AF.Silu), "s"))
                    ev3 = []
                    for Mf in range(4):
                        def f3(t, Mf=Mf, pp=pp, sl=sl):
                            for K in range(DC):
                                t.matmul(ps_p[Mf][:], w3_sb[pp][K][:, Mf * 128:(Mf + 1) * 128],
                                         x1_bf[K][:, sl], start=(K == 0), stop=(K == DC - 1))
                        m3 = _mk(ctx, "tensor", [("s", ev1[Mf])], f3, "t")
                        w3w = [("t", m3)]
                        if prev_gub:
                            w3w.append(("v", prev_gub))
                        ev3.append(_mk(ctx, "scalar", w3w,
                                       lambda s, Mf=Mf: s.activation(out=u_sb[Mf][:], in_=ps_p[Mf][:],
                                                                     func=AF.Copy), "s"))
                        p_evac[Mf] = ev3[-1]
                    if pendb is not None:
                        emit_w2b(pendb)
                        prev_w2b_for[pendb[0] % 2] = evac_w2_mark
                        pendb = None
                    gu_marks = []
                    for Mf in range(4):
                        if e < 16:
                            def fgu(v, Mf=Mf, sl=sl, gset=gset):
                                v.tensor_tensor(out=gset[Mf][:], in0=gset[Mf][:], in1=u_sb[Mf][:],
                                                op=ALU.mult)
                                v.tensor_tensor(out=gset[Mf][:], in0=gset[Mf][:],
                                                in1=gate_bc[:, sl], op=ALU.mult)
                            gu_marks.append(_mk(ctx, "vector",
                                                [("s", ev3[Mf]), ("v", gate_ready)], fgu, "v"))
                        else:
                            gu_marks.append(_mk(ctx, "vector", [("s", ev3[Mf])],
                                                lambda v, Mf=Mf, gset=gset: v.tensor_tensor(
                                                    out=gset[Mf][:], in0=gset[Mf][:],
                                                    in1=u_sb[Mf][:], op=ALU.mult), "v"))
                    prev_gub = gu_marks[-1]
                    pendb = (n, prev_gub, gset, w2_sb[pp])
                if pendb is not None:
                    emit_w2b(pendb)
                    prev_w2b_for[pendb[0] % 2] = evac_w2_mark
                    pendb = None
            ffn_done = evac_w2_mark

            hc = 0
            for M in range(DC):
                hc = _mk(ctx, "vector", [("v", ffn_done)],
                         lambda v, M=M: v.tensor_copy(h_bf[M][:], racc[M][:]), "v")
            m_ln = emit_ln(ctx, h_bf, o_bf, S, ("v", hc), ps_p[0], ps_p[1], ps_ln,
                           hsqA, hsqB, mean_row, var_row, tmp_row, eps_col,
                           mean_bc, rstd_bc, onesb, ones1f)
            for M in range(DC):
                ctx.wait("gpsimd", "v", m_ln)
                ctx.dma("gpsimd", x2[b, M * 128:(M + 1) * 128, :], o_bf[M][:], sem="wr")




def emit_stage_c(nc, ctx, DD):
    x2 = DD["x2"]
    wqkvT = DD["wqkvT"]
    woT = DD["woT"]
    wpl = DD["wpl"]
    pVgs = DD["pVgs"]
    pVT0 = DD["pVT0"]
    pBtau = DD["pBtau"]
    pBgm = DD["pBgm"]
    pCch = DD["pCch"]
    pGam = DD["pGam"]
    pAlp = DD["pAlp"]
    pIC = DD["pIC"]
    repsel = DD["repsel"]
    deltam = DD["deltam"]
    mklnT = DD["mklnT"]
    hselT = DD["hselT"]
    x3 = DD["x3"]

    CT = 512
    NCH = S // CT

    with ExitStack() as es:
        E = es.enter_context
        stg = [E(nc.sbuf_tensor(f"stg{i}", [128, 1024], F32)) for i in range(2)]
        wqkv_bf = [E(nc.sbuf_tensor(f"wqkv{i}", [128, 3 * D], BF16)) for i in range(DC)]
        wo_bf = [E(nc.sbuf_tensor(f"wo{i}", [128, D], BF16)) for i in range(DC)]
        wlt_bf = E(nc.sbuf_tensor("wlt_bf", [128, 1024], BF16))
        r_cols = E(nc.sbuf_tensor("r_cols", [128, 512], F32))
        c_cols = E(nc.sbuf_tensor("c_cols", [128, 512], F32))
        repsel_sb = E(nc.sbuf_tensor("repsel_sb", [128, 32 * 128], BF16))
        deltam_sb = E(nc.sbuf_tensor("deltam_sb", [128, 32 * 64], BF16))
        mkln_sb = E(nc.sbuf_tensor("mkln_sb", [128, 8 * 16], BF16))
        hsel_sb = E(nc.sbuf_tensor("hsel_sb", [16, 8 * 128], BF16))
        states = E(nc.sbuf_tensor("states", [128, 1024], F32))
        onescol = E(nc.sbuf_tensor("onescol", [64, 1], F32))
        ksq = E(nc.sbuf_tensor("ksq", [128, CT], BF16))
        kmu = E(nc.sbuf_tensor("kmu", [16, CT], F32))
        kvar = E(nc.sbuf_tensor("kvar", [16, CT], F32))
        ktmp = E(nc.sbuf_tensor("ktmp", [16, CT], F32))
        kmu_bf = E(nc.sbuf_tensor("kmu_bf", [16, CT], BF16))
        krs_bf = E(nc.sbuf_tensor("krs_bf", [16, CT], BF16))
        mu_bc = E(nc.sbuf_tensor("mu_bc", [128, CT], BF16))
        rs_bc = E(nc.sbuf_tensor("rs_bc", [128, CT], BF16))
        eps16 = E(nc.sbuf_tensor("eps16", [16, 1], F32))
        krep = E(nc.sbuf_tensor("krep", [128, CT], F32))
        qrep = E(nc.sbuf_tensor("qrep", [128, CT], F32))
        u_t = [E(nc.sbuf_tensor(f"u_t{i}", [128, CT], F32)) for i in range(2)]
        ftr = E(nc.sbuf_tensor("ftr", [128, CT], F32))
        p2 = [E(nc.sbuf_tensor(f"p2_{i}", [128, CT], BF16)) for i in range(2)]
        o_sb = E(nc.sbuf_tensor("o_sb", [128, 512], BF16))
        h_bf = [E(nc.sbuf_tensor(f"hb{i}", [128, 512], BF16)) for i in range(DC)]
        x3_bf = [E(nc.sbuf_tensor(f"x3b{i}", [128, 512], BF16)) for i in range(DC)]
        hsqA = E(nc.sbuf_tensor("hsqA", [128, 512], BF16))
        hsqB = E(nc.sbuf_tensor("hsqB", [128, 512], BF16))
        mean_row = E(nc.sbuf_tensor("mean_row", [1, 512], F32))
        var_row = E(nc.sbuf_tensor("var_row", [1, 512], F32))
        tmp_row = E(nc.sbuf_tensor("tmp_row", [1, 512], F32))
        eps1 = E(nc.sbuf_tensor("eps1", [1, 1], F32))
        mean_bc = [E(nc.sbuf_tensor("meanbcC", [128, 512], BF16))]
        rstd_bc = [E(nc.sbuf_tensor("rstdbcC", [128, 512], BF16))]
        onesb = E(nc.sbuf_tensor("onesb", [128, 1], BF16))
        ones1f = E(nc.sbuf_tensor("ones1f", [1, 128], F32))

        ps_a = [E(nc.psum_tensor(f"psa{i}", [128, 512], F32)) for i in range(4)]
        ps_vr = [E(nc.psum_tensor(f"psvr{i}", [128, 512], F32)) for i in range(2)]
        ps_y = E(nc.psum_tensor("psy", [64, 512], F32))
        ps_ln = E(nc.psum_tensor("pslnC", [128, 512], F32))


        cm = []
        cm.append(ctx.dma("sync", repsel_sb[:], repsel))
        cm.append(ctx.dma("sync", deltam_sb[:], deltam))
        cm.append(ctx.dma("sync", mkln_sb[:], mklnT))
        cm.append(ctx.dma("sync", hsel_sb[:], hselT))
        es2 = ExitStack()
        E2 = es2.enter_context
        pl = {}
        for nm in ["w", "vgs", "vt0", "btau", "bgm", "cch", "gam", "alp", "ic"]:
            pl[nm] = E2(nc.sbuf_tensor("pl_" + nm, [64, 1024], F32))
        sA = E2(nc.sbuf_tensor("sA", [64, 1024], F32))
        sB = E2(nc.sbuf_tensor("sB", [64, 1024], F32))
        sC = E2(nc.sbuf_tensor("sC", [64, 1024], F32))
        r_pl = E2(nc.sbuf_tensor("r_pl", [64, 1024], F32))
        c_pl = E2(nc.sbuf_tensor("c_pl", [64, 1024], F32))
        for nm, src in [("w", wpl), ("vgs", pVgs), ("vt0", pVT0), ("btau", pBtau),
                        ("bgm", pBgm), ("cch", pCch), ("gam", pGam), ("alp", pAlp), ("ic", pIC)]:
            cm.append(ctx.dma("sync", pl[nm][:], src))
        pl_ld = cm[-1]
        m_gc = _mk(ctx, "gpsimd", [],
                   lambda g: (g.memset(onesb[:], 1.0), g.memset(ones1f[:], 1.0),
                              g.memset(eps16[:], LN_EPS), g.memset(eps1[:], LN_EPS),
                              g.memset(onescol[:], 1.0)), "g")
        for _e in ("tensor", "vector", "scalar"):
            ctx.wait(_e, "ld", ctx.ctr["ld"].v)
            ctx.wait(_e, "g", ctx.ctr["g"].v)

        def f1(v):
            v.tensor_tensor(out=sA[:], in0=pl["vgs"][:], in1=pl["vt0"][:], op=ALU.subtract)
            v.tensor_tensor(out=sA[:], in0=sA[:], in1=pl["w"][:], op=ALU.add)
        m1 = _mk(ctx, "vector", [pl_ld], f1, "v")
        def f2(s):
            s.activation(out=sB[:], in_=sA[:], func=AF.Abs)
            s.activation(out=sB[:], in_=sB[:], func=AF.Exp, scale=-1.0)
        m2 = _mk(ctx, "scalar", [("v", m1), ("g", m_gc)], f2, "s")
        def f3(s):
            s.activation(out=sB[:], in_=sB[:], func=AF.Ln, bias=onescol[:], scale=1.0)
        m3 = _mk(ctx, "scalar", [("s", m2)], f3, "s")
        def f4(v):
            v.tensor_relu(sC[:], sA[:])
            v.tensor_tensor(out=sB[:], in0=sB[:], in1=sC[:], op=ALU.add)
            v.tensor_tensor(out=sC[:], in0=pl["btau"][:], in1=sB[:], op=ALU.mult)
            v.reciprocal(out=r_pl[:], in_=pl["cch"][:])
            v.tensor_tensor(out=r_pl[:], in0=sC[:], in1=r_pl[:], op=ALU.mult)
        m4 = _mk(ctx, "vector", [("s", m3)], f4, "v")
        def f5(s):
            s.activation(out=r_pl[:], in_=r_pl[:], func=AF.Exp, scale=-1.0)
            s.activation(out=c_pl[:], in_=sA[:], func=AF.Sigmoid)
        m5 = _mk(ctx, "scalar", [("v", m4)], f5, "s")
        def f6(v):
            v.tensor_tensor(out=c_pl[:], in0=c_pl[:], in1=sB[:], op=ALU.mult)
            v.tensor_tensor(out=c_pl[:], in0=c_pl[:], in1=pl["bgm"][:], op=ALU.mult)
            v.tensor_tensor(out=sB[:], in0=sC[:], in1=pl["ic"][:], op=ALU.subtract)
            v.tensor_tensor(out=sB[:], in0=sB[:], in1=pl["alp"][:], op=ALU.mult)
        m6 = _mk(ctx, "vector", [("s", m5)], f6, "v")
        def f7(s):
            s.activation(out=sB[:], in_=sB[:], func=AF.Tanh)
        m7 = _mk(ctx, "scalar", [("v", m6)], f7, "s")
        def f8(v):
            v.tensor_tensor(out=c_pl[:], in0=c_pl[:], in1=sB[:], op=ALU.mult)
            v.tensor_tensor(out=c_pl[:], in0=c_pl[:], in1=pl["gam"][:], op=ALU.mult)
            for dpar in range(2):
                for (colsrc, coldst) in [(r_pl, r_cols), (c_pl, c_cols)]:
                    src = colsrc[:, :].rearrange("e (h d) -> e h d", h=16)[:, :, dpar::2]
                    dst = coldst[dpar * 64:(dpar + 1) * 64, :].rearrange("e (h j) -> e h j", h=16)
                    v.tensor_copy(dst, src)
            v.tensor_copy(wlt_bf[0:64, :], pl["w"][:])
            v.tensor_copy(wlt_bf[64:128, :], pl["w"][:])
            v.memset(states[:], 0.0)
        m_planes = _mk(ctx, "vector", [("s", m7)], f8, "v")
        es2.close()
        x2c = [E(nc.sbuf_tensor(f"x2c_{i}", [128, CT], BF16)) for i in range(DC)]
        qc = [E(nc.sbuf_tensor(f"qc_{i}", [128, CT], BF16)) for i in range(DC)]
        kc = [E(nc.sbuf_tensor(f"kc_{i}", [128, CT], BF16)) for i in range(DC)]
        vc = [E(nc.sbuf_tensor(f"vc_{i}", [128, CT], BF16)) for i in range(DC)]
        yc = [E(nc.sbuf_tensor(f"yc_{i}", [128, CT], BF16)) for i in range(DC)]

        last_cast = {0: 0, 1: 0}
        wm = 0
        for i in range(DC):
            for part in range(3):
                bi = (i * 3 + part) % 2
                ctx.wait("sync", "v", last_cast[bi])
                m_ld = ctx.dma("sync", stg[bi][:], wqkvT[i * 128:(i + 1) * 128,
                                                        part * 1024:(part + 1) * 1024])
                def fc(v, bi=bi, i=i, part=part):
                    v.tensor_copy(wqkv_bf[i][:, part * 1024:(part + 1) * 1024], stg[bi][:])
                wm = _mk(ctx, "vector", [m_ld], fc, "v")
                last_cast[bi] = wm
        for i in range(DC):
            bi = i % 2
            ctx.wait("sync", "v", last_cast[bi])
            m_ld = ctx.dma("sync", stg[bi][:], woT[i * 128:(i + 1) * 128, :])
            def fc2(v, bi=bi, i=i):
                v.tensor_copy(wo_bf[i][:], stg[bi][:])
            wm = _mk(ctx, "vector", [m_ld], fc2, "v")
            last_cast[bi] = wm
        w_done = wm

        chunk_consumed = m_planes
        u_free = [0, 0]
        p2_free = [0, 0]
        vr_free = [0, 0]
        y_wr = 0

        for ch in range(NCH):
            t0 = ch * CT
            for b in range(B_PER):
                ctx.wait("sync", "v", chunk_consumed)
                xm = []
                for i in range(DC):
                    xm.append(ctx.dma("sync", x2c[i][:], x2[b, i * 128:(i + 1) * 128, t0:t0 + CT]))
                x_ld = xm[-1]
                evq = None
                evq_marks = []
                for part, dstt in [(0, qc), (1, kc), (2, vc)]:
                    for M in range(DC):
                        bank = ps_a[(part * DC + M) % 4]
                        waits = [x_ld, ("v", max(w_done, m_planes))]
                        if len(evq_marks) >= 4:
                            waits.append(("s", evq_marks[-4]))
                        def fqkv(t, part=part, M=M, bank=bank, b=b):
                            for K in range(DC):
                                t.matmul(bank[:], wqkv_bf[K][:, part * 1024 + M * 128:
                                                             part * 1024 + (M + 1) * 128],
                                         x2c[K][:], start=(K == 0), stop=(K == DC - 1))
                        m_mm = _mk(ctx, "tensor", waits, fqkv, "t")
                        def fev(s, dstt=dstt, M=M, bank=bank):
                            s.activation(out=dstt[M][:], in_=bank[:], func=AF.Copy)
                        evq = _mk(ctx, "scalar", [("t", m_mm)], fev, "s")
                        evq_marks.append(evq)
                qkv_done = evq

                def fmu(t, b=b):
                    for j in range(DC):
                        t.matmul(ps_a[0][0:16, :], mkln_sb[:, j * 16:(j + 1) * 16], kc[j][:],
                                 start=(j == 0), stop=(j == DC - 1))
                m_mu = _mk(ctx, "tensor", [("s", qkv_done)], fmu, "t")
                sq_prev = None
                for j in range(DC):
                    w = [("s", qkv_done)] if j == 0 else [("t", sq_prev)]
                    m_s = _mk(ctx, "scalar", w,
                              lambda s, j=j, b=b: s.square(out=ksq[:], in_=kc[j][:]), "s")
                    def fsq(t, j=j):
                        t.matmul(ps_a[1][0:16, :], mkln_sb[:, j * 16:(j + 1) * 16], ksq[:],
                                 start=(j == 0), stop=(j == DC - 1))
                    sq_prev = _mk(ctx, "tensor", [("s", m_s)], fsq, "t")
                def fkstat(v):
                    v.tensor_single_scalar(out=kmu[:], in_=ps_a[0][0:16, :], scalar=1.0 / 64, op=ALU.mult)
                    v.tensor_single_scalar(out=kvar[:], in_=ps_a[1][0:16, :], scalar=1.0 / 64, op=ALU.mult)
                    v.tensor_tensor(out=ktmp[:], in0=kmu[:], in1=kmu[:], op=ALU.mult)
                    v.tensor_tensor(out=kvar[:], in0=kvar[:], in1=ktmp[:], op=ALU.subtract)
                    v.tensor_copy(kmu_bf[:], kmu[:])
                m_kst = _mk(ctx, "vector", [("t", m_mu), ("t", sq_prev)], fkstat, "v")
                def fkrs(s):
                    s.activation(out=kvar[:], in_=kvar[:], func=AF.Ln, bias=eps16[:], scale=1.0)
                    s.activation(out=kvar[:], in_=kvar[:], func=AF.Exp, scale=-0.5)
                m_krs = _mk(ctx, "scalar", [("v", m_kst)], fkrs, "s")
                m_krsc = _mk(ctx, "vector", [("s", m_krs)],
                             lambda v: v.tensor_copy(krs_bf[:], kvar[:]), "v")
                kn_done = None
                for j in range(DC):
                    def fbc1(t, j=j):
                        t.matmul(ps_a[2][:, :], hsel_sb[:, j * 128:(j + 1) * 128], kmu_bf[:],
                                 start=True, stop=True)
                        t.matmul(ps_a[3][:, :], hsel_sb[:, j * 128:(j + 1) * 128], krs_bf[:],
                                 start=True, stop=True)
                    m_bc = _mk(ctx, "tensor", [("v", m_krsc)], fbc1, "t")
                    def fbce(v, j=j, b=b):
                        v.tensor_copy(mu_bc[:], ps_a[2][:, :])
                        v.tensor_copy(rs_bc[:], ps_a[3][:, :])
                        v.tensor_tensor(out=kc[j][:], in0=kc[j][:], in1=mu_bc[:], op=ALU.subtract)
                        v.tensor_tensor(out=kc[j][:], in0=kc[j][:], in1=rs_bc[:], op=ALU.mult)
                    kn_done = _mk(ctx, "vector", [("t", m_bc)], fbce, "v")
                    if j < DC - 1:
                        ctx.wait("tensor", "v", kn_done)

                for h in range(16):
                    tile = h // 2
                    r0 = (h % 2) * 64
                    kn_slice = kc[tile][r0:r0 + 64, :]
                    q_slice = qc[tile][r0:r0 + 64, :]
                    v_slice = vc[tile][r0:r0 + 64, :]
                    def frep(v, kn_slice=kn_slice, q_slice=q_slice):
                        v.tensor_copy(krep[0:64, :], kn_slice)
                        v.tensor_copy(krep[64:128, :], kn_slice)
                        v.tensor_copy(qrep[0:64, :], q_slice)
                        v.tensor_copy(qrep[64:128, :], q_slice)
                    waits = [("v", kn_done)]
                    m_rep = _mk(ctx, "vector", waits, frep, "v")
                    def fw(t, h=h, q_slice=q_slice, r0=r0):
                        t.matmul(ps_y[:, :], wlt_bf[r0:r0 + 64, h * 64:(h + 1) * 64], q_slice,
                                 start=True, stop=False, skip_group_check=True)
                    w_y = [("s", qkv_done), ("v", m_planes)]
                    if y_wr:
                        w_y.append(("s", y_wr))
                    m_w = _mk(ctx, "tensor", w_y, fw, "t")
                    last_scan = None
                    for j in range(32):
                        pp = j % 2
                        gidx = b * 512 + h * 32 + j
                        cidx = h * 32 + j
                        def fvr(t, j=j, pp=pp, v_slice=v_slice, r0=r0):
                            t.matmul(ps_vr[pp][:, :], repsel_sb[r0:r0 + 64, j * 128:(j + 1) * 128],
                                     v_slice, start=True, stop=True, skip_group_check=True)
                        w_vr = [("s", qkv_done)]
                        if vr_free[pp]:
                            w_vr.append(("v", vr_free[pp]))
                        m_vr = _mk(ctx, "tensor", w_vr, fvr, "t")
                        def fu(v, pp=pp, cidx=cidx):
                            v.scalar_tensor_tensor(out=u_t[pp][:], in0=ps_vr[pp][:, :],
                                                   scalar=c_cols[:, cidx:cidx + 1], in1=krep[:],
                                                   op0=ALU.mult, op1=ALU.mult)
                        w_u = [("t", m_vr), ("v", m_rep)]
                        if u_free[pp]:
                            w_u.append(("v", u_free[pp]))
                        m_u = _mk(ctx, "vector", w_u, fu, "v")
                        vr_free[pp] = m_u
                        def fscan(v, pp=pp, gidx=gidx, cidx=cidx):
                            v.tensor_tensor_scan(out=ftr[:], data0=r_cols[:, cidx:cidx + 1].broadcast_to([128, CT]),
                                                 data1=u_t[pp][:], initial=states[:, gidx:gidx + 1],
                                                 op0=ALU.mult, op1=ALU.add)
                            v.tensor_copy(states[:, gidx:gidx + 1], ftr[:, CT - 1:CT])
                            v.tensor_tensor(out=p2[pp][:], in0=ftr[:], in1=qrep[:], op=ALU.mult)
                        w_s = [("v", m_u)]
                        if p2_free[pp]:
                            w_s.append(("t", p2_free[pp]))
                        m_scan = _mk(ctx, "vector", w_s, fscan, "v")
                        u_free[pp] = m_scan
                        def fro(t, j=j, pp=pp):
                            t.matmul(ps_y[:, :], deltam_sb[:, j * 64:(j + 1) * 64], p2[pp][:],
                                     start=False, stop=(j == 31), skip_group_check=True)
                        m_ro = _mk(ctx, "tensor", [("v", m_scan), ("t", m_w)], fro, "t")
                        p2_free[pp] = m_ro
                        last_scan = m_ro
                    def fye(s, b=b, tile=tile, r0=r0):
                        s.activation(out=yc[tile][r0:r0 + 64, :], in_=ps_y[:, :], func=AF.Copy)
                    y_wr = _mk(ctx, "scalar", [("t", last_scan)], fye, "s")
                y_done = y_wr

                evo = None
                for M in range(DC):
                    def fwo(t, M=M, b=b):
                        for K in range(DC):
                            t.matmul(ps_ln[:, :], wo_bf[K][:, M * 128:(M + 1) * 128], yc[K][:],
                                     start=(K == 0), stop=(K == DC - 1))
                    w_wo = [("s", y_done)]
                    if evo:
                        w_wo.append(("v", evo))
                    m_wo = _mk(ctx, "tensor", w_wo, fwo, "t")
                    def fre(v, M=M, b=b):
                        v.tensor_copy(o_sb[:], ps_ln[:, :])
                        v.tensor_tensor(out=h_bf[M][:], in0=o_sb[:], in1=x2c[M][:], op=ALU.add)
                    evo = _mk(ctx, "vector", [("t", m_wo)], fre, "v")
                h_done = evo
                ctx.wait("vector", "wr", ctx.ctr["wr"].v)
                m_ln = emit_ln(ctx, h_bf, x3_bf, 512, ("v", h_done), ps_a[0], ps_a[1], ps_ln,
                               hsqA, hsqB, mean_row, var_row, tmp_row, eps1,
                               mean_bc, rstd_bc, onesb, ones1f)
                for M in range(DC):
                    ctx.wait("gpsimd", "v", m_ln)
                    ctx.dma("gpsimd", x3[b, M * 128:(M + 1) * 128, t0:t0 + CT], x3_bf[M][:], sem="wr")
                chunk_consumed = m_ln




def emit_stage_d(nc, ctx, DD):
    x3 = DD["x3"]
    rw = DD["rw"]
    rb = DD["rb"]
    sw1 = DD["sw1"]
    sw2 = DD["sw2"]
    sw3 = DD["sw3"]
    ew1 = DD["ew1"]
    ew2 = DD["ew2"]
    ew3 = DD["ew3"]
    identf = DD["identf"]
    outp = DD["outp"]

    with ExitStack() as es:
        E = es.enter_context
        x3_bf = [E(nc.sbuf_tensor(f"x3b{i}", [128, S], BF16)) for i in range(DC)]
        racc = [E(nc.sbuf_tensor(f"racc{i}", [128, S], F32)) for i in range(DC)]
        stg = [E(nc.sbuf_tensor(f"stg{i}", [128, 1024], F32)) for i in range(2)]
        w1d = [E(nc.sbuf_tensor(f"w1d{i}", [128, 1024], BF16)) for i in range(DC)]
        w2d = [E(nc.sbuf_tensor(f"w2d{i}", [128, 1024], BF16)) for i in range(DC)]
        w3d = [E(nc.sbuf_tensor(f"w3d{i}", [128, 1024], BF16)) for i in range(DC)]
        rw_bf = [E(nc.sbuf_tensor(f"rwb{i}", [128, 8], BF16)) for i in range(DC)]
        rb_sb = E(nc.sbuf_tensor("rb_sb", [8, 1], F32))
        identfs = E(nc.sbuf_tensor("identfs", [128, 128], F32))
        rout_fm = E(nc.sbuf_tensor("rout_fm", [8, S], F32))
        gates_fm = E(nc.sbuf_tensor("gates_fm", [8, S], BF16))
        grow = E(nc.sbuf_tensor("grow", [1, S], BF16))
        ones1b = E(nc.sbuf_tensor("ones1b", [1, 128], BF16))
        gate_bc = E(nc.sbuf_tensor("gate_bc", [128, S], BF16))
        gtm = E(nc.sbuf_tensor("gtm", [128, 8], F32))
        t8a = E(nc.sbuf_tensor("t8a", [128, 8], F32))
        t8b = E(nc.sbuf_tensor("t8b", [128, 8], F32))
        c1 = E(nc.sbuf_tensor("c1", [128, 1], F32))
        c2 = E(nc.sbuf_tensor("c2", [128, 1], F32))
        nbig = E(nc.sbuf_tensor("nbig", [128, 1], F32))
        g_sb = [E(nc.sbuf_tensor(f"g_sb{i}", [128, 512], BF16)) for i in range(DC)]
        g_sb2 = [E(nc.sbuf_tensor(f"g_sb2{i}", [128, 512], BF16)) for i in range(DC)]
        u_sb = [E(nc.sbuf_tensor(f"u_sb{i}", [128, 512], BF16)) for i in range(DC)]
        h_bf = g_sb
        hsqA = E(nc.sbuf_tensor("hsqA", [128, 512], BF16))
        hsqB = E(nc.sbuf_tensor("hsqB", [128, 512], BF16))
        mean_row = E(nc.sbuf_tensor("mean_row", [1, 512], F32))
        var_row = E(nc.sbuf_tensor("var_row", [1, 512], F32))
        tmp_row = E(nc.sbuf_tensor("tmp_row", [1, 512], F32))
        eps1 = E(nc.sbuf_tensor("eps1", [1, 1], F32))
        mean_bc = [E(nc.sbuf_tensor("meanbcD", [128, 512], BF16))]
        rstd_bc = [E(nc.sbuf_tensor("rstdbcD", [128, 512], BF16))]
        onesb = E(nc.sbuf_tensor("onesb", [128, 1], BF16))
        ones1f = E(nc.sbuf_tensor("ones1f", [1, 128], F32))

        ps_p = [E(nc.psum_tensor(f"psp{i}", [128, 512], F32)) for i in range(4)]
        ps_w2 = E(nc.psum_tensor("psw2", [128, 512], F32))
        ps_g = E(nc.psum_tensor("psg", [128, 512], F32))
        ps_ln = E(nc.psum_tensor("pslnD", [128, 512], F32))


        m_id = ctx.dma("sync", identfs[:], identf)
        m_rb = ctx.dma("sync", rb_sb[:], rb)
        m_gc = _mk(ctx, "gpsimd", [],
                   lambda g: (g.memset(onesb[:], 1.0), g.memset(ones1f[:], 1.0),
                              g.memset(ones1b[:], 1.0),
                              g.memset(eps1[:], LN_EPS), g.memset(nbig[:], -BIG)), "g")
        for _e in ("tensor", "vector", "scalar"):
            ctx.wait(_e, "ld", ctx.ctr["ld"].v)
            ctx.wait(_e, "g", ctx.ctr["g"].v)

        last_cast = {0: 0, 1: 0}
        wm = 0
        for i in range(DC):
            bi = i % 2
            ctx.wait("sync", "v", last_cast[bi])
            m_ld = ctx.dma("sync", stg[bi][:, 0:8], rw[i * 128:(i + 1) * 128, :])
            def fc(v, bi=bi, i=i):
                v.tensor_copy(rw_bf[i][:], stg[bi][:, 0:8])
            wm = _mk(ctx, "vector", [m_ld], fc, "v")
            last_cast[bi] = wm
        rw_done = wm

        def load_half(e, half):
            nonlocal wm
            h0 = half * 1024
            srcs = []
            if e < 8:
                for i in range(DC):
                    srcs.append((ew1[e, i * 128:(i + 1) * 128, h0:h0 + 1024], w1d[i]))
                for i in range(DC):
                    srcs.append((ew2[e, i * 128:(i + 1) * 128, h0:h0 + 1024], w2d[i]))
                for i in range(DC):
                    srcs.append((ew3[e, h0 + i * 128:h0 + (i + 1) * 128, :], w3d[i]))
            else:
                for i in range(DC):
                    srcs.append((sw1[i * 128:(i + 1) * 128, h0:h0 + 1024], w1d[i]))
                for i in range(DC):
                    srcs.append((sw2[i * 128:(i + 1) * 128, h0:h0 + 1024], w2d[i]))
                for i in range(DC):
                    srcs.append((sw3[h0 + i * 128:h0 + (i + 1) * 128, :], w3d[i]))
            marks = []
            for j, (src, dst) in enumerate(srcs):
                bi = j % 2
                ctx.wait("sync", "v", last_cast[bi])
                m_ld = ctx.dma("sync", stg[bi][:], src)
                def fc(v, bi=bi, dst=dst):
                    v.tensor_copy(dst[:], stg[bi][:])
                wm = _mk(ctx, "vector", [m_ld], fc, "v")
                last_cast[bi] = wm
                marks.append(wm)
            return marks[-1]

        for b in range(B_PER):
            if b > 0:
                ctx.wait("sync", "wr", ctx.ctr["wr"].v)
            xm = []
            for i in range(DC):
                xm.append(ctx.dma("sync", x3_bf[i][:], x3[b, i * 128:(i + 1) * 128, :]))
            x_ld = xm[-1]

            rlast = 0
            for n in range(NT):
                def fr(t, n=n):
                    for K in range(DC):
                        t.matmul(ps_g[0:8, :], rw_bf[K][:], x3_bf[K][:, n * 512:(n + 1) * 512],
                                 start=(K == 0), stop=(K == DC - 1))
                waits = [x_ld, m_rb, ("v", rw_done)]
                if rlast:
                    waits.append(("s", rlast))
                m_r = _mk(ctx, "tensor", waits, fr, "t")
                def fre(s, n=n):
                    s.activation(out=rout_fm[:, n * 512:(n + 1) * 512], in_=ps_g[0:8, :],
                                 func=AF.Identity, bias=rb_sb[:], scale=1.0)
                rlast = _mk(ctx, "scalar", [("t", m_r)], fre, "s")
            gdone = 0
            for stt in range(16):
                waits = [("s", rlast), m_id, ("g", m_gc)]
                if gdone:
                    waits.append(("v", gdone))
                m_tr = _mk(ctx, "tensor", waits,
                           lambda t, stt=stt: t.transpose(
                               ps_g[:, 0:8], rout_fm[:, stt * 128:(stt + 1) * 128],
                               identfs[0:8, 0:8]),
                           "t")
                def ftop(v):
                    v.tensor_copy(gtm[:], ps_g[:, 0:8])
                    v.tensor_reduce(out=c1[:], in_=gtm[:], axis=AX.X, op=ALU.max)
                    v.tensor_single_scalar(out=c1[:], in_=c1[:], scalar=-1.0, op=ALU.mult)
                m_t1 = _mk(ctx, "vector", [("t", m_tr)], ftop, "v")
                def fexp(s):
                    s.activation(out=gtm[:], in_=gtm[:], func=AF.Exp, bias=c1[:], scale=1.0)
                m_t2 = _mk(ctx, "scalar", [("v", m_t1)], fexp, "s")
                def ftop2(v, stt=stt):
                    v.tensor_reduce(out=c2[:], in_=gtm[:], axis=AX.X, op=ALU.add)
                    v.reciprocal(out=c2[:], in_=c2[:])
                    v.tensor_scalar_mul(gtm[:], gtm[:], c2[:])
                    v.tensor_reduce(out=c1[:], in_=gtm[:], axis=AX.X, op=ALU.max)
                    v.tensor_scalar(out=t8a[:], in0=gtm[:], scalar1=c1[:], scalar2=None,
                                    op0=ALU.is_ge)
                    v.scalar_tensor_tensor(out=t8b[:], in0=t8a[:], scalar=nbig[:],
                                           in1=gtm[:], op0=ALU.mult, op1=ALU.add)
                    v.tensor_reduce(out=c2[:], in_=t8b[:], axis=AX.X, op=ALU.max)
                    v.tensor_scalar(out=t8a[:], in0=gtm[:], scalar1=c2[:], scalar2=None,
                                    op0=ALU.is_ge)
                    v.tensor_tensor(out=t8a[:], in0=gtm[:], in1=t8a[:], op=ALU.mult)
                    v.tensor_reduce(out=c2[:], in_=t8a[:], axis=AX.X, op=ALU.add)
                    v.reciprocal(out=c2[:], in_=c2[:])
                    v.tensor_scalar_mul(t8a[:], t8a[:], c2[:])
                m_t3 = _mk(ctx, "vector", [("s", m_t2)], ftop2, "v")
                m_tb = _mk(ctx, "tensor", [("v", m_t3)],
                           lambda t: t.transpose(ps_g[0:8, 0:128], t8a[:], identfs[:]), "t")
                gdone = _mk(ctx, "vector", [("t", m_tb)],
                            lambda v, stt=stt: v.tensor_copy(
                                gates_fm[:, stt * 128:(stt + 1) * 128], ps_g[0:8, 0:128]), "v")

            racc_mark = 0
            for M in range(DC):
                racc_mark = _mk(ctx, "vector", [x_ld],
                                lambda v, M=M: v.tensor_copy(racc[M][:], x3_bf[M][:]), "v")

            m_wload = load_half(0, 0)
            last_gb_t = 0
            evac_mark = racc_mark
            p_evac = {0: 0, 1: 0, 2: 0, 3: 0}
            pend = None
            prev_gu = 0
            prev_w2_for = {0: 0, 1: 0}
            for ee in range(18):
                e, half = ee // 2, ee % 2
                pp = 0
                w_ready = m_wload
                if e < 8 and half == 0:
                    ctx.wait("gpsimd", "v", gdone)
                    if last_gb_t:
                        ctx.wait("gpsimd", "t", last_gb_t)
                    m_gr = ctx.dma("gpsimd", grow[:], gates_fm[e:e + 1, :], sem="ldg")
                    gb = 0
                    for n in range(NT):
                        m_gb = _mk(ctx, "tensor", [m_gr, ("v", gdone)] +
                                   ([("v", gb)] if gb else []),
                                   lambda t, n=n: t.matmul(ps_g[:], ones1b[:],
                                                           grow[:, n * 512:(n + 1) * 512],
                                                           start=True, stop=True), "t")
                        last_gb_t = m_gb
                        gb = _mk(ctx, "vector", [("t", m_gb)],
                                 lambda v, n=n: v.tensor_copy(
                                     gate_bc[:, n * 512:(n + 1) * 512], ps_g[:]), "v")
                    gate_ready = gb
                elif e >= 8:
                    gate_ready = 0
                def emit_w2(pend):
                    nonlocal evac_mark
                    (pn, p_gu, gset, w2ref) = pend
                    psl = slice(pn * 512, (pn + 1) * 512)
                    for Md in range(DC):
                        def fw2(t, Md=Md, gset=gset, w2ref=w2ref):
                            for Kf in range(DC):
                                t.matmul(ps_w2[:], w2ref[Kf][:, Md * 128:(Md + 1) * 128],
                                         gset[Kf][:], start=(Kf == 0), stop=(Kf == DC - 1))
                        mw2 = _mk(ctx, "tensor", [("v", p_gu), ("v", evac_mark)], fw2, "t")
                        evac_mark = _mk(ctx, "vector", [("t", mw2)],
                                        lambda v, Md=Md, psl=psl: v.tensor_tensor(
                                            out=racc[Md][:, psl], in0=racc[Md][:, psl],
                                            in1=ps_w2[:], op=ALU.add), "v")

                for n in range(NT):
                    sl = slice(n * 512, (n + 1) * 512)
                    gset = g_sb if n % 2 == 0 else g_sb2
                    for Mf in range(DC):
                        bank = ps_p[Mf % 4]
                        waits = [("v", w_ready)]
                        if p_evac[Mf % 4]:
                            waits.append(("s", p_evac[Mf % 4]))
                        if gset is g_sb and prev_w2_for.get(0):
                            waits.append(("v", prev_w2_for[0]))
                        if gset is g_sb2 and prev_w2_for.get(1):
                            waits.append(("v", prev_w2_for[1]))
                        def f1(t, Mf=Mf, bank=bank, sl=sl):
                            for K in range(DC):
                                t.matmul(bank[:], w1d[K][:, Mf * 128:(Mf + 1) * 128],
                                         x3_bf[K][:, sl], start=(K == 0), stop=(K == DC - 1))
                        mm1 = _mk(ctx, "tensor", waits, f1, "t")
                        p_evac[Mf % 4] = _mk(ctx, "scalar", [("t", mm1)],
                                             lambda s, Mf=Mf, bank=bank, gset=gset: s.activation(
                                                 out=gset[Mf][:], in_=bank[:], func=AF.Silu), "s")
                    ev3 = [0] * DC
                    for Mf in range(DC):
                        bank = ps_p[Mf % 4]
                        def f3(t, Mf=Mf, bank=bank, sl=sl):
                            for K in range(DC):
                                t.matmul(bank[:], w2d[K][:, Mf * 128:(Mf + 1) * 128],
                                         x3_bf[K][:, sl], start=(K == 0), stop=(K == DC - 1))
                        mm3 = _mk(ctx, "tensor", [("s", p_evac[Mf % 4])], f3, "t")
                        w3 = [("t", mm3)]
                        if prev_gu:
                            w3.append(("v", prev_gu))
                        ev3[Mf] = _mk(ctx, "scalar", w3,
                                      lambda s, Mf=Mf, bank=bank: s.activation(
                                          out=u_sb[Mf][:], in_=bank[:], func=AF.Copy), "s")
                        p_evac[Mf % 4] = ev3[Mf]
                    if pend is not None:
                        emit_w2(pend)
                        prev_w2_for[pend[0] % 2] = evac_mark
                        pend = None
                    gu_done = 0
                    for Mf in range(DC):
                        if e < 8:
                            def fgu(v, Mf=Mf, sl=sl, gset=gset):
                                v.tensor_tensor(out=gset[Mf][:], in0=gset[Mf][:], in1=u_sb[Mf][:],
                                                op=ALU.mult)
                                v.tensor_tensor(out=gset[Mf][:], in0=gset[Mf][:],
                                                in1=gate_bc[:, sl], op=ALU.mult)
                            gu_done = _mk(ctx, "vector", [("s", ev3[Mf]), ("v", gate_ready)], fgu, "v")
                        else:
                            gu_done = _mk(ctx, "vector", [("s", ev3[Mf])],
                                          lambda v, Mf=Mf, gset=gset: v.tensor_tensor(
                                              out=gset[Mf][:], in0=gset[Mf][:], in1=u_sb[Mf][:],
                                              op=ALU.mult), "v")
                    prev_gu = gu_done
                    pend = (n, gu_done, gset, w3d)
                    prev_w2_for[n % 2] = 0
                if pend is not None:
                    emit_w2(pend)
                    prev_w2_for[pend[0] % 2] = evac_mark
                    pend = None
                if ee < 17:
                    e2, h2 = (ee + 1) // 2, (ee + 1) % 2
                    m_wload = load_half(e2, h2)
            ffn_done = evac_mark

            for n in range(NT):
                sl = slice(n * 512, (n + 1) * 512)
                hc = 0
                for M in range(DC):
                    hc = _mk(ctx, "vector", [("v", ffn_done)],
                             lambda v, M=M, sl=sl: v.tensor_copy(h_bf[M][:], racc[M][:, sl]), "v")
                o_view = [racc[K][:, sl] for K in range(DC)]
                m_ln = emit_ln(ctx, h_bf, o_view, 512, ("v", hc), ps_p[0], ps_p[1], ps_ln,
                               hsqA, hsqB, mean_row, var_row, tmp_row, eps1,
                               mean_bc, rstd_bc, onesb, ones1f)
                for M in range(DC):
                    ctx.wait("gpsimd", "v", m_ln)
                    ctx.dma("gpsimd", outp[b, M * 128:(M + 1) * 128, sl], racc[M][:, sl], sem="wr")




def _constants():
    identbf = np.eye(128, dtype=NPBF)
    identf = np.eye(128, dtype=np.float32)
    repsel = np.zeros((64, 32, 128), dtype=NPBF)
    for j in range(32):
        for p in range(128):
            repsel[2 * j + p // 64, j, p] = 1
    deltam = np.zeros((128, 32, 64), dtype=NPBF)
    for p in range(128):
        for j in range(32):
            deltam[p, j, 2 * j + p // 64] = 1
    mkln = np.zeros((128, 8, 16), dtype=NPBF)
    for jj in range(8):
        for p in range(128):
            mkln[p, jj, 2 * jj + p // 64] = 1
    hsel = np.zeros((16, 8, 128), dtype=NPBF)
    for jj in range(8):
        for p in range(128):
            hsel[2 * jj + p // 64, jj, p] = 1
    repsel2 = np.concatenate([repsel.reshape(64, 32 * 128)] * 2, axis=0)
    return (identbf, identf,
            np.ascontiguousarray(repsel2),
            np.ascontiguousarray(deltam.reshape(128, 32 * 64)),
            np.ascontiguousarray(mkln.reshape(128, 8 * 16)),
            np.ascontiguousarray(hsel.reshape(16, 8 * 128)))


_NC_CACHE = {}


def _get_nc(name, builder):
    if name not in _NC_CACHE:
        _NC_CACHE[name] = builder()
    return _NC_CACHE[name]



def build_full():
    nc = bass.Bass("TRN2", target_bir_lowering=False)
    D_ = {}
    def inp(name, shape, dt=F32):
        D_[name] = nc.dram_tensor(name, shape, dt, kind="ExternalInput").ap()
    def internal(name, shape, dt):
        D_[name] = nc.dram_tensor(name, shape, dt, kind="Internal").ap()
    inp("xf", [B_PER, D, S]); inp("xt", [B_PER, S, D])
    inp("wqT", [D, D]); inp("wkT", [D, D])
    inp("bqv", [128, DC]); inp("bkv", [128, DC])
    inp("identbf", [128, 128], BF16)
    internal("x1", [B_PER, D, S], BF16)
    inp("wg", [D, 16])
    inp("w1eT", [16, D, 512]); inp("w3eT", [16, D, 512]); inp("w2eT", [16, 512, D])
    inp("w1sT", [D, D]); inp("w3sT", [D, D]); inp("w2sT", [D, D])
    inp("identf", [128, 128])
    internal("x2", [B_PER, D, S], BF16)
    inp("wqkvT", [D, 3 * D]); inp("woT", [D, D]); inp("wpl", [64, 1024])
    for nm in ["pVgs", "pVT0", "pBtau", "pBgm", "pCch", "pGam", "pAlp", "pIC"]:
        inp(nm, [64, 1024])
    inp("repsel", [128, 32 * 128], BF16); inp("deltam", [128, 32 * 64], BF16)
    inp("mklnT", [128, 8 * 16], BF16); inp("hselT", [16, 8 * 128], BF16)
    internal("x3", [B_PER, D, S], BF16)
    inp("rw", [D, 8]); inp("rb", [8, 1])
    inp("sw1", [D, 2048]); inp("sw2", [D, 2048]); inp("sw3", [2048, D])
    inp("ew1", [8, D, 2048]); inp("ew2", [8, D, 2048]); inp("ew3", [8, 2048, D])
    D_["outp"] = nc.dram_tensor("outp", [B_PER, D, S], F32, kind="ExternalOutput").ap()

    class _NCProxy:
        def __init__(self, base, px):
            self._b = base
            self._px = px
        def sbuf_tensor(self, name, *a, **k):
            return self._b.sbuf_tensor(self._px + name, *a, **k)
        def psum_tensor(self, name, *a, **k):
            return self._b.psum_tensor(self._px + name, *a, **k)
        def __getattr__(self, n):
            return getattr(self._b, n)

    with ExitStack() as es0:
        ctx = Ctx(nc, es0)
        block = es0.enter_context(nc.Block())

        def barrier():
            marks = [(s, ctx.ctr[s].v) for s in
                     ("t", "v", "s", "g", "wr", "ld", "ld0", "ld1", "ldg")]
            for e in Ctx.ENGS:
                for (sn, v) in marks:
                    if v:
                        ctx.wait(e, sn, v)

        emit_stage_a(_NCProxy(nc, 'a_'), ctx, D_)
        barrier()
        ctx.new_dma_sems(es0, "b")
        emit_stage_b(_NCProxy(nc, 'b_'), ctx, D_)
        barrier()
        ctx.new_dma_sems(es0, "c")
        emit_stage_c(_NCProxy(nc, 'c_'), ctx, D_)
        barrier()
        ctx.new_dma_sems(es0, "d")
        emit_stage_d(_NCProxy(nc, 'd_'), ctx, D_)
        ctx.wait("gpsimd", "wr", ctx.ctr["wr"].v)
        ctx.finalize(block)
    return nc


def kernel(**inp):
    identbf, identf, repsel, deltam, mkln, hsel = _constants()
    x = np.asarray(inp["L0_attn_x"], dtype=np.float32)
    wqT = np.ascontiguousarray(np.asarray(inp["L0_attn_Wq"]).T)
    wkT = np.ascontiguousarray(np.asarray(inp["L0_attn_Wk"]).T)
    bq = np.ascontiguousarray(np.asarray(inp["L0_attn_bq"]).reshape(DC, 128).T)
    bk = np.ascontiguousarray(np.asarray(inp["L0_attn_bk"]).reshape(DC, 128).T)
    w1eT = np.ascontiguousarray(np.transpose(np.asarray(inp["L0_ffn_w1_e"]), (0, 2, 1)))
    w3eT = np.ascontiguousarray(np.transpose(np.asarray(inp["L0_ffn_w3_e"]), (0, 2, 1)))
    w2eT = np.ascontiguousarray(np.transpose(np.asarray(inp["L0_ffn_w2_e"]), (0, 2, 1)))
    w1sT = np.ascontiguousarray(np.asarray(inp["L0_ffn_w1_s"]).T)
    w3sT = np.ascontiguousarray(np.asarray(inp["L0_ffn_w3_s"]).T)
    w2sT = np.ascontiguousarray(np.asarray(inp["L0_ffn_w2_s"]).T)
    wg = np.ascontiguousarray(np.asarray(inp["L0_ffn_w_gate"]).T)
    wqkvT = np.ascontiguousarray(np.asarray(inp["L1_attn_W_qkv"]).T)
    woT = np.ascontiguousarray(np.asarray(inp["L1_attn_W_o"]).T)
    wltm = np.asarray(inp["L1_attn_W_LTM"])
    wpl = np.ascontiguousarray(np.transpose(wltm, (2, 0, 1)).reshape(64, 1024))
    def plane16(v):
        return np.ascontiguousarray(
            np.broadcast_to(np.asarray(v, dtype=np.float32)[None, :, None],
                            (64, 16, 64)).reshape(64, 1024))
    def plane1(v):
        return np.full((64, 1024), np.float32(np.asarray(v).reshape(-1)[0]), dtype=np.float32)
    rb = np.ascontiguousarray(np.asarray(inp["L1_ffn_rb"]).reshape(8, 1))

    nc = _get_nc("full", build_full)
    in_maps = []
    for c in range(NCORES):
        xs = x[2 * c:2 * c + 2]
        in_maps.append(dict(
            xf=np.ascontiguousarray(np.transpose(xs, (0, 2, 1))),
            xt=np.ascontiguousarray(xs),
            wqT=wqT, wkT=wkT, bqv=bq, bkv=bk, identbf=identbf,
            wg=wg, w1eT=w1eT, w3eT=w3eT, w2eT=w2eT,
            w1sT=w1sT, w3sT=w3sT, w2sT=w2sT, identf=identf,
            wqkvT=wqkvT, woT=woT, wpl=wpl,
            pVgs=plane16(inp["L1_attn_V_gs"]), pVT0=plane16(inp["L1_attn_V_T0"]),
            pBtau=plane16(inp["L1_attn_beta_tau"]), pBgm=plane16(inp["L1_attn_beta_gm"]),
            pCch=plane16(inp["L1_attn_C_ch"]), pGam=plane16(inp["L1_attn_gamma"]),
            pAlp=plane1(inp["L1_attn_alpha_ppd"]), pIC=plane1(inp["L1_attn_IC_threshold"]),
            repsel=repsel, deltam=deltam, mklnT=mkln, hselT=hsel,
            rw=np.asarray(inp["L1_ffn_rw"]), rb=rb,
            sw1=np.asarray(inp["L1_ffn_sw1"]), sw2=np.asarray(inp["L1_ffn_sw2"]),
            sw3=np.asarray(inp["L1_ffn_sw3"]), ew1=np.asarray(inp["L1_ffn_ew1"]),
            ew2=np.asarray(inp["L1_ffn_ew2"]), ew3=np.asarray(inp["L1_ffn_ew3"])))
    res = run_bass_kernel_spmd(nc, in_maps, core_ids=list(range(NCORES)))
    outs = [np.transpose(np.asarray(r["outp"], dtype=np.float32), (0, 2, 1))
            for r in res.results]
    return np.ascontiguousarray(np.concatenate(outs, axis=0))


# version 05 (final)
# speedup vs baseline: 1.5115x; 1.5056x over previous
import numpy as np
import ml_dtypes
from contextlib import ExitStack

import concourse.bass as bass
import concourse.mybir as mybir
from concourse.bass_utils import run_bass_kernel_spmd

F32 = mybir.dt.float32
BF16 = mybir.dt.bfloat16
AF = mybir.ActivationFunctionType
ALU = mybir.AluOpType
AX = mybir.AxisListType
NPBF = ml_dtypes.bfloat16

NCORES = 8
B_PER = 2
S = 2048
D = 1024
DC = D // 128
NT = S // 512
LN_EPS = 1e-5
BIG = 1e30


class Ctr:
    def __init__(self):
        self.v = 0
    def inc(self, n=1):
        self.v += n
        return self.v


class Ctx:
    ENGS = ["sync", "tensor", "vector", "scalar", "gpsimd"]

    def __init__(self, nc, es):
        self.nc = nc
        self.prog = {e: [] for e in self.ENGS}
        self.sem = {}
        self.ctr = {}
        for name in ["ld", "ld0", "ld1", "ldg", "t", "v", "s", "g", "wr"]:
            self.sem[name] = es.enter_context(nc.semaphore("sem_" + name))
            self.ctr[name] = Ctr()
        self.chain = {}
        for e in ["vector", "scalar", "gpsimd"]:
            self.chain[e] = [es.enter_context(nc.semaphore("chain_" + e)), 0]

    def new_dma_sems(self, es, suffix):
        for name in ["ld", "ld0", "ld1", "ldg", "wr"]:
            self.sem[name] = es.enter_context(self.nc.semaphore(f"sem_{name}_{suffix}"))
            self.ctr[name] = Ctr()

    def em(self, eng, fn):
        self.prog[eng].append(fn)

    def dma(self, q, out, in_, sem="ld"):
        try:
            nm = out.tensor.name
            if nm.endswith("stg0") or nm.endswith("stg1"):
                sem = "ld" + nm[-1]
        except Exception:
            pass
        s = self.sem[sem]
        def f(eng):
            eng.dma_start(out=out, in_=in_).then_inc(s, 16)
        self.em(q, f)
        return (sem, self.ctr[sem].inc(16))

    def wait(self, eng, sem, val):
        s = self.sem[sem]
        self.em(eng, lambda e: e.wait_ge(s, val))

    def finalize(self, block):
        for eng in self.ENGS:
            fns = self.prog[eng]
            dec = getattr(block, eng)
            def make(fns):
                def body(engine):
                    for f in fns:
                        f(engine)
                return body
            dec(make(fns))


class _Rec:
    def __init__(self, eng, chain):
        object.__setattr__(self, "_e", eng)
        object.__setattr__(self, "last", None)
        object.__setattr__(self, "chain", chain)

    def __getattr__(self, n):
        a = getattr(self._e, n)
        if callable(a):
            def w(*args, **kw):
                if self.last is not None and self.chain is not None:
                    csem, _ = self.chain
                    self.last.then_inc(csem, 1)
                    self.chain[1] += 1
                    self._e.wait_ge(csem, self.chain[1])
                    object.__setattr__(self, "last", None)
                r = a(*args, **kw)
                if hasattr(r, "then_inc"):
                    object.__setattr__(self, "last", r)
                return r
            return w
        return a


def _mk(ctx, eng, waits, fn, inc_sem=None):
    sems = ctx.sem
    chain = ctx.chain.get(eng)
    waits = [(sn, ctx.ctr["ld"].v) if sn == "ld" else (sn, v) for (sn, v) in waits]
    waits = [(ctx.sem[sn], v) for (sn, v) in waits]
    inc_h = ctx.sem[inc_sem] if inc_sem is not None else None
    def f(engine):
        for (sh, val) in waits:
            engine.wait_ge(sh, val)
        rec = _Rec(engine, chain)
        fn(rec)
        if inc_h is not None:
            if rec.last is not None:
                rec.last.then_inc(inc_h, 1)
            else:
                engine.sem_inc(inc_h, 1)
    ctx.em(eng, f)
    if inc_sem is not None:
        return ctx.ctr[inc_sem].inc()
    return None



def emit_ln(ctx, h_tiles, out_tiles, ns, ready, ps_a, ps_b, ps_c,
            hsqA, hsqB, mean_row, var_row, tmp_row, eps_col,
            mean_bc, rstd_bc, onesb, ones1f, col0=0):
    nslab = ns // 512
    out_marks = []
    for n in range(nslab):
        c0 = col0 + n * 512
        sl = slice(c0, c0 + 512)
        def fsum(t, sl=sl):
            for K in range(DC):
                t.matmul(ps_a[0:1, :], onesb[:], h_tiles[K][:, sl],
                         start=(K == 0), stop=(K == DC - 1))
        m_sum = _mk(ctx, "tensor", [ready], fsum, "t")
        sq_marks = []
        for K in range(DC):
            buf = hsqA if K % 2 == 0 else hsqB
            w = [ready]
            if K >= 2:
                w.append(("t", sq_marks[K - 2]))
            m_sq = _mk(ctx, "scalar", w,
                       lambda s, K=K, buf=buf, sl=sl: s.square(out=buf[:], in_=h_tiles[K][:, sl]),
                       "s")
            def fmm(t, K=K, buf=buf):
                t.matmul(ps_b[0:1, :], onesb[:], buf[:],
                         start=(K == 0), stop=(K == DC - 1))
            m_mm = _mk(ctx, "tensor", [("s", m_sq)], fmm, "t")
            sq_marks.append(m_mm)
        m_stats = _mk(ctx, "vector", [("t", sq_marks[-1]), ("t", m_sum)],
                      lambda v, sl=sl: (
                          v.tensor_single_scalar(out=mean_row[:], in_=ps_a[0:1, :], scalar=1.0 / D, op=ALU.mult),
                          v.tensor_single_scalar(out=var_row[:], in_=ps_b[0:1, :], scalar=1.0 / D, op=ALU.mult),
                          v.tensor_tensor(out=tmp_row[:], in0=mean_row[:], in1=mean_row[:], op=ALU.mult),
                          v.tensor_tensor(out=var_row[:], in0=var_row[:], in1=tmp_row[:], op=ALU.subtract),
                      ), "v")
        m_rstd = _mk(ctx, "scalar", [("v", m_stats)],
                     lambda s: (
                         s.activation(out=var_row[:], in_=var_row[:], func=AF.Ln, bias=eps_col[:], scale=1.0),
                         s.activation(out=var_row[:], in_=var_row[:], func=AF.Exp, scale=-0.5),
                     ), "s")
        m_bc1 = _mk(ctx, "tensor", [("s", m_rstd)],
                    lambda t: t.matmul(ps_c[:, :], ones1f[:], mean_row[:], start=True, stop=True),
                    "t")
        m_bc1e = _mk(ctx, "vector", [("t", m_bc1)],
                     lambda v, n=n: v.tensor_copy(mean_bc[n][:], ps_c[:, :]), "v")
        m_bc2 = _mk(ctx, "tensor", [("v", m_bc1e)],
                    lambda t: t.matmul(ps_c[:, :], ones1f[:], var_row[:], start=True, stop=True),
                    "t")
        m_bc2e = _mk(ctx, "vector", [("t", m_bc2)],
                     lambda v, n=n: v.tensor_copy(rstd_bc[n][:], ps_c[:, :]), "v")
        for K in range(DC):
            m_ap = _mk(ctx, "vector", [("v", m_bc2e)],
                       lambda v, K=K, n=n, sl=sl: (
                           v.tensor_tensor(out=out_tiles[K][:, sl], in0=h_tiles[K][:, sl],
                                           in1=mean_bc[n][:], op=ALU.subtract),
                           v.tensor_tensor(out=out_tiles[K][:, sl], in0=out_tiles[K][:, sl],
                                           in1=rstd_bc[n][:], op=ALU.mult),
                       ), "v")
            out_marks.append(m_ap)
    return out_marks[-1]



def emit_stage_a(nc, ctx, DD):
    xf = DD["xf"]
    xt = DD["xt"]
    wqT = DD["wqT"]
    wkT = DD["wkT"]
    bqv = DD["bqv"]
    bkv = DD["bkv"]
    identbf = DD["identbf"]
    x1 = DD["x1"]
    SCALE = 1.0 / np.sqrt(D)

    with ExitStack() as es:
        E = es.enter_context
        pool = [E(nc.sbuf_tensor(f"pool{i}", [128, S], BF16)) for i in range(8)]
        wq_bf = [pool[i][:, 0:1024] for i in range(DC)]
        wk_bf = [pool[i][:, 1024:2048] for i in range(DC)]
        stg = [E(nc.sbuf_tensor(f"stg{i}", [128, S], F32)) for i in range(2)]
        bq_sb = E(nc.sbuf_tensor("bq_sb", [128, DC], F32))
        bk_sb = E(nc.sbuf_tensor("bk_sb", [128, DC], F32))
        ident = E(nc.sbuf_tensor("ident", [128, 128], BF16))
        onesb = E(nc.sbuf_tensor("onesb", [128, 1], BF16))
        ones1f = E(nc.sbuf_tensor("ones1f", [1, 128], F32))
        two_col = E(nc.sbuf_tensor("two_col", [128, 1], F32))
        eps_col = E(nc.sbuf_tensor("eps_col", [1, 1], F32))
        xf_bf = [E(nc.sbuf_tensor(f"xfb{i}", [128, S], BF16)) for i in range(DC)]
        xt_bf = [E(nc.sbuf_tensor(f"xtb{i}", [128, D], BF16)) for i in range(16)]
        q_bf = [E(nc.sbuf_tensor(f"qb{i}", [128, S], BF16)) for i in range(DC)]
        k_bf = [E(nc.sbuf_tensor(f"kb{i}", [128, S], BF16)) for i in range(DC)]
        p_grp = [pool[g][:, :] for g in range(4)]
        pt_grp = [pool[4 + tc // 4][:, (tc % 4) * 512:(tc % 4 + 1) * 512] for tc in range(16)]
        mrow = E(nc.sbuf_tensor("mrow", [128, 4], F32))
        mcol = E(nc.sbuf_tensor("mcol", [128, 1], F32))
        mbias = E(nc.sbuf_tensor("mbias", [128, 1], F32))
        zrow = E(nc.sbuf_tensor("zrow", [128, 4], F32))
        zcol = E(nc.sbuf_tensor("zcol", [128, 1], F32))
        corr = E(nc.sbuf_tensor("corr", [128, DC], F32))
        h_bf = xf_bf
        hsqA = E(nc.sbuf_tensor("hsqA", [128, 512], BF16))
        hsqB = E(nc.sbuf_tensor("hsqB", [128, 512], BF16))
        xf32 = E(nc.sbuf_tensor("xf32", [128, 512], F32))
        attn_sb = E(nc.sbuf_tensor("attn_sb", [128, 512], F32))
        mean_row = E(nc.sbuf_tensor("mean_row", [1, 512], F32))
        var_row = E(nc.sbuf_tensor("var_row", [1, 512], F32))
        tmp_row = E(nc.sbuf_tensor("tmp_row", [1, 512], F32))
        mean_bc = [E(nc.sbuf_tensor(f"meanbc{i}", [128, 512], BF16)) for i in range(NT)]
        rstd_bc = [E(nc.sbuf_tensor(f"rstdbc{i}", [128, 512], BF16)) for i in range(NT)]
        o_bf = h_bf

        ps_sc = [E(nc.psum_tensor(f"pssc{i}", [128, 512], F32)) for i in range(4)]
        ps_tr = E(nc.psum_tensor("pstr", [128, 128], BF16))
        ps_pv = E(nc.psum_tensor("pspv", [128, 512], F32))
        ps_ln = E(nc.psum_tensor("psln", [128, 512], F32))


        m_c1 = ctx.dma("sync", ident[:], identbf)
        m_c2 = ctx.dma("sync", bq_sb[:], bqv)
        m_c3 = ctx.dma("sync", bk_sb[:], bkv)
        m_const = _mk(ctx, "gpsimd", [],
                      lambda g: (g.memset(onesb[:], 1.0), g.memset(ones1f[:], 1.0),
                                 g.memset(two_col[:], 2.0), g.memset(eps_col[:], LN_EPS)),
                      "g")
        for _e in ("tensor", "vector", "scalar"):
            ctx.wait(_e, "ld", ctx.ctr["ld"].v)
            ctx.wait(_e, "g", ctx.ctr["g"].v)

        last_cast = {0: 0, 1: 0}
        wm = 0
        for wi, (wT, wdst) in enumerate([(wqT, wq_bf), (wkT, wk_bf)]):
            for i in range(DC):
                bi = (wi * DC + i) % 2
                ctx.wait("sync", "v", last_cast[bi])
                m_ld = ctx.dma("sync", stg[bi][:, 0:D], wT[i * 128:(i + 1) * 128, :])
                wm = _mk(ctx, "vector", [m_ld],
                         lambda v, bi=bi, dst=wdst[i]: v.tensor_copy(dst[:, :], stg[bi][:, 0:D]), "v")
                last_cast[bi] = wm
        w_done = wm

        for b in range(B_PER):
            if b > 0:
                ctx.wait("vector", "wr", ctx.ctr["wr"].v)
            xm = 0
            for i in range(DC):
                bi = i % 2
                ctx.wait("sync", "v", last_cast[bi])
                m_ld = ctx.dma("sync", stg[bi][:], xf[b, i * 128:(i + 1) * 128, :])
                xm = _mk(ctx, "vector", [m_ld],
                         lambda v, bi=bi, dst=xf_bf[i]: v.tensor_copy(dst[:], stg[bi][:]), "v")
                last_cast[bi] = xm
            for i in range(16):
                bi = i % 2
                ctx.wait("sync", "v", last_cast[bi])
                m_ld = ctx.dma("sync", stg[bi][:, 0:D], xt[b, i * 128:(i + 1) * 128, :])
                xm = _mk(ctx, "vector", [m_ld],
                         lambda v, bi=bi, dst=xt_bf[i]: v.tensor_copy(dst[:], stg[bi][:, 0:D]), "v")
                last_cast[bi] = xm
            x_done = xm

            evac_marks = []
            for wtiles, btile, dst in [(wq_bf, bq_sb, q_bf), (wk_bf, bk_sb, k_bf)]:
                for M in range(DC):
                    for n in range(NT):
                        bank = ps_sc[n % 4]
                        waits = [("v", max(w_done, x_done))]
                        if evac_marks:
                            idx = len(evac_marks) - 4
                            if idx >= 0:
                                waits.append(("s", evac_marks[idx]))
                        def fmm(t, wtiles=wtiles, M=M, n=n, bank=bank):
                            for K in range(DC):
                                t.matmul(bank[:], wtiles[K][:, M * 128:(M + 1) * 128],
                                         xf_bf[K][:, n * 512:(n + 1) * 512],
                                         start=(K == 0), stop=(K == DC - 1))
                        m_mm = _mk(ctx, "tensor", waits, fmm, "t")
                        m_ev = _mk(ctx, "scalar", [("t", m_mm)],
                                   lambda s, M=M, n=n, dst=dst, btile=btile, bank=bank:
                                   s.activation(out=dst[M][:, n * 512:(n + 1) * 512],
                                                in_=bank[:], func=AF.Identity,
                                                bias=btile[:, M:M + 1], scale=1.0),
                                   "s")
                        evac_marks.append(m_ev)
            qk_done_s = evac_marks[-1]

            corr_mark = 0
            for M in range(DC):
                m_cs = _mk(ctx, "tensor",
                           ([("v", x_done)] if M == 0 else []) +
                           ([("v", corr_mark)] if M > 0 else []),
                           lambda t, M=M: [t.matmul(ps_pv[:, 0:1],
                                                    xt_bf[K][:, M * 128:(M + 1) * 128], onesb[:],
                                                    start=(K == 0), stop=(K == 15)) for K in range(16)],
                           "t")
                corr_mark = _mk(ctx, "vector", [("t", m_cs)],
                                lambda v, M=M: v.tensor_single_scalar(
                                    out=corr[:, M:M + 1], in_=ps_pv[:, 0:1],
                                    scalar=-2.0 / S, op=ALU.mult), "v")

            h_mark = 0
            for grp in range(4):
                exp_marks = []
                norm_marks = []
                for sti in range(4):
                    st = grp * 4 + sti
                    waits = [("s", qk_done_s)]
                    if exp_marks:
                        waits.append(("s", exp_marks[-1]))
                    def fsc(t, st=st):
                        for K in range(DC):
                            for tc in range(4):
                                t.matmul(ps_sc[tc][:],
                                         q_bf[K][:, st * 128:(st + 1) * 128],
                                         k_bf[K][:, tc * 512:(tc + 1) * 512],
                                         start=(K == 0), stop=(K == DC - 1))
                    m_sc = _mk(ctx, "tensor", waits, fsc, "t")
                    m_max = _mk(ctx, "vector", [("t", m_sc)],
                                lambda v: (
                                    [v.tensor_reduce(out=mrow[:, tc:tc + 1], in_=ps_sc[tc][:],
                                                     axis=AX.X, op=ALU.max) for tc in range(4)],
                                    v.tensor_reduce(out=mcol[:], in_=mrow[:], axis=AX.X, op=ALU.max),
                                    v.tensor_single_scalar(out=mbias[:], in_=mcol[:],
                                                           scalar=-SCALE, op=ALU.mult),
                                ), "v")
                    m_exp = _mk(ctx, "scalar", [("v", m_max)],
                                lambda s, sti=sti: [s.activation(
                                    out=p_grp[sti][:, tc * 512:(tc + 1) * 512],
                                    in_=ps_sc[tc][:], func=AF.Exp,
                                    bias=mbias[:], scale=SCALE) for tc in range(4)],
                                "s")
                    exp_marks.append(m_exp)
                    m_norm = _mk(ctx, "vector", [("s", m_exp)],
                                 lambda v, sti=sti: (
                                     [v.tensor_reduce(out=zrow[:, tc:tc + 1],
                                                      in_=p_grp[sti][:, tc * 512:(tc + 1) * 512],
                                                      axis=AX.X, op=ALU.add) for tc in range(4)],
                                     v.tensor_reduce(out=zcol[:], in_=zrow[:], axis=AX.X, op=ALU.add),
                                     v.reciprocal(out=zcol[:], in_=zcol[:]),
                                     v.tensor_scalar_mul(p_grp[sti][:], p_grp[sti][:], zcol[:]),
                                 ), "v")
                    norm_marks.append(m_norm)
                norm_done = norm_marks[-1]

                tr_ev = 0
                for tc in range(16):
                    for sti in range(4):
                        waits = [("v", norm_done)]
                        if tr_ev:
                            waits.append(("v", tr_ev))
                        m_tr = _mk(ctx, "tensor", waits,
                                   lambda t, tc=tc, sti=sti: t.transpose(
                                       ps_tr[:], p_grp[sti][:, tc * 128:(tc + 1) * 128], ident[:]),
                                   "t")
                        tr_ev = _mk(ctx, "vector", [("t", m_tr)],
                                    lambda v, tc=tc, sti=sti: v.tensor_copy(
                                        pt_grp[tc][:, sti * 128:(sti + 1) * 128], ps_tr[:]),
                                    "v")
                trans_done = tr_ev

                for M in range(DC):
                    def fpv(t, M=M):
                        for K in range(16):
                            t.matmul(ps_pv[:], xt_bf[K][:, M * 128:(M + 1) * 128],
                                     pt_grp[K][:], start=(K == 0), stop=(K == 15))
                    waits = [("v", trans_done), ("v", corr_mark)]
                    if h_mark:
                        waits.append(("v", h_mark))
                    m_pv = _mk(ctx, "tensor", waits, fpv, "t")
                    if h_mark:
                        ctx.wait("gpsimd", "v", h_mark)
                    m_x32 = ctx.dma("gpsimd", xf32[:],
                                    xf[b, M * 128:(M + 1) * 128, grp * 512:(grp + 1) * 512],
                                    sem="ldg")
                    m_ep1 = _mk(ctx, "scalar", [("t", m_pv)],
                                lambda s, M=M: s.activation(out=attn_sb[:], in_=ps_pv[:],
                                                            func=AF.Identity, bias=corr[:, M:M + 1],
                                                            scale=1.0), "s")
                    h_mark = _mk(ctx, "vector", [("s", m_ep1), m_x32],
                                 lambda v, M=M, grp=grp: v.scalar_tensor_tensor(
                                     out=h_bf[M][:, grp * 512:(grp + 1) * 512],
                                     in0=xf32[:], scalar=two_col[:], in1=attn_sb[:],
                                     op0=ALU.mult, op1=ALU.add), "v")
            h_done = h_mark

            m_ln = emit_ln(ctx, h_bf, o_bf, S, ("v", h_done), ps_sc[0], ps_sc[1], ps_ln,
                           hsqA, hsqB, mean_row, var_row, tmp_row, eps_col,
                           mean_bc, rstd_bc, onesb, ones1f)
            for M in range(DC):
                ctx.wait("gpsimd", "v", m_ln)
                ctx.dma("gpsimd", x1[b, M * 128:(M + 1) * 128, :], o_bf[M][:], sem="wr")




def emit_stage_b(nc, ctx, DD):
    x1 = DD["x1"]
    wg = DD["wg"]
    w1eT = DD["w1eT"]
    w3eT = DD["w3eT"]
    w2eT = DD["w2eT"]
    w1sT = DD["w1sT"]
    w3sT = DD["w3sT"]
    w2sT = DD["w2sT"]
    identf = DD["identf"]
    x2 = DD["x2"]

    NE = 16

    with ExitStack() as es:
        E = es.enter_context
        x1_bf = [E(nc.sbuf_tensor(f"x1b{i}", [128, S], BF16)) for i in range(DC)]
        wg_bf = [E(nc.sbuf_tensor(f"wgb{i}", [128, 16], BF16)) for i in range(DC)]
        identfs = E(nc.sbuf_tensor("identfs", [128, 128], F32))
        stg = [E(nc.sbuf_tensor(f"stg{i}", [128, 1024], F32)) for i in range(2)]
        w1_sb = [[E(nc.sbuf_tensor(f"w1_{pp}_{i}", [128, 512], BF16)) for i in range(DC)] for pp in range(2)]
        w3_sb = [[E(nc.sbuf_tensor(f"w3_{pp}_{i}", [128, 512], BF16)) for i in range(DC)] for pp in range(2)]
        w2_sb = [[E(nc.sbuf_tensor(f"w2_{pp}_{i}", [128, 1024], BF16)) for i in range(4)] for pp in range(2)]
        gsc_fm = E(nc.sbuf_tensor("gsc_fm", [16, S], F32))
        gtm = E(nc.sbuf_tensor("gtm", [128, 16], F32))
        gates_fm = E(nc.sbuf_tensor("gates_fm", [16, S], BF16))
        grow = E(nc.sbuf_tensor("grow", [1, S], BF16))
        gate_bc = E(nc.sbuf_tensor("gate_bc", [128, S], BF16))
        t4a = E(nc.sbuf_tensor("t4a", [128, 4], F32))
        t4b = E(nc.sbuf_tensor("t4b", [128, 4], F32))
        t16a = E(nc.sbuf_tensor("t16a", [128, 16], F32))
        t16b = E(nc.sbuf_tensor("t16b", [128, 16], F32))
        c1 = E(nc.sbuf_tensor("c1", [128, 1], F32))
        c2 = E(nc.sbuf_tensor("c2", [128, 1], F32))
        nbig = E(nc.sbuf_tensor("nbig", [128, 1], F32))
        pbig = E(nc.sbuf_tensor("pbig", [128, 1], F32))
        g_sb = [E(nc.sbuf_tensor(f"g_sb{i}", [128, 512], BF16)) for i in range(4)]
        g_sb2 = [E(nc.sbuf_tensor(f"g_sb2{i}", [128, 512], BF16)) for i in range(4)]
        u_sb = [E(nc.sbuf_tensor(f"u_sb{i}", [128, 512], BF16)) for i in range(4)]
        racc = [E(nc.sbuf_tensor(f"racc{i}", [128, S], F32)) for i in range(DC)]
        h_bf = x1_bf
        hsqA = E(nc.sbuf_tensor("hsqA", [128, 512], BF16))
        hsqB = E(nc.sbuf_tensor("hsqB", [128, 512], BF16))
        mean_row = E(nc.sbuf_tensor("mean_row", [1, 512], F32))
        var_row = E(nc.sbuf_tensor("var_row", [1, 512], F32))
        tmp_row = E(nc.sbuf_tensor("tmp_row", [1, 512], F32))
        eps_col = E(nc.sbuf_tensor("eps_col", [1, 1], F32))
        mean_bc = [E(nc.sbuf_tensor("meanbcB", [128, 512], BF16))] * NT
        rstd_bc = [E(nc.sbuf_tensor("rstdbcB", [128, 512], BF16))] * NT
        onesb = E(nc.sbuf_tensor("onesb", [128, 1], BF16))
        ones1f = E(nc.sbuf_tensor("ones1f", [1, 128], F32))
        ones1b = E(nc.sbuf_tensor("ones1b", [1, 128], BF16))
        o_bf = h_bf

        ps_p = [E(nc.psum_tensor(f"psp{i}", [128, 512], F32)) for i in range(4)]
        ps_w2 = E(nc.psum_tensor("psw2", [128, 512], F32))
        ps_g = E(nc.psum_tensor("psg", [128, 512], F32))
        ps_ln = E(nc.psum_tensor("psln", [128, 512], F32))


        m_id = ctx.dma("sync", identfs[:], identf)
        _mk(ctx, "gpsimd", [],
            lambda g: (g.memset(onesb[:], 1.0), g.memset(ones1f[:], 1.0),
                       g.memset(ones1b[:], 1.0),
                       g.memset(eps_col[:], LN_EPS), g.memset(nbig[:], -BIG),
                       g.memset(pbig[:], BIG)), "g")
        m_gconst = ctx.ctr["g"].v
        for _e in ("tensor", "vector", "scalar"):
            ctx.wait(_e, "ld", ctx.ctr["ld"].v)
            ctx.wait(_e, "g", ctx.ctr["g"].v)

        last_cast = {0: 0, 1: 0}
        wgm = 0
        for i in range(DC):
            bi = i % 2
            ctx.wait("sync", "v", last_cast[bi])
            m_ld = ctx.dma("sync", stg[bi][:, 0:16], wg[i * 128:(i + 1) * 128, :])
            wgm = _mk(ctx, "vector", [m_ld],
                      lambda v, bi=bi, i=i: v.tensor_copy(wg_bf[i][:], stg[bi][:, 0:16]), "v")
            last_cast[bi] = wgm

        def load_expert(e, pp):
            nonlocal last_cast
            mm = 0
            srcs = []
            if e < 16:
                for i in range(DC):
                    srcs.append((w1eT[e, i * 128:(i + 1) * 128, :], w1_sb[pp][i], 512))
                for i in range(DC):
                    srcs.append((w3eT[e, i * 128:(i + 1) * 128, :], w3_sb[pp][i], 512))
                for i in range(4):
                    srcs.append((w2eT[e, i * 128:(i + 1) * 128, :], w2_sb[pp][i], 1024))
            else:
                half = e - 16
                c0 = half * 512
                for i in range(DC):
                    srcs.append((w1sT[i * 128:(i + 1) * 128, c0:c0 + 512], w1_sb[pp][i], 512))
                for i in range(DC):
                    srcs.append((w3sT[i * 128:(i + 1) * 128, c0:c0 + 512], w3_sb[pp][i], 512))
                for i in range(4):
                    srcs.append((w2sT[c0 + i * 128:c0 + (i + 1) * 128, :], w2_sb[pp][i], 1024))
            for j, (src, dst, w) in enumerate(srcs):
                bi = j % 2
                ctx.wait("sync", "v", last_cast[bi])
                m_ld = ctx.dma("sync", stg[bi][:, 0:w], src)
                mm = _mk(ctx, "vector", [m_ld],
                         lambda v, bi=bi, dst=dst, w=w: v.tensor_copy(dst[:], stg[bi][:, 0:w]), "v")
                last_cast[bi] = mm
            return mm

        for b in range(B_PER):
            if b > 0:
                ctx.wait("sync", "wr", ctx.ctr["wr"].v)
            x_marks = []
            for i in range(DC):
                x_marks.append(ctx.dma("sync", x1_bf[i][:], x1[b, i * 128:(i + 1) * 128, :]))
            x_done_ld = x_marks[-1]

            glast = 0
            for n in range(NT):
                def fg(t, n=n):
                    for K in range(DC):
                        t.matmul(ps_g[0:16, :], wg_bf[K][:], x1_bf[K][:, n * 512:(n + 1) * 512],
                                 start=(K == 0), stop=(K == DC - 1))
                waits = [x_done_ld, ("v", wgm)]
                if glast:
                    waits.append(("s", glast))
                m_g = _mk(ctx, "tensor", waits, fg, "t")
                glast = _mk(ctx, "scalar", [("t", m_g)],
                            lambda s, n=n: s.activation(out=gsc_fm[:, n * 512:(n + 1) * 512],
                                                        in_=ps_g[0:16, :], func=AF.Sigmoid),
                            "s")
            gdone = 0
            for stt in range(16):
                waits = [("s", glast), m_id, ("g", m_gconst)]
                if gdone:
                    waits.append(("v", gdone))
                m_tr = _mk(ctx, "tensor", waits,
                           lambda t, stt=stt: t.transpose(
                               ps_g[:, 0:16], gsc_fm[:, stt * 128:(stt + 1) * 128],
                               identfs[0:16, 0:16]),
                           "t")
                m_tm = _mk(ctx, "vector", [("t", m_tr)],
                           lambda v: v.tensor_copy(gtm[:], ps_g[:, 0:16]), "v")

                def ftopk(v):
                    sv = gtm[:, :].rearrange("p (g e) -> p g e", g=4)
                    v.tensor_reduce(out=t4a[:], in_=sv, axis=AX.X, op=ALU.max)
                    m1bc = t4a[:, :].unsqueeze(2).broadcast_to([128, 4, 4])
                    v.tensor_tensor(out=t16a[:, :].rearrange("p (g e) -> p g e", g=4),
                                    in0=sv, in1=m1bc, op=ALU.is_ge)
                    v.scalar_tensor_tensor(out=t16b[:], in0=t16a[:], scalar=nbig[:],
                                           in1=gtm[:], op0=ALU.mult, op1=ALU.add)
                    v.tensor_reduce(out=t4b[:], in_=t16b[:, :].rearrange("p (g e) -> p g e", g=4),
                                    axis=AX.X, op=ALU.max)
                    v.tensor_tensor(out=t4a[:], in0=t4a[:], in1=t4b[:], op=ALU.add)
                    v.tensor_reduce(out=c1[:], in_=t4a[:], axis=AX.X, op=ALU.max)
                    v.tensor_scalar(out=t4b[:], in0=t4a[:], scalar1=c1[:], scalar2=None,
                                    op0=ALU.is_ge)
                    v.scalar_tensor_tensor(out=t4b[:], in0=t4b[:], scalar=nbig[:],
                                           in1=t4a[:], op0=ALU.mult, op1=ALU.add)
                    v.tensor_reduce(out=c2[:], in_=t4b[:], axis=AX.X, op=ALU.max)
                    v.tensor_scalar(out=t4a[:], in0=t4a[:], scalar1=c2[:], scalar2=None,
                                    op0=ALU.is_ge)
                    v.tensor_single_scalar(out=t4a[:], in_=t4a[:], scalar=-1.0, op=ALU.add)
                    a_bc = t4a[:, :].unsqueeze(2).broadcast_to([128, 4, 4])
                    v.scalar_tensor_tensor(out=t16a[:, :].rearrange("p (g e) -> p g e", g=4),
                                           in0=a_bc, scalar=pbig[:],
                                           in1=sv, op0=ALU.mult, op1=ALU.add)
                    for it in range(3):
                        v.tensor_reduce(out=c1[:], in_=t16a[:], axis=AX.X, op=ALU.max)
                        v.tensor_scalar(out=t16b[:], in0=t16a[:], scalar1=c1[:], scalar2=None,
                                        op0=ALU.is_ge)
                        v.scalar_tensor_tensor(out=t16a[:], in0=t16b[:], scalar=nbig[:],
                                               in1=t16a[:], op0=ALU.mult, op1=ALU.add)
                    v.tensor_reduce(out=c1[:], in_=t16a[:], axis=AX.X, op=ALU.max)
                    a_bc2 = t4a[:, :].unsqueeze(2).broadcast_to([128, 4, 4])
                    v.scalar_tensor_tensor(out=t16b[:, :].rearrange("p (g e) -> p g e", g=4),
                                           in0=a_bc2, scalar=pbig[:],
                                           in1=sv, op0=ALU.mult, op1=ALU.add)
                    v.tensor_scalar(out=t16b[:], in0=t16b[:], scalar1=c1[:], scalar2=None,
                                    op0=ALU.is_ge)
                    v.tensor_tensor(out=t16b[:], in0=gtm[:], in1=t16b[:], op=ALU.mult)
                    v.tensor_reduce(out=c2[:], in_=t16b[:], axis=AX.X, op=ALU.add)
                    v.tensor_single_scalar(out=c2[:], in_=c2[:], scalar=1e-6, op=ALU.add)
                    v.reciprocal(out=c2[:], in_=c2[:])
                    v.tensor_scalar_mul(t16b[:], t16b[:], c2[:])
                m_tk = _mk(ctx, "vector", [("v", m_tm)], ftopk, "v")
                m_tb = _mk(ctx, "tensor", [("v", m_tk)],
                           lambda t: t.transpose(ps_g[0:16, 0:128], t16b[:], identfs[:]), "t")
                gdone = _mk(ctx, "vector", [("t", m_tb)],
                            lambda v, stt=stt: v.tensor_copy(
                                gates_fm[:, stt * 128:(stt + 1) * 128], ps_g[0:16, 0:128]), "v")

            racc_mark = 0
            for M in range(DC):
                racc_mark = _mk(ctx, "vector", [x_done_ld],
                                lambda v, M=M: v.tensor_copy(racc[M][:], x1_bf[M][:]), "v")

            m_wload = load_expert(0, 0)
            last_gb_t = 0
            evac_w2_mark = racc_mark
            p_evac = {0: 0, 1: 0, 2: 0, 3: 0}
            pendb = None
            prev_gub = 0
            prev_w2b_for = {0: 0, 1: 0}
            for e in range(18):
                pp = e % 2
                w_ready = m_wload
                if e < 17:
                    m_wload = load_expert(e + 1, (e + 1) % 2)
                if e < 16:
                    ctx.wait("gpsimd", "v", gdone)
                    if last_gb_t:
                        ctx.wait("gpsimd", "t", last_gb_t)
                    m_gr = ctx.dma("gpsimd", grow[:], gates_fm[e:e + 1, :], sem="ldg")
                    gb_marks = []
                    for n in range(NT):
                        m_gb = _mk(ctx, "tensor", [m_gr, ("v", gdone)] +
                                   ([("v", gb_marks[-1])] if gb_marks else []),
                                   lambda t, n=n: t.matmul(ps_g[:], ones1b[:],
                                                           grow[:, n * 512:(n + 1) * 512],
                                                           start=True, stop=True), "t")
                        last_gb_t = m_gb
                        gb_marks.append(_mk(ctx, "vector", [("t", m_gb)],
                                            lambda v, n=n: v.tensor_copy(
                                                gate_bc[:, n * 512:(n + 1) * 512], ps_g[:]), "v"))
                    gate_ready = gb_marks[-1]
                else:
                    gate_ready = 0

                def emit_w2b(pendb):
                    nonlocal evac_w2_mark
                    (pn, p_gu, gset, wref) = pendb
                    psl = slice(pn * 512, (pn + 1) * 512)
                    for Md in range(DC):
                        def fw2(t, Md=Md, gset=gset, wref=wref):
                            for Kf in range(4):
                                t.matmul(ps_w2[:], wref[Kf][:, Md * 128:(Md + 1) * 128],
                                         gset[Kf][:], start=(Kf == 0), stop=(Kf == 3))
                        mw2 = _mk(ctx, "tensor", [("v", p_gu), ("v", evac_w2_mark)], fw2, "t")
                        evac_w2_mark = _mk(ctx, "vector", [("t", mw2)],
                                           lambda v, Md=Md, psl=psl: v.tensor_tensor(
                                               out=racc[Md][:, psl], in0=racc[Md][:, psl],
                                               in1=ps_w2[:], op=ALU.add), "v")

                for n in range(NT):
                    sl = slice(n * 512, (n + 1) * 512)
                    gset = g_sb if n % 2 == 0 else g_sb2
                    ev1 = []
                    for Mf in range(4):
                        waits = [("v", w_ready)]
                        if p_evac[Mf]:
                            waits.append(("s", p_evac[Mf]))
                        if prev_w2b_for.get(n % 2):
                            waits.append(("v", prev_w2b_for[n % 2]))
                        def f1(t, Mf=Mf, pp=pp, sl=sl):
                            for K in range(DC):
                                t.matmul(ps_p[Mf][:], w1_sb[pp][K][:, Mf * 128:(Mf + 1) * 128],
                                         x1_bf[K][:, sl], start=(K == 0), stop=(K == DC - 1))
                        m1 = _mk(ctx, "tensor", waits, f1, "t")
                        ev1.append(_mk(ctx, "scalar", [("t", m1)],
                                       lambda s, Mf=Mf, gset=gset: s.activation(
                                           out=gset[Mf][:], in_=ps_p[Mf][:], func=AF.Silu), "s"))
                    ev3 = []
                    for Mf in range(4):
                        def f3(t, Mf=Mf, pp=pp, sl=sl):
                            for K in range(DC):
                                t.matmul(ps_p[Mf][:], w3_sb[pp][K][:, Mf * 128:(Mf + 1) * 128],
                                         x1_bf[K][:, sl], start=(K == 0), stop=(K == DC - 1))
                        m3 = _mk(ctx, "tensor", [("s", ev1[Mf])], f3, "t")
                        w3w = [("t", m3)]
                        if prev_gub:
                            w3w.append(("v", prev_gub))
                        ev3.append(_mk(ctx, "scalar", w3w,
                                       lambda s, Mf=Mf: s.activation(out=u_sb[Mf][:], in_=ps_p[Mf][:],
                                                                     func=AF.Copy), "s"))
                        p_evac[Mf] = ev3[-1]
                    if pendb is not None:
                        emit_w2b(pendb)
                        prev_w2b_for[pendb[0] % 2] = evac_w2_mark
                        pendb = None
                    gu_marks = []
                    for Mf in range(4):
                        if e < 16:
                            def fgu(v, Mf=Mf, sl=sl, gset=gset):
                                v.tensor_tensor(out=gset[Mf][:], in0=gset[Mf][:], in1=u_sb[Mf][:],
                                                op=ALU.mult)
                                v.tensor_tensor(out=gset[Mf][:], in0=gset[Mf][:],
                                                in1=gate_bc[:, sl], op=ALU.mult)
                            gu_marks.append(_mk(ctx, "vector",
                                                [("s", ev3[Mf]), ("v", gate_ready)], fgu, "v"))
                        else:
                            gu_marks.append(_mk(ctx, "vector", [("s", ev3[Mf])],
                                                lambda v, Mf=Mf, gset=gset: v.tensor_tensor(
                                                    out=gset[Mf][:], in0=gset[Mf][:],
                                                    in1=u_sb[Mf][:], op=ALU.mult), "v"))
                    prev_gub = gu_marks[-1]
                    pendb = (n, prev_gub, gset, w2_sb[pp])
                if pendb is not None:
                    emit_w2b(pendb)
                    prev_w2b_for[pendb[0] % 2] = evac_w2_mark
                    pendb = None
            ffn_done = evac_w2_mark

            hc = 0
            for M in range(DC):
                hc = _mk(ctx, "vector", [("v", ffn_done)],
                         lambda v, M=M: v.tensor_copy(h_bf[M][:], racc[M][:]), "v")
            m_ln = emit_ln(ctx, h_bf, o_bf, S, ("v", hc), ps_p[0], ps_p[1], ps_ln,
                           hsqA, hsqB, mean_row, var_row, tmp_row, eps_col,
                           mean_bc, rstd_bc, onesb, ones1f)
            for M in range(DC):
                ctx.wait("gpsimd", "v", m_ln)
                ctx.dma("gpsimd", x2[b, M * 128:(M + 1) * 128, :], o_bf[M][:], sem="wr")




def emit_stage_c(nc, ctx, DD):
    x2 = DD["x2"]
    wqkvT = DD["wqkvT"]
    woT = DD["woT"]
    wpl = DD["wpl"]
    pVgs = DD["pVgs"]
    pVT0 = DD["pVT0"]
    pBtau = DD["pBtau"]
    pBgm = DD["pBgm"]
    pCch = DD["pCch"]
    pGam = DD["pGam"]
    pAlp = DD["pAlp"]
    pIC = DD["pIC"]
    repsel = DD["repsel"]
    deltam = DD["deltam"]
    mklnT = DD["mklnT"]
    hselT = DD["hselT"]
    x3 = DD["x3"]

    CT = 512
    NCH = S // CT

    with ExitStack() as es:
        E = es.enter_context
        stg = [E(nc.sbuf_tensor(f"stg{i}", [128, 1024], F32)) for i in range(2)]
        wqkv_bf = [E(nc.sbuf_tensor(f"wqkv{i}", [128, 3 * D], BF16)) for i in range(DC)]
        wo_bf = [E(nc.sbuf_tensor(f"wo{i}", [128, D], BF16)) for i in range(DC)]
        wlt_bf = E(nc.sbuf_tensor("wlt_bf", [128, 1024], BF16))
        r_cols = E(nc.sbuf_tensor("r_cols", [128, 512], F32))
        c_cols = E(nc.sbuf_tensor("c_cols", [128, 512], F32))
        repsel_sb = E(nc.sbuf_tensor("repsel_sb", [128, 32 * 128], BF16))
        deltam_sb = E(nc.sbuf_tensor("deltam_sb", [128, 32 * 64], BF16))
        mkln_sb = E(nc.sbuf_tensor("mkln_sb", [128, 8 * 16], BF16))
        hsel_sb = E(nc.sbuf_tensor("hsel_sb", [16, 8 * 128], BF16))
        states = E(nc.sbuf_tensor("states", [128, 1024], F32))
        onescol = E(nc.sbuf_tensor("onescol", [64, 1], F32))
        ksq = E(nc.sbuf_tensor("ksq", [128, CT], BF16))
        kmu = E(nc.sbuf_tensor("kmu", [16, CT], F32))
        kvar = E(nc.sbuf_tensor("kvar", [16, CT], F32))
        ktmp = E(nc.sbuf_tensor("ktmp", [16, CT], F32))
        kmu_bf = E(nc.sbuf_tensor("kmu_bf", [16, CT], BF16))
        krs_bf = E(nc.sbuf_tensor("krs_bf", [16, CT], BF16))
        mu_bc = E(nc.sbuf_tensor("mu_bc", [128, CT], BF16))
        rs_bc = E(nc.sbuf_tensor("rs_bc", [128, CT], BF16))
        eps16 = E(nc.sbuf_tensor("eps16", [16, 1], F32))
        krep = E(nc.sbuf_tensor("krep", [128, CT], F32))
        qrep = E(nc.sbuf_tensor("qrep", [128, CT], F32))
        u_t = [E(nc.sbuf_tensor(f"u_t{i}", [128, CT], F32)) for i in range(2)]
        ftr = E(nc.sbuf_tensor("ftr", [128, CT], F32))
        p2 = [E(nc.sbuf_tensor(f"p2_{i}", [128, CT], BF16)) for i in range(2)]
        o_sb = E(nc.sbuf_tensor("o_sb", [128, 512], BF16))
        h_bf = [E(nc.sbuf_tensor(f"hb{i}", [128, 512], BF16)) for i in range(DC)]
        x3_bf = [E(nc.sbuf_tensor(f"x3b{i}", [128, 512], BF16)) for i in range(DC)]
        hsqA = E(nc.sbuf_tensor("hsqA", [128, 512], BF16))
        hsqB = E(nc.sbuf_tensor("hsqB", [128, 512], BF16))
        mean_row = E(nc.sbuf_tensor("mean_row", [1, 512], F32))
        var_row = E(nc.sbuf_tensor("var_row", [1, 512], F32))
        tmp_row = E(nc.sbuf_tensor("tmp_row", [1, 512], F32))
        eps1 = E(nc.sbuf_tensor("eps1", [1, 1], F32))
        mean_bc = [E(nc.sbuf_tensor("meanbcC", [128, 512], BF16))]
        rstd_bc = [E(nc.sbuf_tensor("rstdbcC", [128, 512], BF16))]
        onesb = E(nc.sbuf_tensor("onesb", [128, 1], BF16))
        ones1f = E(nc.sbuf_tensor("ones1f", [1, 128], F32))

        ps_a = [E(nc.psum_tensor(f"psa{i}", [128, 512], F32)) for i in range(4)]
        ps_vr = [E(nc.psum_tensor(f"psvr{i}", [128, 512], F32)) for i in range(2)]
        ps_y = E(nc.psum_tensor("psy", [64, 512], F32))
        ps_ln = E(nc.psum_tensor("pslnC", [128, 512], F32))


        cm = []
        cm.append(ctx.dma("sync", repsel_sb[:], repsel))
        cm.append(ctx.dma("sync", deltam_sb[:], deltam))
        cm.append(ctx.dma("sync", mkln_sb[:], mklnT))
        cm.append(ctx.dma("sync", hsel_sb[:], hselT))
        es2 = ExitStack()
        E2 = es2.enter_context
        pl = {}
        for nm in ["w", "vgs", "vt0", "btau", "bgm", "cch", "gam", "alp", "ic"]:
            pl[nm] = E2(nc.sbuf_tensor("pl_" + nm, [64, 1024], F32))
        sA = E2(nc.sbuf_tensor("sA", [64, 1024], F32))
        sB = E2(nc.sbuf_tensor("sB", [64, 1024], F32))
        sC = E2(nc.sbuf_tensor("sC", [64, 1024], F32))
        r_pl = E2(nc.sbuf_tensor("r_pl", [64, 1024], F32))
        c_pl = E2(nc.sbuf_tensor("c_pl", [64, 1024], F32))
        for nm, src in [("w", wpl), ("vgs", pVgs), ("vt0", pVT0), ("btau", pBtau),
                        ("bgm", pBgm), ("cch", pCch), ("gam", pGam), ("alp", pAlp), ("ic", pIC)]:
            cm.append(ctx.dma("sync", pl[nm][:], src))
        pl_ld = cm[-1]
        m_gc = _mk(ctx, "gpsimd", [],
                   lambda g: (g.memset(onesb[:], 1.0), g.memset(ones1f[:], 1.0),
                              g.memset(eps16[:], LN_EPS), g.memset(eps1[:], LN_EPS),
                              g.memset(onescol[:], 1.0)), "g")
        for _e in ("tensor", "vector", "scalar"):
            ctx.wait(_e, "ld", ctx.ctr["ld"].v)
            ctx.wait(_e, "g", ctx.ctr["g"].v)

        def f1(v):
            v.tensor_tensor(out=sA[:], in0=pl["vgs"][:], in1=pl["vt0"][:], op=ALU.subtract)
            v.tensor_tensor(out=sA[:], in0=sA[:], in1=pl["w"][:], op=ALU.add)
        m1 = _mk(ctx, "vector", [pl_ld], f1, "v")
        def f2(s):
            s.activation(out=sB[:], in_=sA[:], func=AF.Abs)
            s.activation(out=sB[:], in_=sB[:], func=AF.Exp, scale=-1.0)
        m2 = _mk(ctx, "scalar", [("v", m1), ("g", m_gc)], f2, "s")
        def f3(s):
            s.activation(out=sB[:], in_=sB[:], func=AF.Ln, bias=onescol[:], scale=1.0)
        m3 = _mk(ctx, "scalar", [("s", m2)], f3, "s")
        def f4(v):
            v.tensor_relu(sC[:], sA[:])
            v.tensor_tensor(out=sB[:], in0=sB[:], in1=sC[:], op=ALU.add)
            v.tensor_tensor(out=sC[:], in0=pl["btau"][:], in1=sB[:], op=ALU.mult)
            v.reciprocal(out=r_pl[:], in_=pl["cch"][:])
            v.tensor_tensor(out=r_pl[:], in0=sC[:], in1=r_pl[:], op=ALU.mult)
        m4 = _mk(ctx, "vector", [("s", m3)], f4, "v")
        def f5(s):
            s.activation(out=r_pl[:], in_=r_pl[:], func=AF.Exp, scale=-1.0)
            s.activation(out=c_pl[:], in_=sA[:], func=AF.Sigmoid)
        m5 = _mk(ctx, "scalar", [("v", m4)], f5, "s")
        def f6(v):
            v.tensor_tensor(out=c_pl[:], in0=c_pl[:], in1=sB[:], op=ALU.mult)
            v.tensor_tensor(out=c_pl[:], in0=c_pl[:], in1=pl["bgm"][:], op=ALU.mult)
            v.tensor_tensor(out=sB[:], in0=sC[:], in1=pl["ic"][:], op=ALU.subtract)
            v.tensor_tensor(out=sB[:], in0=sB[:], in1=pl["alp"][:], op=ALU.mult)
        m6 = _mk(ctx, "vector", [("s", m5)], f6, "v")
        def f7(s):
            s.activation(out=sB[:], in_=sB[:], func=AF.Tanh)
        m7 = _mk(ctx, "scalar", [("v", m6)], f7, "s")
        def f8(v):
            v.tensor_tensor(out=c_pl[:], in0=c_pl[:], in1=sB[:], op=ALU.mult)
            v.tensor_tensor(out=c_pl[:], in0=c_pl[:], in1=pl["gam"][:], op=ALU.mult)
            for dpar in range(2):
                for (colsrc, coldst) in [(r_pl, r_cols), (c_pl, c_cols)]:
                    src = colsrc[:, :].rearrange("e (h d) -> e h d", h=16)[:, :, dpar::2]
                    dst = coldst[dpar * 64:(dpar + 1) * 64, :].rearrange("e (h j) -> e h j", h=16)
                    v.tensor_copy(dst, src)
            v.tensor_copy(wlt_bf[0:64, :], pl["w"][:])
            v.tensor_copy(wlt_bf[64:128, :], pl["w"][:])
            v.memset(states[:], 0.0)
        m_planes = _mk(ctx, "vector", [("s", m7)], f8, "v")
        es2.close()
        x2c = [E(nc.sbuf_tensor(f"x2c_{i}", [128, CT], BF16)) for i in range(DC)]
        qc = [E(nc.sbuf_tensor(f"qc_{i}", [128, CT], BF16)) for i in range(DC)]
        kc = [E(nc.sbuf_tensor(f"kc_{i}", [128, CT], BF16)) for i in range(DC)]
        vc = [E(nc.sbuf_tensor(f"vc_{i}", [128, CT], BF16)) for i in range(DC)]
        yc = [E(nc.sbuf_tensor(f"yc_{i}", [128, CT], BF16)) for i in range(DC)]

        last_cast = {0: 0, 1: 0}
        wm = 0
        for i in range(DC):
            for part in range(3):
                bi = (i * 3 + part) % 2
                ctx.wait("sync", "v", last_cast[bi])
                m_ld = ctx.dma("sync", stg[bi][:], wqkvT[i * 128:(i + 1) * 128,
                                                        part * 1024:(part + 1) * 1024])
                def fc(v, bi=bi, i=i, part=part):
                    v.tensor_copy(wqkv_bf[i][:, part * 1024:(part + 1) * 1024], stg[bi][:])
                wm = _mk(ctx, "vector", [m_ld], fc, "v")
                last_cast[bi] = wm
        for i in range(DC):
            bi = i % 2
            ctx.wait("sync", "v", last_cast[bi])
            m_ld = ctx.dma("sync", stg[bi][:], woT[i * 128:(i + 1) * 128, :])
            def fc2(v, bi=bi, i=i):
                v.tensor_copy(wo_bf[i][:], stg[bi][:])
            wm = _mk(ctx, "vector", [m_ld], fc2, "v")
            last_cast[bi] = wm
        w_done = wm

        chunk_consumed = m_planes
        u_free = [0, 0]
        p2_free = [0, 0]
        vr_free = [0, 0]
        y_wr = 0

        for ch in range(NCH):
            t0 = ch * CT
            for b in range(B_PER):
                ctx.wait("sync", "v", chunk_consumed)
                xm = []
                for i in range(DC):
                    xm.append(ctx.dma("sync", x2c[i][:], x2[b, i * 128:(i + 1) * 128, t0:t0 + CT]))
                x_ld = xm[-1]
                evq = None
                evq_marks = []
                for part, dstt in [(0, qc), (1, kc), (2, vc)]:
                    for M in range(DC):
                        bank = ps_a[(part * DC + M) % 4]
                        waits = [x_ld, ("v", max(w_done, m_planes))]
                        if len(evq_marks) >= 4:
                            waits.append(("s", evq_marks[-4]))
                        def fqkv(t, part=part, M=M, bank=bank, b=b):
                            for K in range(DC):
                                t.matmul(bank[:], wqkv_bf[K][:, part * 1024 + M * 128:
                                                             part * 1024 + (M + 1) * 128],
                                         x2c[K][:], start=(K == 0), stop=(K == DC - 1))
                        m_mm = _mk(ctx, "tensor", waits, fqkv, "t")
                        def fev(s, dstt=dstt, M=M, bank=bank):
                            s.activation(out=dstt[M][:], in_=bank[:], func=AF.Copy)
                        evq = _mk(ctx, "scalar", [("t", m_mm)], fev, "s")
                        evq_marks.append(evq)
                qkv_done = evq

                def fmu(t, b=b):
                    for j in range(DC):
                        t.matmul(ps_a[0][0:16, :], mkln_sb[:, j * 16:(j + 1) * 16], kc[j][:],
                                 start=(j == 0), stop=(j == DC - 1))
                m_mu = _mk(ctx, "tensor", [("s", qkv_done)], fmu, "t")
                sq_prev = None
                for j in range(DC):
                    w = [("s", qkv_done)] if j == 0 else [("t", sq_prev)]
                    m_s = _mk(ctx, "scalar", w,
                              lambda s, j=j, b=b: s.square(out=ksq[:], in_=kc[j][:]), "s")
                    def fsq(t, j=j):
                        t.matmul(ps_a[1][0:16, :], mkln_sb[:, j * 16:(j + 1) * 16], ksq[:],
                                 start=(j == 0), stop=(j == DC - 1))
                    sq_prev = _mk(ctx, "tensor", [("s", m_s)], fsq, "t")
                def fkstat(v):
                    v.tensor_single_scalar(out=kmu[:], in_=ps_a[0][0:16, :], scalar=1.0 / 64, op=ALU.mult)
                    v.tensor_single_scalar(out=kvar[:], in_=ps_a[1][0:16, :], scalar=1.0 / 64, op=ALU.mult)
                    v.tensor_tensor(out=ktmp[:], in0=kmu[:], in1=kmu[:], op=ALU.mult)
                    v.tensor_tensor(out=kvar[:], in0=kvar[:], in1=ktmp[:], op=ALU.subtract)
                    v.tensor_copy(kmu_bf[:], kmu[:])
                m_kst = _mk(ctx, "vector", [("t", m_mu), ("t", sq_prev)], fkstat, "v")
                def fkrs(s):
                    s.activation(out=kvar[:], in_=kvar[:], func=AF.Ln, bias=eps16[:], scale=1.0)
                    s.activation(out=kvar[:], in_=kvar[:], func=AF.Exp, scale=-0.5)
                m_krs = _mk(ctx, "scalar", [("v", m_kst)], fkrs, "s")
                m_krsc = _mk(ctx, "vector", [("s", m_krs)],
                             lambda v: v.tensor_copy(krs_bf[:], kvar[:]), "v")
                kn_done = None
                for j in range(DC):
                    def fbc1(t, j=j):
                        t.matmul(ps_a[2][:, :], hsel_sb[:, j * 128:(j + 1) * 128], kmu_bf[:],
                                 start=True, stop=True)
                        t.matmul(ps_a[3][:, :], hsel_sb[:, j * 128:(j + 1) * 128], krs_bf[:],
                                 start=True, stop=True)
                    m_bc = _mk(ctx, "tensor", [("v", m_krsc)], fbc1, "t")
                    def fbce(v, j=j, b=b):
                        v.tensor_copy(mu_bc[:], ps_a[2][:, :])
                        v.tensor_copy(rs_bc[:], ps_a[3][:, :])
                        v.tensor_tensor(out=kc[j][:], in0=kc[j][:], in1=mu_bc[:], op=ALU.subtract)
                        v.tensor_tensor(out=kc[j][:], in0=kc[j][:], in1=rs_bc[:], op=ALU.mult)
                    kn_done = _mk(ctx, "vector", [("t", m_bc)], fbce, "v")
                    if j < DC - 1:
                        ctx.wait("tensor", "v", kn_done)

                for h in range(16):
                    tile = h // 2
                    r0 = (h % 2) * 64
                    kn_slice = kc[tile][r0:r0 + 64, :]
                    q_slice = qc[tile][r0:r0 + 64, :]
                    v_slice = vc[tile][r0:r0 + 64, :]
                    def frep(v, kn_slice=kn_slice, q_slice=q_slice):
                        v.tensor_copy(krep[0:64, :], kn_slice)
                        v.tensor_copy(krep[64:128, :], kn_slice)
                        v.tensor_copy(qrep[0:64, :], q_slice)
                        v.tensor_copy(qrep[64:128, :], q_slice)
                    waits = [("v", kn_done)]
                    m_rep = _mk(ctx, "vector", waits, frep, "v")
                    def fw(t, h=h, q_slice=q_slice, r0=r0):
                        t.matmul(ps_y[:, :], wlt_bf[r0:r0 + 64, h * 64:(h + 1) * 64], q_slice,
                                 start=True, stop=False, skip_group_check=True)
                    w_y = [("s", qkv_done), ("v", m_planes)]
                    if y_wr:
                        w_y.append(("s", y_wr))
                    m_w = _mk(ctx, "tensor", w_y, fw, "t")
                    last_scan = None
                    for j in range(32):
                        pp = j % 2
                        gidx = b * 512 + h * 32 + j
                        cidx = h * 32 + j
                        def fvr(t, j=j, pp=pp, v_slice=v_slice, r0=r0):
                            t.matmul(ps_vr[pp][:, :], repsel_sb[r0:r0 + 64, j * 128:(j + 1) * 128],
                                     v_slice, start=True, stop=True, skip_group_check=True)
                        w_vr = [("s", qkv_done)]
                        if vr_free[pp]:
                            w_vr.append(("v", vr_free[pp]))
                        m_vr = _mk(ctx, "tensor", w_vr, fvr, "t")
                        def fu(v, pp=pp, cidx=cidx):
                            v.scalar_tensor_tensor(out=u_t[pp][:], in0=ps_vr[pp][:, :],
                                                   scalar=c_cols[:, cidx:cidx + 1], in1=krep[:],
                                                   op0=ALU.mult, op1=ALU.mult)
                        w_u = [("t", m_vr), ("v", m_rep)]
                        if u_free[pp]:
                            w_u.append(("v", u_free[pp]))
                        m_u = _mk(ctx, "vector", w_u, fu, "v")
                        vr_free[pp] = m_u
                        def fscan(v, pp=pp, gidx=gidx, cidx=cidx):
                            v.tensor_tensor_scan(out=ftr[:], data0=r_cols[:, cidx:cidx + 1].broadcast_to([128, CT]),
                                                 data1=u_t[pp][:], initial=states[:, gidx:gidx + 1],
                                                 op0=ALU.mult, op1=ALU.add)
                            v.tensor_copy(states[:, gidx:gidx + 1], ftr[:, CT - 1:CT])
                            v.tensor_tensor(out=p2[pp][:], in0=ftr[:], in1=qrep[:], op=ALU.mult)
                        w_s = [("v", m_u)]
                        if p2_free[pp]:
                            w_s.append(("t", p2_free[pp]))
                        m_scan = _mk(ctx, "vector", w_s, fscan, "v")
                        u_free[pp] = m_scan
                        def fro(t, j=j, pp=pp):
                            t.matmul(ps_y[:, :], deltam_sb[:, j * 64:(j + 1) * 64], p2[pp][:],
                                     start=False, stop=(j == 31), skip_group_check=True)
                        m_ro = _mk(ctx, "tensor", [("v", m_scan), ("t", m_w)], fro, "t")
                        p2_free[pp] = m_ro
                        last_scan = m_ro
                    def fye(s, b=b, tile=tile, r0=r0):
                        s.activation(out=yc[tile][r0:r0 + 64, :], in_=ps_y[:, :], func=AF.Copy)
                    y_wr = _mk(ctx, "scalar", [("t", last_scan)], fye, "s")
                y_done = y_wr

                evo = None
                for M in range(DC):
                    def fwo(t, M=M, b=b):
                        for K in range(DC):
                            t.matmul(ps_ln[:, :], wo_bf[K][:, M * 128:(M + 1) * 128], yc[K][:],
                                     start=(K == 0), stop=(K == DC - 1))
                    w_wo = [("s", y_done)]
                    if evo:
                        w_wo.append(("v", evo))
                    m_wo = _mk(ctx, "tensor", w_wo, fwo, "t")
                    def fre(v, M=M, b=b):
                        v.tensor_copy(o_sb[:], ps_ln[:, :])
                        v.tensor_tensor(out=h_bf[M][:], in0=o_sb[:], in1=x2c[M][:], op=ALU.add)
                    evo = _mk(ctx, "vector", [("t", m_wo)], fre, "v")
                h_done = evo
                ctx.wait("vector", "wr", ctx.ctr["wr"].v)
                m_ln = emit_ln(ctx, h_bf, x3_bf, 512, ("v", h_done), ps_a[0], ps_a[1], ps_ln,
                               hsqA, hsqB, mean_row, var_row, tmp_row, eps1,
                               mean_bc, rstd_bc, onesb, ones1f)
                for M in range(DC):
                    ctx.wait("gpsimd", "v", m_ln)
                    ctx.dma("gpsimd", x3[b, M * 128:(M + 1) * 128, t0:t0 + CT], x3_bf[M][:], sem="wr")
                chunk_consumed = m_ln




def emit_stage_d(nc, ctx, DD):
    x3 = DD["x3"]
    rw = DD["rw"]
    rb = DD["rb"]
    sw1 = DD["sw1"]
    sw2 = DD["sw2"]
    sw3 = DD["sw3"]
    ew1 = DD["ew1"]
    ew2 = DD["ew2"]
    ew3 = DD["ew3"]
    identf = DD["identf"]
    outp = DD["outp"]

    with ExitStack() as es:
        E = es.enter_context
        x3_bf = [E(nc.sbuf_tensor(f"x3b{i}", [128, S], BF16)) for i in range(DC)]
        racc = [E(nc.sbuf_tensor(f"racc{i}", [128, S], F32)) for i in range(DC)]
        stg = [E(nc.sbuf_tensor(f"stg{i}", [128, 1024], F32)) for i in range(2)]
        w1d = [E(nc.sbuf_tensor(f"w1d{i}", [128, 1024], BF16)) for i in range(DC)]
        w2d = [E(nc.sbuf_tensor(f"w2d{i}", [128, 1024], BF16)) for i in range(DC)]
        w3d = [E(nc.sbuf_tensor(f"w3d{i}", [128, 1024], BF16)) for i in range(DC)]
        rw_bf = [E(nc.sbuf_tensor(f"rwb{i}", [128, 8], BF16)) for i in range(DC)]
        rb_sb = E(nc.sbuf_tensor("rb_sb", [8, 1], F32))
        identfs = E(nc.sbuf_tensor("identfs", [128, 128], F32))
        rout_fm = E(nc.sbuf_tensor("rout_fm", [8, S], F32))
        gates_fm = E(nc.sbuf_tensor("gates_fm", [8, S], BF16))
        grow = E(nc.sbuf_tensor("grow", [1, S], BF16))
        ones1b = E(nc.sbuf_tensor("ones1b", [1, 128], BF16))
        gate_bc = E(nc.sbuf_tensor("gate_bc", [128, S], BF16))
        gtm = E(nc.sbuf_tensor("gtm", [128, 8], F32))
        t8a = E(nc.sbuf_tensor("t8a", [128, 8], F32))
        t8b = E(nc.sbuf_tensor("t8b", [128, 8], F32))
        c1 = E(nc.sbuf_tensor("c1", [128, 1], F32))
        c2 = E(nc.sbuf_tensor("c2", [128, 1], F32))
        nbig = E(nc.sbuf_tensor("nbig", [128, 1], F32))
        g_sb = [E(nc.sbuf_tensor(f"g_sb{i}", [128, 512], BF16)) for i in range(DC)]
        g_sb2 = [E(nc.sbuf_tensor(f"g_sb2{i}", [128, 512], BF16)) for i in range(DC)]
        u_sb = [E(nc.sbuf_tensor(f"u_sb{i}", [128, 512], BF16)) for i in range(DC)]
        h_bf = g_sb
        hsqA = E(nc.sbuf_tensor("hsqA", [128, 512], BF16))
        hsqB = E(nc.sbuf_tensor("hsqB", [128, 512], BF16))
        mean_row = E(nc.sbuf_tensor("mean_row", [1, 512], F32))
        var_row = E(nc.sbuf_tensor("var_row", [1, 512], F32))
        tmp_row = E(nc.sbuf_tensor("tmp_row", [1, 512], F32))
        eps1 = E(nc.sbuf_tensor("eps1", [1, 1], F32))
        mean_bc = [E(nc.sbuf_tensor("meanbcD", [128, 512], BF16))]
        rstd_bc = [E(nc.sbuf_tensor("rstdbcD", [128, 512], BF16))]
        onesb = E(nc.sbuf_tensor("onesb", [128, 1], BF16))
        ones1f = E(nc.sbuf_tensor("ones1f", [1, 128], F32))

        ps_p = [E(nc.psum_tensor(f"psp{i}", [128, 512], F32)) for i in range(4)]
        ps_w2 = E(nc.psum_tensor("psw2", [128, 512], F32))
        ps_g = E(nc.psum_tensor("psg", [128, 512], F32))
        ps_ln = E(nc.psum_tensor("pslnD", [128, 512], F32))


        m_id = ctx.dma("sync", identfs[:], identf)
        m_rb = ctx.dma("sync", rb_sb[:], rb)
        m_gc = _mk(ctx, "gpsimd", [],
                   lambda g: (g.memset(onesb[:], 1.0), g.memset(ones1f[:], 1.0),
                              g.memset(ones1b[:], 1.0),
                              g.memset(eps1[:], LN_EPS), g.memset(nbig[:], -BIG)), "g")
        for _e in ("tensor", "vector", "scalar"):
            ctx.wait(_e, "ld", ctx.ctr["ld"].v)
            ctx.wait(_e, "g", ctx.ctr["g"].v)

        last_cast = {0: 0, 1: 0}
        wm = 0
        for i in range(DC):
            bi = i % 2
            ctx.wait("sync", "v", last_cast[bi])
            m_ld = ctx.dma("sync", stg[bi][:, 0:8], rw[i * 128:(i + 1) * 128, :])
            def fc(v, bi=bi, i=i):
                v.tensor_copy(rw_bf[i][:], stg[bi][:, 0:8])
            wm = _mk(ctx, "vector", [m_ld], fc, "v")
            last_cast[bi] = wm
        rw_done = wm

        def load_half(e, half):
            nonlocal wm
            h0 = half * 1024
            srcs = []
            if e < 8:
                for i in range(DC):
                    srcs.append((ew1[e, i * 128:(i + 1) * 128, h0:h0 + 1024], w1d[i]))
                for i in range(DC):
                    srcs.append((ew2[e, i * 128:(i + 1) * 128, h0:h0 + 1024], w2d[i]))
                for i in range(DC):
                    srcs.append((ew3[e, h0 + i * 128:h0 + (i + 1) * 128, :], w3d[i]))
            else:
                for i in range(DC):
                    srcs.append((sw1[i * 128:(i + 1) * 128, h0:h0 + 1024], w1d[i]))
                for i in range(DC):
                    srcs.append((sw2[i * 128:(i + 1) * 128, h0:h0 + 1024], w2d[i]))
                for i in range(DC):
                    srcs.append((sw3[h0 + i * 128:h0 + (i + 1) * 128, :], w3d[i]))
            marks = []
            for j, (src, dst) in enumerate(srcs):
                bi = j % 2
                ctx.wait("sync", "v", last_cast[bi])
                m_ld = ctx.dma("sync", stg[bi][:], src)
                def fc(v, bi=bi, dst=dst):
                    v.tensor_copy(dst[:], stg[bi][:])
                wm = _mk(ctx, "vector", [m_ld], fc, "v")
                last_cast[bi] = wm
                marks.append(wm)
            return marks[-1]

        for b in range(B_PER):
            if b > 0:
                ctx.wait("sync", "wr", ctx.ctr["wr"].v)
            xm = []
            for i in range(DC):
                xm.append(ctx.dma("sync", x3_bf[i][:], x3[b, i * 128:(i + 1) * 128, :]))
            x_ld = xm[-1]

            rlast = 0
            for n in range(NT):
                def fr(t, n=n):
                    for K in range(DC):
                        t.matmul(ps_g[0:8, :], rw_bf[K][:], x3_bf[K][:, n * 512:(n + 1) * 512],
                                 start=(K == 0), stop=(K == DC - 1))
                waits = [x_ld, m_rb, ("v", rw_done)]
                if rlast:
                    waits.append(("s", rlast))
                m_r = _mk(ctx, "tensor", waits, fr, "t")
                def fre(s, n=n):
                    s.activation(out=rout_fm[:, n * 512:(n + 1) * 512], in_=ps_g[0:8, :],
                                 func=AF.Identity, bias=rb_sb[:], scale=1.0)
                rlast = _mk(ctx, "scalar", [("t", m_r)], fre, "s")
            gdone = 0
            for stt in range(16):
                waits = [("s", rlast), m_id, ("g", m_gc)]
                if gdone:
                    waits.append(("v", gdone))
                m_tr = _mk(ctx, "tensor", waits,
                           lambda t, stt=stt: t.transpose(
                               ps_g[:, 0:8], rout_fm[:, stt * 128:(stt + 1) * 128],
                               identfs[0:8, 0:8]),
                           "t")
                def ftop(v):
                    v.tensor_copy(gtm[:], ps_g[:, 0:8])
                    v.tensor_reduce(out=c1[:], in_=gtm[:], axis=AX.X, op=ALU.max)
                    v.tensor_single_scalar(out=c1[:], in_=c1[:], scalar=-1.0, op=ALU.mult)
                m_t1 = _mk(ctx, "vector", [("t", m_tr)], ftop, "v")
                def fexp(s):
                    s.activation(out=gtm[:], in_=gtm[:], func=AF.Exp, bias=c1[:], scale=1.0)
                m_t2 = _mk(ctx, "scalar", [("v", m_t1)], fexp, "s")
                def ftop2(v, stt=stt):
                    v.tensor_reduce(out=c2[:], in_=gtm[:], axis=AX.X, op=ALU.add)
                    v.reciprocal(out=c2[:], in_=c2[:])
                    v.tensor_scalar_mul(gtm[:], gtm[:], c2[:])
                    v.tensor_reduce(out=c1[:], in_=gtm[:], axis=AX.X, op=ALU.max)
                    v.tensor_scalar(out=t8a[:], in0=gtm[:], scalar1=c1[:], scalar2=None,
                                    op0=ALU.is_ge)
                    v.scalar_tensor_tensor(out=t8b[:], in0=t8a[:], scalar=nbig[:],
                                           in1=gtm[:], op0=ALU.mult, op1=ALU.add)
                    v.tensor_reduce(out=c2[:], in_=t8b[:], axis=AX.X, op=ALU.max)
                    v.tensor_scalar(out=t8a[:], in0=gtm[:], scalar1=c2[:], scalar2=None,
                                    op0=ALU.is_ge)
                    v.tensor_tensor(out=t8a[:], in0=gtm[:], in1=t8a[:], op=ALU.mult)
                    v.tensor_reduce(out=c2[:], in_=t8a[:], axis=AX.X, op=ALU.add)
                    v.reciprocal(out=c2[:], in_=c2[:])
                    v.tensor_scalar_mul(t8a[:], t8a[:], c2[:])
                m_t3 = _mk(ctx, "vector", [("s", m_t2)], ftop2, "v")
                m_tb = _mk(ctx, "tensor", [("v", m_t3)],
                           lambda t: t.transpose(ps_g[0:8, 0:128], t8a[:], identfs[:]), "t")
                gdone = _mk(ctx, "vector", [("t", m_tb)],
                            lambda v, stt=stt: v.tensor_copy(
                                gates_fm[:, stt * 128:(stt + 1) * 128], ps_g[0:8, 0:128]), "v")

            racc_mark = 0
            for M in range(DC):
                racc_mark = _mk(ctx, "vector", [x_ld],
                                lambda v, M=M: v.tensor_copy(racc[M][:], x3_bf[M][:]), "v")

            m_wload = load_half(0, 0)
            last_gb_t = 0
            evac_mark = racc_mark
            p_evac = {0: 0, 1: 0, 2: 0, 3: 0}
            pend = None
            prev_gu = 0
            prev_w2_for = {0: 0, 1: 0}
            for ee in range(18):
                e, half = ee // 2, ee % 2
                pp = 0
                w_ready = m_wload
                if e < 8 and half == 0:
                    ctx.wait("gpsimd", "v", gdone)
                    if last_gb_t:
                        ctx.wait("gpsimd", "t", last_gb_t)
                    m_gr = ctx.dma("gpsimd", grow[:], gates_fm[e:e + 1, :], sem="ldg")
                    gb = 0
                    for n in range(NT):
                        m_gb = _mk(ctx, "tensor", [m_gr, ("v", gdone)] +
                                   ([("v", gb)] if gb else []),
                                   lambda t, n=n: t.matmul(ps_g[:], ones1b[:],
                                                           grow[:, n * 512:(n + 1) * 512],
                                                           start=True, stop=True), "t")
                        last_gb_t = m_gb
                        gb = _mk(ctx, "vector", [("t", m_gb)],
                                 lambda v, n=n: v.tensor_copy(
                                     gate_bc[:, n * 512:(n + 1) * 512], ps_g[:]), "v")
                    gate_ready = gb
                elif e >= 8:
                    gate_ready = 0
                def emit_w2(pend):
                    nonlocal evac_mark
                    (pn, p_gu, gset, w2ref) = pend
                    psl = slice(pn * 512, (pn + 1) * 512)
                    for Md in range(DC):
                        def fw2(t, Md=Md, gset=gset, w2ref=w2ref):
                            for Kf in range(DC):
                                t.matmul(ps_w2[:], w2ref[Kf][:, Md * 128:(Md + 1) * 128],
                                         gset[Kf][:], start=(Kf == 0), stop=(Kf == DC - 1))
                        mw2 = _mk(ctx, "tensor", [("v", p_gu), ("v", evac_mark)], fw2, "t")
                        evac_mark = _mk(ctx, "vector", [("t", mw2)],
                                        lambda v, Md=Md, psl=psl: v.tensor_tensor(
                                            out=racc[Md][:, psl], in0=racc[Md][:, psl],
                                            in1=ps_w2[:], op=ALU.add), "v")

                for n in range(NT):
                    sl = slice(n * 512, (n + 1) * 512)
                    gset = g_sb if n % 2 == 0 else g_sb2
                    for Mf in range(DC):
                        bank = ps_p[Mf % 4]
                        waits = [("v", w_ready)]
                        if p_evac[Mf % 4]:
                            waits.append(("s", p_evac[Mf % 4]))
                        if gset is g_sb and prev_w2_for.get(0):
                            waits.append(("v", prev_w2_for[0]))
                        if gset is g_sb2 and prev_w2_for.get(1):
                            waits.append(("v", prev_w2_for[1]))
                        def f1(t, Mf=Mf, bank=bank, sl=sl):
                            for K in range(DC):
                                t.matmul(bank[:], w1d[K][:, Mf * 128:(Mf + 1) * 128],
                                         x3_bf[K][:, sl], start=(K == 0), stop=(K == DC - 1))
                        mm1 = _mk(ctx, "tensor", waits, f1, "t")
                        p_evac[Mf % 4] = _mk(ctx, "scalar", [("t", mm1)],
                                             lambda s, Mf=Mf, bank=bank, gset=gset: s.activation(
                                                 out=gset[Mf][:], in_=bank[:], func=AF.Silu), "s")
                    ev3 = [0] * DC
                    for Mf in range(DC):
                        bank = ps_p[Mf % 4]
                        def f3(t, Mf=Mf, bank=bank, sl=sl):
                            for K in range(DC):
                                t.matmul(bank[:], w2d[K][:, Mf * 128:(Mf + 1) * 128],
                                         x3_bf[K][:, sl], start=(K == 0), stop=(K == DC - 1))
                        mm3 = _mk(ctx, "tensor", [("s", p_evac[Mf % 4])], f3, "t")
                        w3 = [("t", mm3)]
                        if prev_gu:
                            w3.append(("v", prev_gu))
                        ev3[Mf] = _mk(ctx, "scalar", w3,
                                      lambda s, Mf=Mf, bank=bank: s.activation(
                                          out=u_sb[Mf][:], in_=bank[:], func=AF.Copy), "s")
                        p_evac[Mf % 4] = ev3[Mf]
                    if pend is not None:
                        emit_w2(pend)
                        prev_w2_for[pend[0] % 2] = evac_mark
                        pend = None
                    gu_done = 0
                    for Mf in range(DC):
                        if e < 8:
                            def fgu(v, Mf=Mf, sl=sl, gset=gset):
                                v.tensor_tensor(out=gset[Mf][:], in0=gset[Mf][:], in1=u_sb[Mf][:],
                                                op=ALU.mult)
                                v.tensor_tensor(out=gset[Mf][:], in0=gset[Mf][:],
                                                in1=gate_bc[:, sl], op=ALU.mult)
                            gu_done = _mk(ctx, "vector", [("s", ev3[Mf]), ("v", gate_ready)], fgu, "v")
                        else:
                            gu_done = _mk(ctx, "vector", [("s", ev3[Mf])],
                                          lambda v, Mf=Mf, gset=gset: v.tensor_tensor(
                                              out=gset[Mf][:], in0=gset[Mf][:], in1=u_sb[Mf][:],
                                              op=ALU.mult), "v")
                    prev_gu = gu_done
                    pend = (n, gu_done, gset, w3d)
                    prev_w2_for[n % 2] = 0
                if pend is not None:
                    emit_w2(pend)
                    prev_w2_for[pend[0] % 2] = evac_mark
                    pend = None
                if ee < 17:
                    e2, h2 = (ee + 1) // 2, (ee + 1) % 2
                    m_wload = load_half(e2, h2)
            ffn_done = evac_mark

            for n in range(NT):
                sl = slice(n * 512, (n + 1) * 512)
                hc = 0
                for M in range(DC):
                    hc = _mk(ctx, "vector", [("v", ffn_done)],
                             lambda v, M=M, sl=sl: v.tensor_copy(h_bf[M][:], racc[M][:, sl]), "v")
                o_view = [racc[K][:, sl] for K in range(DC)]
                m_ln = emit_ln(ctx, h_bf, o_view, 512, ("v", hc), ps_p[0], ps_p[1], ps_ln,
                               hsqA, hsqB, mean_row, var_row, tmp_row, eps1,
                               mean_bc, rstd_bc, onesb, ones1f)
                for M in range(DC):
                    ctx.wait("gpsimd", "v", m_ln)
                    ctx.dma("gpsimd", outp[b, M * 128:(M + 1) * 128, sl], racc[M][:, sl], sem="wr")




def _constants():
    identbf = np.eye(128, dtype=NPBF)
    identf = np.eye(128, dtype=np.float32)
    repsel = np.zeros((64, 32, 128), dtype=NPBF)
    for j in range(32):
        for p in range(128):
            repsel[2 * j + p // 64, j, p] = 1
    deltam = np.zeros((128, 32, 64), dtype=NPBF)
    for p in range(128):
        for j in range(32):
            deltam[p, j, 2 * j + p // 64] = 1
    mkln = np.zeros((128, 8, 16), dtype=NPBF)
    for jj in range(8):
        for p in range(128):
            mkln[p, jj, 2 * jj + p // 64] = 1
    hsel = np.zeros((16, 8, 128), dtype=NPBF)
    for jj in range(8):
        for p in range(128):
            hsel[2 * jj + p // 64, jj, p] = 1
    repsel2 = np.concatenate([repsel.reshape(64, 32 * 128)] * 2, axis=0)
    return (identbf, identf,
            np.ascontiguousarray(repsel2),
            np.ascontiguousarray(deltam.reshape(128, 32 * 64)),
            np.ascontiguousarray(mkln.reshape(128, 8 * 16)),
            np.ascontiguousarray(hsel.reshape(16, 8 * 128)))


_NC_CACHE = {}


def _get_nc(name, builder):
    if name not in _NC_CACHE:
        _NC_CACHE[name] = builder()
    return _NC_CACHE[name]



def build_full():
    nc = bass.Bass("TRN2", target_bir_lowering=False)
    D_ = {}
    def inp(name, shape, dt=F32):
        D_[name] = nc.dram_tensor(name, shape, dt, kind="ExternalInput").ap()
    def internal(name, shape, dt):
        D_[name] = nc.dram_tensor(name, shape, dt, kind="Internal").ap()
    inp("xf", [B_PER, D, S]); inp("xt", [B_PER, S, D])
    inp("wqT", [D, D]); inp("wkT", [D, D])
    inp("bqv", [128, DC]); inp("bkv", [128, DC])
    inp("identbf", [128, 128], BF16)
    internal("x1", [B_PER, D, S], BF16)
    inp("wg", [D, 16])
    inp("w1eT", [16, D, 512]); inp("w3eT", [16, D, 512]); inp("w2eT", [16, 512, D])
    inp("w1sT", [D, D]); inp("w3sT", [D, D]); inp("w2sT", [D, D])
    inp("identf", [128, 128])
    internal("x2", [B_PER, D, S], BF16)
    inp("wqkvT", [D, 3 * D]); inp("woT", [D, D]); inp("wpl", [64, 1024])
    for nm in ["pVgs", "pVT0", "pBtau", "pBgm", "pCch", "pGam", "pAlp", "pIC"]:
        inp(nm, [64, 1024])
    inp("repsel", [128, 32 * 128], BF16); inp("deltam", [128, 32 * 64], BF16)
    inp("mklnT", [128, 8 * 16], BF16); inp("hselT", [16, 8 * 128], BF16)
    internal("x3", [B_PER, D, S], BF16)
    inp("rw", [D, 8]); inp("rb", [8, 1])
    inp("sw1", [D, 2048]); inp("sw2", [D, 2048]); inp("sw3", [2048, D])
    inp("ew1", [8, D, 2048]); inp("ew2", [8, D, 2048]); inp("ew3", [8, 2048, D])
    D_["outp"] = nc.dram_tensor("outp", [B_PER, D, S], F32, kind="ExternalOutput").ap()

    class _NCProxy:
        def __init__(self, base, px):
            self._b = base
            self._px = px
        def sbuf_tensor(self, name, *a, **k):
            return self._b.sbuf_tensor(self._px + name, *a, **k)
        def psum_tensor(self, name, *a, **k):
            return self._b.psum_tensor(self._px + name, *a, **k)
        def __getattr__(self, n):
            return getattr(self._b, n)

    with ExitStack() as es0:
        ctx = Ctx(nc, es0)
        block = es0.enter_context(nc.Block())

        def barrier():
            marks = [(s, ctx.ctr[s].v) for s in
                     ("t", "v", "s", "g", "wr", "ld", "ld0", "ld1", "ldg")]
            for e in Ctx.ENGS:
                for (sn, v) in marks:
                    if v:
                        ctx.wait(e, sn, v)

        emit_stage_a(_NCProxy(nc, 'a_'), ctx, D_)
        barrier()
        ctx.new_dma_sems(es0, "b")
        emit_stage_b(_NCProxy(nc, 'b_'), ctx, D_)
        barrier()
        ctx.new_dma_sems(es0, "c")
        emit_stage_c(_NCProxy(nc, 'c_'), ctx, D_)
        barrier()
        ctx.new_dma_sems(es0, "d")
        emit_stage_d(_NCProxy(nc, 'd_'), ctx, D_)
        ctx.wait("gpsimd", "wr", ctx.ctr["wr"].v)
        ctx.finalize(block)
    return nc


def kernel(**inp):
    identbf, identf, repsel, deltam, mkln, hsel = _constants()
    x = np.asarray(inp["L0_attn_x"], dtype=np.float32)
    wqT = np.ascontiguousarray(np.asarray(inp["L0_attn_Wq"]).T)
    wkT = np.ascontiguousarray(np.asarray(inp["L0_attn_Wk"]).T)
    bq = np.ascontiguousarray(np.asarray(inp["L0_attn_bq"]).reshape(DC, 128).T)
    bk = np.ascontiguousarray(np.asarray(inp["L0_attn_bk"]).reshape(DC, 128).T)
    w1eT = np.ascontiguousarray(np.transpose(np.asarray(inp["L0_ffn_w1_e"]), (0, 2, 1)))
    w3eT = np.ascontiguousarray(np.transpose(np.asarray(inp["L0_ffn_w3_e"]), (0, 2, 1)))
    w2eT = np.ascontiguousarray(np.transpose(np.asarray(inp["L0_ffn_w2_e"]), (0, 2, 1)))
    w1sT = np.ascontiguousarray(np.asarray(inp["L0_ffn_w1_s"]).T)
    w3sT = np.ascontiguousarray(np.asarray(inp["L0_ffn_w3_s"]).T)
    w2sT = np.ascontiguousarray(np.asarray(inp["L0_ffn_w2_s"]).T)
    wg = np.ascontiguousarray(np.asarray(inp["L0_ffn_w_gate"]).T)
    wqkvT = np.ascontiguousarray(np.asarray(inp["L1_attn_W_qkv"]).T)
    woT = np.ascontiguousarray(np.asarray(inp["L1_attn_W_o"]).T)
    wltm = np.asarray(inp["L1_attn_W_LTM"])
    wpl = np.ascontiguousarray(np.transpose(wltm, (2, 0, 1)).reshape(64, 1024))
    def plane16(v):
        return np.ascontiguousarray(
            np.broadcast_to(np.asarray(v, dtype=np.float32)[None, :, None],
                            (64, 16, 64)).reshape(64, 1024))
    def plane1(v):
        return np.full((64, 1024), np.float32(np.asarray(v).reshape(-1)[0]), dtype=np.float32)
    rb = np.ascontiguousarray(np.asarray(inp["L1_ffn_rb"]).reshape(8, 1))

    nc = _get_nc("full", build_full)
    in_maps = []
    for c in range(NCORES):
        xs = x[2 * c:2 * c + 2]
        in_maps.append(dict(
            xf=np.ascontiguousarray(np.transpose(xs, (0, 2, 1))),
            xt=np.ascontiguousarray(xs),
            wqT=wqT, wkT=wkT, bqv=bq, bkv=bk, identbf=identbf,
            wg=wg, w1eT=w1eT, w3eT=w3eT, w2eT=w2eT,
            w1sT=w1sT, w3sT=w3sT, w2sT=w2sT, identf=identf,
            wqkvT=wqkvT, woT=woT, wpl=wpl,
            pVgs=plane16(inp["L1_attn_V_gs"]), pVT0=plane16(inp["L1_attn_V_T0"]),
            pBtau=plane16(inp["L1_attn_beta_tau"]), pBgm=plane16(inp["L1_attn_beta_gm"]),
            pCch=plane16(inp["L1_attn_C_ch"]), pGam=plane16(inp["L1_attn_gamma"]),
            pAlp=plane1(inp["L1_attn_alpha_ppd"]), pIC=plane1(inp["L1_attn_IC_threshold"]),
            repsel=repsel, deltam=deltam, mklnT=mkln, hselT=hsel,
            rw=np.asarray(inp["L1_ffn_rw"]), rb=rb,
            sw1=np.asarray(inp["L1_ffn_sw1"]), sw2=np.asarray(inp["L1_ffn_sw2"]),
            sw3=np.asarray(inp["L1_ffn_sw3"]), ew1=np.asarray(inp["L1_ffn_ew1"]),
            ew2=np.asarray(inp["L1_ffn_ew2"]), ew3=np.asarray(inp["L1_ffn_ew3"])))
    res = run_bass_kernel_spmd(nc, in_maps, core_ids=list(range(NCORES)))
    outs = [np.transpose(np.asarray(r["outp"], dtype=np.float32), (0, 2, 1))
            for r in res.results]
    return np.ascontiguousarray(np.concatenate(outs, axis=0))
